# version 03
# baseline (speedup 1.0000x reference)
import math
import os as _os
import numpy as np
import ml_dtypes
import concourse.bass as bass
import concourse.mybir as mybir
from concourse.expressions import RuntimeValue
from concourse.bass_utils import run_bass_kernel_spmd

F32 = mybir.dt.float32
BF16 = mybir.dt.bfloat16
I32 = mybir.dt.int32
AF = mybir.ActivationFunctionType
ALU = mybir.AluOpType
AX = mybir.AxisListType

D = 2048
T = 8192
B = 2
H = 16
DH = 128
HID = 8192
NT = 2048
SCALE = 1.0 / math.sqrt(T)
EPS = 1e-5
NDK = D // 128
NSB = T // 128

SET_LN_EXP = 6
SET_SILU = 18


def load_act_set(nc, set_id):
    return nc.scalar.add_instruction(mybir.InstLoadActFuncSet(
        name=nc.get_next_instruction_name(),
        act_func_set_id=set_id, ins=[], outs=[]))


class FW:
    ENG_SEM = {
        "tensor": ("T", None),
        "scalar": ("S", None),
        "vector": ("V", None),
        "gpsimd": ("PG", "DG"),
        "sync": ("SY", "DQ"),
    }
    ALL_SEMS = ["T", "S", "V", "PG", "SY", "DG", "DQ"]

    def __init__(self, nc):
        self.nc = nc
        self.sem = {}
        self.scope = ""
        self.cnt = {}
        self.known = {}
        self.bufs = {}
        for s in self.ALL_SEMS:
            for suf in ("", "b"):
                k = s + suf
                self.sem[k] = nc.alloc_semaphore(name="sem_" + k)
                self.cnt[k] = 0

    def semkey(self, engine, dma):
        c, d = self.ENG_SEM[engine]
        base = d if dma else c
        if self.scope and base in ("T", "S", "V", "PG", "SY", "DG", "DQ"):
            return base + self.scope
        return base

    def enter_body(self):
        self.scope = "b"
        for s in self.ALL_SEMS:
            self.cnt[s + "b"] = 0
        self.known = {k: v for k, v in self.known.items() if not k[1].endswith("b")}
        self._saved_bufs = {k: {"w": dict(v["w"]), "r": dict(v["r"])}
                            for k, v in self.bufs.items()}
        self._saved_known = dict(self.known)
        self._saved_cnt = dict(self.cnt)

    def exit_body(self):
        nc = self.nc
        finals = {s + "b": self.cnt[s + "b"] for s in self.ALL_SEMS}
        for ename in self.ENG_SEM:
            eng = getattr(nc, ename)
            for sk, v in finals.items():
                if v > 0:
                    eng.wait_ge(self.sem[sk], v)
            gk = self.ENG_SEM[ename][0]
            eng.drain().then_inc(self.sem[gk], 1)
        self.scope = ""

    def body_reset(self):
        self.scope = "b"
        self.bufs = {k: {"w": dict(v["w"]), "r": dict(v["r"])}
                     for k, v in self._saved_bufs.items()}
        self.known = dict(self._saved_known)
        self.cnt = dict(self._saved_cnt)
        for s in self.ALL_SEMS:
            self.cnt[s + "b"] = 0

    def finish_switch(self):
        self.scope = ""
        for ename in self.ENG_SEM:
            gk = self.ENG_SEM[ename][0]
            self.cnt[gk] += 1

    def wrote(self, buf, engine):
        gk = self.ENG_SEM[engine][0]
        self.bufs[buf] = {"w": {gk: self.cnt[gk]}, "r": {}}

    def _waits(self, ename, deps):
        eng = getattr(self.nc, ename)
        for sk, v in sorted(deps.items()):
            if v <= 0:
                continue
            kk = (ename, sk)
            if self.known.get(kk, 0) >= v:
                continue
            eng.wait_ge(self.sem[sk], v)
            self.known[kk] = v

    def op(self, ename, fn, reads=(), writes=(), dma=False, inc=True):
        deps = {}
        my_sk = self.semkey(ename, dma)
        my_csk = self.semkey(ename, False)

        def merge(d):
            for sk, v in d.items():
                if sk == my_csk and not dma and ename == "tensor":
                    continue
                if deps.get(sk, 0) < v:
                    deps[sk] = v

        for b in reads:
            st = self.bufs.get(b)
            if st:
                merge(st["w"])
        for b in writes:
            st = self.bufs.get(b)
            if st:
                merge(st["w"])
                merge(st["r"])
        self._waits(ename, deps)
        inst = fn()
        if inc:
            amt = 16 if dma else 1
            inst.then_inc(self.sem[my_sk], amt)
            self.cnt[my_sk] += amt
        tok = (my_sk, self.cnt[my_sk])
        for b in writes:
            self.bufs[b] = {"w": {tok[0]: tok[1]}, "r": {}}
        for b in reads:
            st = self.bufs.setdefault(b, {"w": {}, "r": {}})
            if st["r"].get(tok[0], 0) < tok[1]:
                st["r"][tok[0]] = tok[1]
        return tok

    def barrier(self):
        assert self.scope == ""
        nc = self.nc
        for ename in self.ENG_SEM:
            gk = self.ENG_SEM[ename][0]
            getattr(nc, ename).drain().then_inc(self.sem[gk], 1)
            self.cnt[gk] += 1
        targets = {s: self.cnt[s] for s in self.ALL_SEMS}
        for ename in self.ENG_SEM:
            eng = getattr(nc, ename)
            for s, v in targets.items():
                if v > 0:
                    eng.wait_ge(self.sem[s], v)
            for s, v in targets.items():
                self.known[(ename, s)] = v
        for b in self.bufs.values():
            pass


def build_program(stop_after=None, dbg=()):
    KSC = int(_os.environ.get("KDBG_SC", "16"))
    KSKIP = set(_os.environ.get("KDBG_SKIP", "").split(","))
    """stop_after in {None,'P1k','P1v','P2','A','O','F1'}; dbg: iterable of
    scratch names to expose as ExternalOutput."""
    nc = bass.Bass("TRN2", target_bir_lowering=False)

    def kind_of(name):
        return "ExternalOutput" if name in dbg else "Internal"

    xT = nc.dram_tensor("xT", [D, T], F32, kind="ExternalInput").ap()
    x_ownT = nc.dram_tensor("x_ownT", [D, NT], F32, kind="ExternalInput").ap()
    wq = nc.dram_tensor("wq", [D, D], F32, kind="ExternalInput").ap()
    wk = nc.dram_tensor("wk", [D, D], F32, kind="ExternalInput").ap()
    wv = nc.dram_tensor("wv", [D, D], F32, kind="ExternalInput").ap()
    wo = nc.dram_tensor("wo", [D, D], F32, kind="ExternalInput").ap()
    wgt = nc.dram_tensor("wgt", [D, HID], F32, kind="ExternalInput").ap()
    wut = nc.dram_tensor("wut", [D, HID], F32, kind="ExternalInput").ap()
    wdt = nc.dram_tensor("wdt", [HID, D], F32, kind="ExternalInput").ap()
    cidx = nc.dram_tensor("cidx", [1, 1], I32, kind="ExternalInput").ap()
    out_fm = nc.dram_tensor("out_fm", [D, NT], F32, kind="ExternalOutput").ap()
    k_scr = nc.dram_tensor("k_scr", [H, 128, T], BF16, kind=kind_of("k_scr")).ap()
    v_scr = nc.dram_tensor("v_scr", [NSB, 128, D], BF16, kind=kind_of("v_scr")).ap()
    q_scr = nc.dram_tensor("q_scr", [H, 128, NT], BF16, kind=kind_of("q_scr")).ap()
    a_scr = nc.dram_tensor("a_scr", [H, 128, NT], BF16, kind=kind_of("a_scr")).ap()
    y1f_scr = nc.dram_tensor("y1f_scr", [NDK, 128, NT], F32, kind=kind_of("y1f_scr")).ap()
    y1b_scr = nc.dram_tensor("y1b_scr", [NDK, 128, NT], BF16, kind=kind_of("y1b_scr")).ap()
    hid_scr = nc.dram_tensor("hid_scr", [HID // 128, 128, NT], BF16, kind=kind_of("hid_scr")).ap()
    dn_scr = nc.dram_tensor("dn_scr", [NDK, 128, NT], F32, kind=kind_of("dn_scr")).ap()
    r2_scr = nc.dram_tensor("r2_scr", [NDK, 128, NT], F32, kind=kind_of("r2_scr")).ap()

    fw = FW(nc)
    from contextlib import ExitStack

    with ExitStack() as outer:
        c_sb = outer.enter_context(nc.sbuf_tensor("c_sb", [1, 1], I32))
        ones_f = outer.enter_context(nc.sbuf_tensor("ones_f", [128, 1], F32))
        ones_b = outer.enter_context(nc.sbuf_tensor("ones_b", [128, 1], BF16))
        onesr_f = outer.enter_context(nc.sbuf_tensor("onesr_f", [1, 128], F32))
        masks = outer.enter_context(nc.sbuf_tensor("masks", [128, 4 * 512], F32))
        iotat = outer.enter_context(nc.sbuf_tensor("iotat", [128, 512], F32))
        sufA = outer.enter_context(nc.sbuf_tensor("sufA", [128, 65 * NDK], F32))
        sufB = outer.enter_context(nc.sbuf_tensor("sufB", [128, 65 * NDK], F32))
        sufXb = outer.enter_context(nc.sbuf_tensor("sufXb", [128, NDK * 65], BF16))
        sufV = outer.enter_context(nc.sbuf_tensor("sufV", [128, H * 65], F32))

        fw.op("sync", lambda: nc.sync.dma_start(c_sb[:, :], cidx),
              writes=["c_sb"], dma=True)
        if "init" not in KSKIP:
            fw.op("vector", lambda: nc.vector.memset(ones_f[:, :], 1.0), writes=["ones_f"])
            fw.op("vector", lambda: nc.vector.memset(ones_b[:, :], 1.0), writes=["ones_b"])
            fw.op("vector", lambda: nc.vector.memset(onesr_f[:, :], 1.0), writes=["onesr_f"])
            fw.op("scalar", lambda: load_act_set(nc, SET_LN_EXP), inc=False)
        for k in range(4 if "init" not in KSKIP else 0):
            fw.op("gpsimd",
                  lambda k=k: nc.gpsimd.iota(
                      iotat[:, :], [[1, 512]], base=-(128 * k),
                      channel_multiplier=-1,
                      allow_small_or_imprecise_dtypes=True),
                  writes=["iotat"])
            fw.op("vector",
                  lambda k=k: nc.vector.tensor_single_scalar(
                      masks[:, 512 * k:512 * (k + 1)], iotat[:, :], 0.0, ALU.is_ge),
                  reads=["iotat"], writes=["masks"])

        with ExitStack() as st:
            wk_sb = st.enter_context(nc.sbuf_tensor("wk_sb", [128, NDK * D], BF16))
            xc = [st.enter_context(nc.sbuf_tensor(f"xc{i}", [128, NDK * 512], BF16))
                  for i in range(2)]
            kst = [st.enter_context(nc.sbuf_tensor(f"kst{i}", [128, 512], BF16))
                   for i in range(2)]
            kps = st.enter_context(nc.psum_tensor("kps", [128, 4 * 512], F32))

            fw.op("gpsimd", lambda: nc.gpsimd.dma_start(
                wk_sb[:, :].rearrange("p (a n) -> p a n", a=NDK),
                wk.rearrange("(a p) n -> p a n", p=128)),
                writes=["wk_sb"], dma=True)

            for sc in range(KSC):
                xb = f"xc{sc % 2}"
                fw.op("gpsimd", lambda sc=sc: nc.gpsimd.dma_start(
                    xc[sc % 2][:, :].rearrange("p (a s) -> p a s", a=NDK),
                    xT.rearrange("(a p) s -> p a s", p=128)[:, :, 512 * sc:512 * (sc + 1)]),
                    writes=[xb], dma=True)
                for sb in range(4 if "bsum" not in KSKIP else 0):
                    j = sc * 4 + sb
                    fw.op("vector", lambda sc=sc, sb=sb, j=j: nc.vector.reduce_sum(
                        sufA[:, 16 * j:16 * (j + 1)].rearrange("p (a o) -> p a o", o=1),
                        xc[sc % 2][:, :].rearrange("p (a s) -> p a s", a=NDK)[:, :, 128 * sb:128 * (sb + 1)],
                        axis=AX.X),
                        reads=[xb], writes=["sufA"])
                for dcol in range(int(_os.environ.get("KDBG_DCOL", "16")) if "kproj" not in KSKIP else 0):
                    pb = f"kps{dcol % 4}"
                    for dk in range(NDK):
                        fw.op("tensor", lambda sc=sc, dcol=dcol, dk=dk: nc.tensor.matmul(
                            kps[:, 512 * (dcol % 4):512 * (dcol % 4 + 1)],
                            wk_sb[:, D * dk + 128 * dcol: D * dk + 128 * (dcol + 1)],
                            xc[sc % 2][:, 512 * dk:512 * (dk + 1)],
                            start=(dk == 0), stop=(dk == NDK - 1)),
                            reads=["wk_sb", xb], writes=[pb])
                    ks = f"kst{dcol % 2}"
                    if "evac" not in KSKIP:
                        fw.op("scalar", lambda dcol=dcol: nc.scalar.copy(
                            kst[dcol % 2][:, :], kps[:, 512 * (dcol % 4):512 * (dcol % 4 + 1)]),
                            reads=[pb], writes=[ks])
                    if "kwr" not in KSKIP:
                        fw.op("sync", lambda sc=sc, dcol=dcol: nc.sync.dma_start(
                            k_scr[dcol, :, 512 * sc:512 * (sc + 1)], kst[dcol % 2][:, :]),
                            reads=[ks], writes=[f"k_scr{dcol}"], dma=True)

            if "suf" in KSKIP:
                fw.barrier()
                if stop_after == "P1k":
                    return nc
            fw.op("vector", lambda: nc.vector.memset(sufA[:, 64 * 16:65 * 16], 0.0),
                  writes=["sufA"])
            cur, oth = sufA, sufB
            curb, othb = "sufA", "sufB"
            for lvl in range(6):
                sh = 1 << lvl
                w = (65 - sh) * 16
                fw.op("vector", lambda cur=cur, oth=oth, sh=sh, w=w: nc.vector.tensor_add(
                    oth[:, 0:w], cur[:, 0:w], cur[:, 16 * sh:16 * sh + w]),
                    reads=[curb], writes=[othb])
                fw.op("vector", lambda cur=cur, oth=oth, w=w: nc.vector.tensor_copy(
                    oth[:, w:65 * 16], cur[:, w:65 * 16]),
                    reads=[curb], writes=[othb])
                cur, oth = oth, cur
                curb, othb = othb, curb
            fw.op("scalar", lambda cur=cur: nc.scalar.copy(
                sufXb[:, :].rearrange("p (a j) -> p a j", a=NDK),
                cur[:, :].rearrange("p (j a) -> p a j", a=NDK)),
                reads=[curb], writes=["sufXb"])
        fw.barrier()
        if stop_after == "P1k":
            return nc

        with ExitStack() as st:
            wv_sb = st.enter_context(nc.sbuf_tensor("wv_sb", [128, NDK * D], BF16))
            xc = [st.enter_context(nc.sbuf_tensor(f"xv{i}", [128, NDK * 512], BF16))
                  for i in range(2)]
            vst = [st.enter_context(nc.sbuf_tensor(f"vst{i}", [128, D], BF16))
                   for i in range(2)]
            vps = [st.enter_context(nc.psum_tensor(f"vps{i}", [128, D], F32))
                   for i in range(2)]
            svps = vps[0]

            fw.op("gpsimd", lambda: nc.gpsimd.dma_start(
                wv_sb[:, :].rearrange("p (a n) -> p a n", a=NDK),
                wv.rearrange("(a p) n -> p a n", p=128)),
                writes=["wv_sb"], dma=True)

            for sc in range(16):
                xb = f"xv{sc % 2}"
                fw.op("gpsimd", lambda sc=sc: nc.gpsimd.dma_start(
                    xc[sc % 2][:, :].rearrange("p (a s) -> p a s", a=NDK),
                    xT.rearrange("(a p) s -> p a s", p=128)[:, :, 512 * sc:512 * (sc + 1)]),
                    writes=[xb], dma=True)
                for sblk in range(4):
                    sb = sc * 4 + sblk
                    pb = f"vps{sb % 2}"
                    for dcb in range(4):
                        for dk in range(NDK):
                            fw.op("tensor", lambda sc=sc, sblk=sblk, sb=sb, dcb=dcb, dk=dk:
                                  nc.tensor.matmul(
                                      vps[sb % 2][:, 512 * dcb:512 * (dcb + 1)],
                                      xc[sc % 2][:, 512 * dk + 128 * sblk:512 * dk + 128 * (sblk + 1)],
                                      wv_sb[:, D * dk + 512 * dcb: D * dk + 512 * (dcb + 1)],
                                      start=(dk == 0), stop=(dk == NDK - 1)),
                                  reads=["wv_sb", xb], writes=[pb])
                    vs = f"vst{sb % 2}"
                    fw.op("vector", lambda sb=sb: nc.vector.tensor_copy(
                        vst[sb % 2][:, :], vps[sb % 2][:, :]),
                        reads=[pb], writes=[vs])
                    fw.op("sync", lambda sb=sb: nc.sync.dma_start(
                        v_scr[sb, :, :], vst[sb % 2][:, :]),
                        reads=[vs], writes=[f"v_scr{sb}"], dma=True)

            for h in range(H):
                for dk in range(NDK):
                    fw.op("tensor", lambda h=h, dk=dk: nc.tensor.matmul(
                        svps[:, 0:65],
                        wv_sb[:, D * dk + 128 * h: D * dk + 128 * (h + 1)],
                        sufXb[:, 65 * dk:65 * (dk + 1)],
                        start=(dk == 0), stop=(dk == NDK - 1)),
                        reads=["wv_sb", "sufXb"], writes=["vps0"])
                fw.op("scalar", lambda h=h: nc.scalar.copy(
                    sufV[:, 65 * h:65 * (h + 1)], svps[:, 0:65]),
                    reads=["vps0"], writes=["sufV"])
        fw.barrier()
        if stop_after == "P1v":
            return nc

        with ExitStack() as st:
            wq_sb = st.enter_context(nc.sbuf_tensor("wq_sb", [128, NDK * D], BF16))
            xc = [st.enter_context(nc.sbuf_tensor(f"xq{i}", [128, NDK * 512], BF16))
                  for i in range(2)]
            qst = [st.enter_context(nc.sbuf_tensor(f"qst{i}", [128, 512], BF16))
                   for i in range(2)]
            qps = st.enter_context(nc.psum_tensor("qps", [128, 4 * 512], F32))

            fw.op("gpsimd", lambda: nc.gpsimd.dma_start(
                wq_sb[:, :].rearrange("p (a n) -> p a n", a=NDK),
                wq.rearrange("(a p) n -> p a n", p=128)),
                writes=["wq_sb"], dma=True)
            for sc in range(4):
                xb = f"xq{sc % 2}"
                fw.op("gpsimd", lambda sc=sc: nc.gpsimd.dma_start(
                    xc[sc % 2][:, :].rearrange("p (a s) -> p a s", a=NDK),
                    x_ownT.rearrange("(a p) s -> p a s", p=128)[:, :, 512 * sc:512 * (sc + 1)]),
                    writes=[xb], dma=True)
                for dcol in range(16):
                    pb = f"qps{dcol % 4}"
                    for dk in range(NDK):
                        fw.op("tensor", lambda sc=sc, dcol=dcol, dk=dk: nc.tensor.matmul(
                            qps[:, 512 * (dcol % 4):512 * (dcol % 4 + 1)],
                            wq_sb[:, D * dk + 128 * dcol: D * dk + 128 * (dcol + 1)],
                            xc[sc % 2][:, 512 * dk:512 * (dk + 1)],
                            start=(dk == 0), stop=(dk == NDK - 1)),
                            reads=["wq_sb", xb], writes=[pb])
                    qs = f"qst{dcol % 2}"
                    fw.op("scalar", lambda dcol=dcol: nc.scalar.copy(
                        qst[dcol % 2][:, :], qps[:, 512 * (dcol % 4):512 * (dcol % 4 + 1)]),
                        reads=[pb], writes=[qs])
                    fw.op("sync", lambda sc=sc, dcol=dcol: nc.sync.dma_start(
                        q_scr[dcol, :, 512 * sc:512 * (sc + 1)], qst[dcol % 2][:, :]),
                        reads=[qs], writes=[f"q_scr{dcol}"], dma=True)
        fw.barrier()
        if stop_after == "P2":
            return nc

        with ExitStack() as st:
            qh = [st.enter_context(nc.sbuf_tensor(f"qh{i}", [128, NT], BF16))
                  for i in range(2)]
            kh = [st.enter_context(nc.sbuf_tensor(f"kh{i}", [128, T], BF16))
                  for i in range(2)]
            vh = [st.enter_context(nc.sbuf_tensor(f"vh{i}", [128, T], BF16))
                  for i in range(2)]
            pp = [st.enter_context(nc.sbuf_tensor(f"pp{i}", [128, 1024], BF16))
                  for i in range(2)]
            pacc_v = st.enter_context(nc.sbuf_tensor("pacc_v", [128, 1024], F32))
            pacc_g = st.enter_context(nc.sbuf_tensor("pacc_g", [128, 1024], F32))
            zc_sb = st.enter_context(nc.sbuf_tensor("zc_sb", [1, 512], F32))
            zi_sb = st.enter_context(nc.sbuf_tensor("zi_sb", [1, 512], F32))
            zb_sb = st.enter_context(nc.sbuf_tensor("zb_sb", [128, 512], F32))
            ast = [st.enter_context(nc.sbuf_tensor(f"ast{i}", [128, 512], BF16))
                   for i in range(2)]
            nps = st.enter_context(nc.psum_tensor("nps", [128, 2 * 512], F32))
            stp = [st.enter_context(nc.psum_tensor(f"stp{i}", [128, 1024], F32))
                   for i in range(2)]
            zps = st.enter_context(nc.psum_tensor("zps", [128, 512], F32))
            zbp = st.enter_context(nc.psum_tensor("zbp", [128, 512], F32))

            engines = [nc.tensor, nc.scalar, nc.vector, nc.gpsimd, nc.sync]
            idxs = []
            for eng, ename in zip(engines, ["tensor", "scalar", "vector", "gpsimd", "sync"]):
                fw._waits(ename, {"DQ": 16})
                r = eng.alloc_register()
                eng.reg_load(r, c_sb[0:1, 0:1])
                idxs.append(RuntimeValue(r, min_val=0, max_val=3))

            first_body = True
            for c in nc.Switch(engines=engines, index=idxs, n=4):
                if first_body:
                    fw.enter_body()
                    first_body = False
                else:
                    fw.body_reset()
                emit_attention(nc, fw, c, qh, kh, vh, pp, pacc_v, pacc_g, zc_sb, zi_sb,
                               zb_sb, ast, nps, stp, zps, zbp,
                               masks, sufV, ones_f, onesr_f,
                               q_scr, k_scr, v_scr, a_scr)
                fw.exit_body()
            fw.finish_switch()
            for h in range(H):
                fw.wrote(f"a_scr{h}", "gpsimd")
        fw.barrier()
        if stop_after == "A":
            return nc

        with ExitStack() as st:
            wo_sb = st.enter_context(nc.sbuf_tensor("wo_sb", [128, NDK * D], BF16))
            atb = [st.enter_context(nc.sbuf_tensor(f"atb{i}", [128, NDK * 512], BF16))
                   for i in range(2)]
            xres = [st.enter_context(nc.sbuf_tensor(f"xres{i}", [128, 512], F32))
                    for i in range(4)]
            r1 = st.enter_context(nc.sbuf_tensor("r1", [128, NDK * 512], F32))
            r1b = [st.enter_context(nc.sbuf_tensor(f"r1b{i}", [128, 512], BF16))
                   for i in range(2)]
            sqb = [st.enter_context(nc.sbuf_tensor(f"sqb{i}", [128, 512], BF16))
                   for i in range(2)]
            trow = st.enter_context(nc.sbuf_tensor("trow", [1, 512], F32))
            mrow = st.enter_context(nc.sbuf_tensor("mrow", [1, 512], F32))
            vrow = st.enter_context(nc.sbuf_tensor("vrow", [1, 512], F32))
            irow = st.enter_context(nc.sbuf_tensor("irow", [1, 512], F32))
            brow = st.enter_context(nc.sbuf_tensor("brow", [1, 512], F32))
            y1fs = [st.enter_context(nc.sbuf_tensor(f"y1fs{i}", [128, 512], F32))
                    for i in range(2)]
            abc_sb = st.enter_context(nc.sbuf_tensor("abc_sb", [128, 512], F32))
            bbc_sb = st.enter_context(nc.sbuf_tensor("bbc_sb", [128, 512], F32))
            y1bs = [st.enter_context(nc.sbuf_tensor(f"y1bs{i}", [128, 512], BF16))
                    for i in range(2)]
            ops_ = st.enter_context(nc.psum_tensor("ops", [128, 4 * 512], F32))
            sumps = st.enter_context(nc.psum_tensor("sumps", [128, 512], F32))
            sqps = st.enter_context(nc.psum_tensor("sqps", [128, 512], F32))
            abc = st.enter_context(nc.psum_tensor("abc", [128, 512], F32))
            bbc = st.enter_context(nc.psum_tensor("bbc", [128, 512], F32))

            fw.op("gpsimd", lambda: nc.gpsimd.dma_start(
                wo_sb[:, :].rearrange("p (a n) -> p a n", a=NDK),
                wo.rearrange("(a p) n -> p a n", p=128)),
                writes=["wo_sb"], dma=True)

            for tb in range(4):
                ab = f"atb{tb % 2}"
                fw.op("sync", lambda tb=tb: nc.sync.dma_start(
                    atb[tb % 2][:, :].rearrange("p (a s) -> p a s", a=NDK),
                    a_scr.rearrange("h p t -> p h t")[:, :, 512 * tb:512 * (tb + 1)]),
                    reads=[f"a_scr{h}" for h in range(H)], writes=[ab], dma=True)
                for dcol in range(16):
                    pb = f"ops{dcol % 4}"
                    xr = f"xres{dcol % 4}"
                    fw.op("sync", lambda tb=tb, dcol=dcol: nc.sync.dma_start(
                        xres[dcol % 4][:, :],
                        x_ownT[128 * dcol:128 * (dcol + 1), 512 * tb:512 * (tb + 1)]),
                        writes=[xr], dma=True)
                    for dk in range(NDK):
                        fw.op("tensor", lambda tb=tb, dcol=dcol, dk=dk: nc.tensor.matmul(
                            ops_[:, 512 * (dcol % 4):512 * (dcol % 4 + 1)],
                            wo_sb[:, D * dk + 128 * dcol: D * dk + 128 * (dcol + 1)],
                            atb[tb % 2][:, 512 * dk:512 * (dk + 1)],
                            start=(dk == 0), stop=(dk == NDK - 1)),
                            reads=["wo_sb", ab], writes=[pb])
                    fw.op("vector", lambda dcol=dcol: nc.vector.tensor_add(
                        r1[:, 512 * dcol:512 * (dcol + 1)],
                        xres[dcol % 4][:, :],
                        ops_[:, 512 * (dcol % 4):512 * (dcol % 4 + 1)]),
                        reads=[xr, pb], writes=["r1"])
                    fw.op("scalar", lambda dcol=dcol: nc.scalar.copy(
                        r1b[dcol % 2][:, :], r1[:, 512 * dcol:512 * (dcol + 1)]),
                        reads=["r1"], writes=[f"r1b{dcol % 2}"])
                    fw.op("scalar", lambda dcol=dcol: nc.scalar.activation(
                        sqb[dcol % 2][:, :], r1[:, 512 * dcol:512 * (dcol + 1)], AF.Square),
                        reads=["r1"], writes=[f"sqb{dcol % 2}"])
                    fw.op("tensor", lambda dcol=dcol: nc.tensor.matmul(
                        sumps[0:1, :], ones_b[:, :], r1b[dcol % 2][:, :],
                        start=(dcol == 0), stop=(dcol == 15)),
                        reads=[f"r1b{dcol % 2}", "ones_b"], writes=["sumps"])
                    fw.op("tensor", lambda dcol=dcol: nc.tensor.matmul(
                        sqps[0:1, :], ones_b[:, :], sqb[dcol % 2][:, :],
                        start=(dcol == 0), stop=(dcol == 15)),
                        reads=[f"sqb{dcol % 2}", "ones_b"], writes=["sqps"])
                fw.op("scalar", lambda: nc.scalar.activation(mrow[:, :], sumps[0:1, :], AF.Identity, scale=1.0 / D),
                      reads=["sumps"], writes=["mrow"])
                fw.op("scalar", lambda: nc.scalar.activation(vrow[:, :], sqps[0:1, :], AF.Identity, scale=1.0 / D),
                      reads=["sqps"], writes=["vrow"])
                fw.op("vector", lambda: nc.vector.tensor_mul(trow[:, :], mrow[:, :], mrow[:, :]),
                      reads=["mrow"], writes=["trow"])
                fw.op("vector", lambda: nc.vector.tensor_sub(vrow[:, :], vrow[:, :], trow[:, :]),
                      reads=["vrow", "trow"], writes=["vrow"])
                fw.op("vector", lambda: nc.vector.tensor_scalar(vrow[:, :], vrow[:, :], EPS, None, ALU.add),
                      reads=["vrow"], writes=["vrow"])
                fw.op("scalar", lambda: nc.scalar.activation(trow[:, :], vrow[:, :], AF.Ln),
                      reads=["vrow"], writes=["trow"])
                fw.op("scalar", lambda: nc.scalar.activation(irow[:, :], trow[:, :], AF.Exp, scale=-0.5),
                      reads=["trow"], writes=["irow"])
                fw.op("vector", lambda: nc.vector.tensor_mul(brow[:, :], mrow[:, :], irow[:, :]),
                      reads=["mrow", "irow"], writes=["brow"])
                fw.op("vector", lambda: nc.vector.tensor_scalar_mul(brow[:, :], brow[:, :], -1.0),
                      reads=["brow"], writes=["brow"])
                fw.op("tensor", lambda: nc.tensor.matmul(
                    abc[:, :], onesr_f[:, :], irow[:, :], start=True, stop=True),
                    reads=["irow", "onesr_f"], writes=["abc"])
                fw.op("tensor", lambda: nc.tensor.matmul(
                    bbc[:, :], onesr_f[:, :], brow[:, :], start=True, stop=True),
                    reads=["brow", "onesr_f"], writes=["bbc"])
                fw.op("scalar", lambda: nc.scalar.copy(abc_sb[:, :], abc[:, :]),
                      reads=["abc", "abc_sb"], writes=["abc_sb"])
                fw.op("scalar", lambda: nc.scalar.copy(bbc_sb[:, :], bbc[:, :]),
                      reads=["bbc", "bbc_sb"], writes=["bbc_sb"])
                for dcol in range(16):
                    yf = f"y1fs{dcol % 2}"
                    yb = f"y1bs{dcol % 2}"
                    fw.op("vector", lambda dcol=dcol: nc.vector.tensor_mul(
                        y1fs[dcol % 2][:, :], r1[:, 512 * dcol:512 * (dcol + 1)], abc_sb[:, :]),
                        reads=["r1", "abc_sb"], writes=[yf])
                    fw.op("vector", lambda dcol=dcol: nc.vector.tensor_add(
                        y1fs[dcol % 2][:, :], y1fs[dcol % 2][:, :], bbc_sb[:, :]),
                        reads=[yf, "bbc_sb"], writes=[yf])
                    fw.op("scalar", lambda dcol=dcol: nc.scalar.copy(
                        y1bs[dcol % 2][:, :], y1fs[dcol % 2][:, :]),
                        reads=[yf], writes=[yb])
                    fw.op("gpsimd", lambda tb=tb, dcol=dcol: nc.gpsimd.dma_start(
                        y1f_scr[dcol, :, 512 * tb:512 * (tb + 1)], y1fs[dcol % 2][:, :]),
                        reads=[yf], writes=[f"y1f_scr{dcol}"], dma=True)
                    fw.op("gpsimd", lambda tb=tb, dcol=dcol: nc.gpsimd.dma_start(
                        y1b_scr[dcol, :, 512 * tb:512 * (tb + 1)], y1bs[dcol % 2][:, :]),
                        reads=[yb], writes=["y1b_scr"], dma=True)
        fw.barrier()
        if stop_after == "O":
            return nc

        with ExitStack() as st:
            y1 = st.enter_context(nc.sbuf_tensor("y1", [128, NDK * NT], BF16))
            wgu = [st.enter_context(nc.sbuf_tensor(f"wgu{i}", [128, 2 * NDK * 128], BF16))
                   for i in range(2)]
            sg = [st.enter_context(nc.sbuf_tensor(f"sg{i}", [128, 1024], F32))
                  for i in range(2)]
            hst = [st.enter_context(nc.sbuf_tensor(f"hst{i}", [128, 1024], BF16))
                   for i in range(2)]
            gps = [st.enter_context(nc.psum_tensor(f"gps{i}", [128, 1024], F32))
                   for i in range(2)]
            ups = [st.enter_context(nc.psum_tensor(f"ups{i}", [128, 1024], F32))
                   for i in range(2)]

            fw.op("scalar", lambda: load_act_set(nc, SET_SILU), inc=False)
            fw.op("sync", lambda: nc.sync.dma_start(
                y1[:, :].rearrange("p (a t) -> p a t", a=NDK),
                y1b_scr.rearrange("a p t -> p a t")),
                reads=["y1b_scr"], writes=["y1"], dma=True)

            for ht in range(HID // 128):
                wb = f"wgu{ht % 2}"
                fw.op("gpsimd", lambda ht=ht: nc.gpsimd.dma_start(
                    wgu[ht % 2][:, 0:NDK * 128].rearrange("p (a n) -> p a n", a=NDK),
                    wgt.rearrange("(a p) n -> p a n", p=128)[:, :, 128 * ht:128 * (ht + 1)]),
                    writes=[wb], dma=True)
                fw.op("gpsimd", lambda ht=ht: nc.gpsimd.dma_start(
                    wgu[ht % 2][:, NDK * 128:2 * NDK * 128].rearrange("p (a n) -> p a n", a=NDK),
                    wut.rearrange("(a p) n -> p a n", p=128)[:, :, 128 * ht:128 * (ht + 1)]),
                    writes=[wb], dma=True)
                for half in range(2):
                    gb, ub = f"gps{half}", f"ups{half}"
                    for dk in range(NDK):
                        for q in range(2):
                            fw.op("tensor", lambda ht=ht, half=half, dk=dk, q=q: nc.tensor.matmul(
                                gps[half][:, 512 * q:512 * (q + 1)],
                                wgu[ht % 2][:, 128 * dk:128 * (dk + 1)],
                                y1[:, NT * dk + 1024 * half + 512 * q: NT * dk + 1024 * half + 512 * (q + 1)],
                                start=(dk == 0), stop=(dk == NDK - 1)),
                                reads=[wb, "y1"], writes=[gb])
                        for q in range(2):
                            fw.op("tensor", lambda ht=ht, half=half, dk=dk, q=q: nc.tensor.matmul(
                                ups[half][:, 512 * q:512 * (q + 1)],
                                wgu[ht % 2][:, NDK * 128 + 128 * dk:NDK * 128 + 128 * (dk + 1)],
                                y1[:, NT * dk + 1024 * half + 512 * q: NT * dk + 1024 * half + 512 * (q + 1)],
                                start=(dk == 0), stop=(dk == NDK - 1)),
                                reads=[wb, "y1"], writes=[ub])
                    sb_ = f"sg{half}"
                    hb = f"hst{half}"
                    fw.op("scalar", lambda half=half: nc.scalar.activation(
                        sg[half][:, :], gps[half][:, :], AF.Silu),
                        reads=[gb], writes=[sb_])
                    fw.op("vector", lambda half=half: nc.vector.tensor_mul(
                        hst[half][:, :], sg[half][:, :], ups[half][:, :]),
                        reads=[sb_, ub], writes=[hb])
                    fw.op("sync", lambda ht=ht, half=half: nc.sync.dma_start(
                        hid_scr[ht, :, 1024 * half:1024 * (half + 1)], hst[half][:, :]),
                        reads=[hb], writes=["hid_scr"], dma=True)
        fw.barrier()
        if stop_after == "F1":
            return nc

        with ExitStack() as st:
            hidh = st.enter_context(nc.sbuf_tensor("hidh", [128, 32 * 1024], BF16))
            wd = [st.enter_context(nc.sbuf_tensor(f"wd{i}", [128, 32 * 128], BF16))
                  for i in range(2)]
            acc = st.enter_context(nc.sbuf_tensor("acc", [128, NDK * 1024], F32))
            y1r = [st.enter_context(nc.sbuf_tensor(f"y1r{i}", [128, 1024], F32))
                   for i in range(2)]
            r2b = [st.enter_context(nc.sbuf_tensor(f"r2b{i}", [128, 1024], BF16))
                   for i in range(2)]
            sq2 = [st.enter_context(nc.sbuf_tensor(f"sq2{i}", [128, 1024], BF16))
                   for i in range(2)]
            trow = st.enter_context(nc.sbuf_tensor("trow2", [1, 1024], F32))
            mrow = st.enter_context(nc.sbuf_tensor("mrow2", [1, 1024], F32))
            vrow = st.enter_context(nc.sbuf_tensor("vrow2", [1, 1024], F32))
            irow = st.enter_context(nc.sbuf_tensor("irow2", [1, 1024], F32))
            brow = st.enter_context(nc.sbuf_tensor("brow2", [1, 1024], F32))
            dps = [st.enter_context(nc.psum_tensor(f"dps{i}", [128, 1024], F32))
                   for i in range(2)]
            sum2 = st.enter_context(nc.psum_tensor("sum2", [128, 1024], F32))
            sq2p = st.enter_context(nc.psum_tensor("sq2p", [128, 1024], F32))

            fw.op("scalar", lambda: load_act_set(nc, SET_LN_EXP), inc=False)

            for tp in range(2):
                for half in range(2):
                    fw.op("sync", lambda tp=tp, half=half: nc.sync.dma_start(
                        hidh[:, :].rearrange("p (a t) -> p a t", a=32),
                        hid_scr.rearrange("a p t -> p a t")[:, 32 * half:32 * (half + 1),
                                                           1024 * tp:1024 * (tp + 1)]),
                        reads=["hid_scr"], writes=["hidh"], dma=True)
                    for dcol in range(16):
                        wb = f"wd{dcol % 2}"
                        fw.op("gpsimd", lambda tp=tp, half=half, dcol=dcol: nc.gpsimd.dma_start(
                            wd[dcol % 2][:, :].rearrange("p (a n) -> p a n", a=32),
                            wdt[4096 * half:4096 * (half + 1),
                                128 * dcol:128 * (dcol + 1)].rearrange("(a p) n -> p a n", p=128)),
                            writes=[wb], dma=True)
                        pb = f"dps{dcol % 2}"
                        for hb in range(32):
                            for q in range(2):
                                fw.op("tensor", lambda half=half, dcol=dcol, hb=hb, q=q:
                                      nc.tensor.matmul(
                                          dps[dcol % 2][:, 512 * q:512 * (q + 1)],
                                          wd[dcol % 2][:, 128 * hb:128 * (hb + 1)],
                                          hidh[:, 1024 * hb + 512 * q:1024 * hb + 512 * (q + 1)],
                                          start=(hb == 0), stop=(hb == 31)),
                                      reads=[wb, "hidh"], writes=[pb])
                        if half == 0:
                            fw.op("vector", lambda dcol=dcol: nc.vector.tensor_copy(
                                acc[:, 1024 * dcol:1024 * (dcol + 1)], dps[dcol % 2][:, :]),
                                reads=[pb], writes=["acc"])
                        else:
                            fw.op("vector", lambda dcol=dcol: nc.vector.tensor_add(
                                acc[:, 1024 * dcol:1024 * (dcol + 1)],
                                acc[:, 1024 * dcol:1024 * (dcol + 1)], dps[dcol % 2][:, :]),
                                reads=[pb, "acc"], writes=["acc"])
                            if "dn_scr" in dbg:
                                fw.op("sync", lambda tp=tp, dcol=dcol: nc.sync.dma_start(
                                    dn_scr[dcol, :, 1024 * tp:1024 * (tp + 1)],
                                    acc[:, 1024 * dcol:1024 * (dcol + 1)]),
                                    reads=["acc"], writes=["dn_scr"], dma=True)
                for dcol in range(16):
                    yb = f"y1r{dcol % 2}"
                    fw.op("sync", lambda tp=tp, dcol=dcol: nc.sync.dma_start(
                        y1r[dcol % 2][:, :],
                        y1f_scr[dcol, :, 1024 * tp:1024 * (tp + 1)]),
                        reads=[f"y1f_scr{dcol}"], writes=[yb], dma=True)
                    fw.op("vector", lambda dcol=dcol: nc.vector.tensor_add(
                        acc[:, 1024 * dcol:1024 * (dcol + 1)],
                        acc[:, 1024 * dcol:1024 * (dcol + 1)], y1r[dcol % 2][:, :]),
                        reads=["acc", yb], writes=["acc"])
                    if "r2_scr" in dbg:
                        fw.op("sync", lambda tp=tp, dcol=dcol: nc.sync.dma_start(
                            r2_scr[dcol, :, 1024 * tp:1024 * (tp + 1)],
                            acc[:, 1024 * dcol:1024 * (dcol + 1)]),
                            reads=["acc"], writes=["r2_scr"], dma=True)
                    fw.op("scalar", lambda dcol=dcol: nc.scalar.copy(
                        r2b[dcol % 2][:, :], acc[:, 1024 * dcol:1024 * (dcol + 1)]),
                        reads=["acc"], writes=[f"r2b{dcol % 2}"])
                    fw.op("scalar", lambda dcol=dcol: nc.scalar.activation(
                        sq2[dcol % 2][:, :], acc[:, 1024 * dcol:1024 * (dcol + 1)], AF.Square),
                        reads=["acc"], writes=[f"sq2{dcol % 2}"])
                    for q in range(2):
                        fw.op("tensor", lambda dcol=dcol, q=q: nc.tensor.matmul(
                            sum2[0:1, 512 * q:512 * (q + 1)], ones_b[:, :],
                            r2b[dcol % 2][:, 512 * q:512 * (q + 1)],
                            start=(dcol == 0), stop=(dcol == 15)),
                            reads=[f"r2b{dcol % 2}", "ones_b"], writes=["sum2"])
                        fw.op("tensor", lambda dcol=dcol, q=q: nc.tensor.matmul(
                            sq2p[0:1, 512 * q:512 * (q + 1)], ones_b[:, :],
                            sq2[dcol % 2][:, 512 * q:512 * (q + 1)],
                            start=(dcol == 0), stop=(dcol == 15)),
                            reads=[f"sq2{dcol % 2}", "ones_b"], writes=["sq2p"])
                fw.op("scalar", lambda: nc.scalar.activation(mrow[:, :], sum2[0:1, :], AF.Identity, scale=1.0 / D),
                      reads=["sum2"], writes=["mrow2"])
                fw.op("scalar", lambda: nc.scalar.activation(vrow[:, :], sq2p[0:1, :], AF.Identity, scale=1.0 / D),
                      reads=["sq2p"], writes=["vrow2"])
                fw.op("vector", lambda: nc.vector.tensor_mul(trow[:, :], mrow[:, :], mrow[:, :]),
                      reads=["mrow2"], writes=["trow2"])
                fw.op("vector", lambda: nc.vector.tensor_sub(vrow[:, :], vrow[:, :], trow[:, :]),
                      reads=["vrow2", "trow2"], writes=["vrow2"])
                fw.op("vector", lambda: nc.vector.tensor_scalar(vrow[:, :], vrow[:, :], EPS, None, ALU.add),
                      reads=["vrow2"], writes=["vrow2"])
                fw.op("scalar", lambda: nc.scalar.activation(trow[:, :], vrow[:, :], AF.Ln),
                      reads=["vrow2"], writes=["trow2"])
                fw.op("scalar", lambda: nc.scalar.activation(irow[:, :], trow[:, :], AF.Exp, scale=-0.5),
                      reads=["trow2"], writes=["irow2"])
                fw.op("vector", lambda: nc.vector.tensor_mul(brow[:, :], mrow[:, :], irow[:, :]),
                      reads=["mrow2", "irow2"], writes=["brow2"])
                fw.op("vector", lambda: nc.vector.tensor_scalar_mul(brow[:, :], brow[:, :], -1.0),
                      reads=["brow2"], writes=["brow2"])
                for q in range(2):
                    fw.op("tensor", lambda q=q: nc.tensor.matmul(
                        dps[0][:, 512 * q:512 * (q + 1)], onesr_f[:, :],
                        irow[0:1, 512 * q:512 * (q + 1)], start=True, stop=True),
                        reads=["irow2"], writes=["dps0"])
                    fw.op("tensor", lambda q=q: nc.tensor.matmul(
                        dps[1][:, 512 * q:512 * (q + 1)], onesr_f[:, :],
                        brow[0:1, 512 * q:512 * (q + 1)], start=True, stop=True),
                        reads=["brow2"], writes=["dps1"])
                fw.op("scalar", lambda: nc.scalar.copy(y1r[0][:, :], dps[0][:, :]),
                      reads=["dps0", "y1r0"], writes=["y1r0"])
                fw.op("scalar", lambda: nc.scalar.copy(y1r[1][:, :], dps[1][:, :]),
                      reads=["dps1", "y1r1"], writes=["y1r1"])
                for dcol in range(16):
                    fw.op("vector", lambda dcol=dcol: nc.vector.tensor_mul(
                        acc[:, 1024 * dcol:1024 * (dcol + 1)],
                        acc[:, 1024 * dcol:1024 * (dcol + 1)], y1r[0][:, :]),
                        reads=["acc", "y1r0"], writes=["acc"])
                    fw.op("vector", lambda dcol=dcol: nc.vector.tensor_add(
                        acc[:, 1024 * dcol:1024 * (dcol + 1)],
                        acc[:, 1024 * dcol:1024 * (dcol + 1)], y1r[1][:, :]),
                        reads=["acc", "y1r1"], writes=["acc"])
                    fw.op("sync", lambda tp=tp, dcol=dcol: nc.sync.dma_start(
                        out_fm[128 * dcol:128 * (dcol + 1), 1024 * tp:1024 * (tp + 1)],
                        acc[:, 1024 * dcol:1024 * (dcol + 1)]),
                        reads=["acc"], writes=["out_fm"], dma=True)
        fw.barrier()
    return nc


def emit_attention(nc, fw, c, qh, kh, vh, pp, pacc_v, pacc_g, zc_sb, zi_sb,
                   zb_sb, ast, nps, stp, zps, zbp, masks, sufV, ones_f, onesr_f,
                   q_scr, k_scr, v_scr, a_scr):
    ns = [8 * c + 4, 8 * c + 8, 60 - 8 * c, 64 - 8 * c]
    counts = [float(T - 128 * n) for n in ns]
    span_sb = ns[3]
    vchunk = span_sb // 8

    def load_head(h):
        par = h % 2
        qb, kb, vb = f"qh{par}", f"kh{par}", f"vh{par}"
        fw.op("sync", lambda h=h: nc.sync.dma_start(
            qh[h % 2][:, :], q_scr[h, :, :]), writes=[qb], dma=True)
        fw.op("sync", lambda h=h: nc.sync.dma_start(
            kh[h % 2][:, 0:128 * span_sb], k_scr[h, :, 0:128 * span_sb]),
            writes=[kb], dma=True)
        for i in range(8):
            fw.op("gpsimd", lambda h=h, i=i: nc.gpsimd.dma_start(
                vh[h % 2][:, 128 * vchunk * i:128 * vchunk * (i + 1)]
                  .rearrange("p (a d) -> p a d", a=vchunk),
                v_scr[vchunk * i:vchunk * (i + 1), :, 128 * h:128 * (h + 1)]
                  .rearrange("a p d -> p a d")),
                writes=[vb], dma=True)

    def finalize(h, tb, ti):
        prow = 0 if ti == 0 else 32
        fw.op("tensor", lambda ti=ti, prow=prow: nc.tensor.matmul(
            zps[prow:prow + 1, :], ones_f[:, :],
            pacc_v[:, 512 * ti:512 * (ti + 1)], start=True, stop=False),
            reads=[f"paccv{ti}", "ones_f"], writes=["zps"])
        fw.op("tensor", lambda ti=ti, prow=prow: nc.tensor.matmul(
            zps[prow:prow + 1, :], ones_f[:, :],
            pacc_g[:, 512 * ti:512 * (ti + 1)], start=False, stop=True),
            reads=[f"paccg{ti}", "ones_f"], writes=["zps"])
        fw.op("vector", lambda tb=tb, prow=prow: nc.vector.tensor_scalar(
            zc_sb[:, :], zps[prow:prow + 1, :], counts[tb], None, ALU.add),
            reads=["zps"], writes=["zc_sb"])
        fw.op("vector", lambda: nc.vector.reciprocal(zi_sb[:, :], zc_sb[:, :]),
              reads=["zc_sb"], writes=["zi_sb"])
        fw.op("tensor", lambda: nc.tensor.matmul(
            zbp[:, :], onesr_f[:, :], zi_sb[:, :], start=True, stop=True),
            reads=["zi_sb", "onesr_f"], writes=["zbp"])
        fw.op("scalar", lambda: nc.scalar.copy(zb_sb[:, :], zbp[:, :]),
              reads=["zbp", "zb_sb"], writes=["zb_sb"])
        apar = tb % 2
        fw.op("vector", lambda h=h, tb=tb, ti=ti, apar=apar: nc.vector.scalar_tensor_tensor(
            ast[apar][:, :], nps[:, 512 * ti:512 * (ti + 1)],
            sufV[:, 65 * h + ns[tb]:65 * h + ns[tb] + 1],
            zb_sb[:, :], ALU.add, ALU.mult),
            reads=[f"nps{ti}", "sufV", "zb_sb"], writes=[f"ast{apar}"])
        fw.op("gpsimd", lambda h=h, tb=tb, apar=apar: nc.gpsimd.dma_start(
            a_scr[h, :, 512 * tb:512 * (tb + 1)], ast[apar][:, :]),
            reads=[f"ast{apar}"], writes=[f"a_scr{h}"], dma=True)

    load_head(0)
    for h in range(H):
        par = h % 2
        qb, kb, vb = f"qh{par}", f"kh{par}", f"vh{par}"
        if h + 1 < H:
            load_head(h + 1)
        for p in range(2):
            lo, hi = 2 * p, 2 * p + 1
            n_lo, n_hi = ns[lo], ns[hi]
            for sb in range(n_hi):
                both = sb < n_lo
                ppar = sb % 2
                spar = sb % 2
                ppb = f"pp{ppar}"
                stL, stH = f"st{spar}L", f"st{spar}H"
                lo0 = 0 if both else 512
                if both:
                    fw.op("tensor", lambda sb=sb, lo=lo, par=par, spar=spar: nc.tensor.matmul(
                        stp[spar][:, 0:512],
                        kh[par][:, 128 * sb:128 * (sb + 1)],
                        qh[par][:, 512 * lo:512 * (lo + 1)],
                        start=True, stop=True),
                        reads=[kb, qb], writes=[stL])
                fw.op("tensor", lambda sb=sb, hi=hi, par=par, spar=spar: nc.tensor.matmul(
                    stp[spar][:, 512:1024],
                    kh[par][:, 128 * sb:128 * (sb + 1)],
                    qh[par][:, 512 * hi:512 * (hi + 1)],
                    start=True, stop=True),
                    reads=[kb, qb], writes=[stH])
                if both and n_lo - 4 <= sb < n_lo:
                    k = sb - (n_lo - 4)
                    fw.op("vector", lambda k=k, spar=spar: nc.vector.tensor_mul(
                        stp[spar][:, 0:512], stp[spar][:, 0:512],
                        masks[:, 512 * k:512 * (k + 1)]),
                        reads=[stL, "masks"], writes=[stL])
                if n_hi - 4 <= sb < n_hi:
                    k = sb - (n_hi - 4)
                    fw.op("vector", lambda k=k, spar=spar: nc.vector.tensor_mul(
                        stp[spar][:, 512:1024], stp[spar][:, 512:1024],
                        masks[:, 512 * k:512 * (k + 1)]),
                        reads=[stH, "masks"], writes=[stH])
                st_reads = ([stL, stH] if both else [stH])
                fw.op("scalar", lambda ppar=ppar, lo0=lo0, spar=spar: nc.scalar.activation(
                    pp[ppar][:, lo0:1024], stp[spar][:, lo0:1024], AF.Exp, scale=SCALE),
                    reads=st_reads, writes=[ppb])
                pkeys = (["paccv0", "paccv1"] if both else ["paccv1"]) if sb % 2 == 0 \
                    else (["paccg0", "paccg1"] if both else ["paccg1"])
                if sb == 0:
                    fw.op("vector", lambda ppar=ppar: nc.vector.tensor_copy(
                        pacc_v[:, :], pp[ppar][:, :]),
                        reads=[ppb], writes=["paccv0", "paccv1"])
                elif sb == 1:
                    fw.op("gpsimd", lambda ppar=ppar: nc.gpsimd.tensor_copy(
                        pacc_g[:, :], pp[ppar][:, :]),
                        reads=[ppb], writes=["paccg0", "paccg1"])
                elif sb % 2 == 0:
                    fw.op("vector", lambda ppar=ppar, lo0=lo0: nc.vector.tensor_add(
                        pacc_v[:, lo0:1024], pacc_v[:, lo0:1024], pp[ppar][:, lo0:1024]),
                        reads=[ppb] + pkeys, writes=pkeys)
                else:
                    fw.op("gpsimd", lambda ppar=ppar, lo0=lo0: nc.gpsimd.tensor_add(
                        pacc_g[:, lo0:1024], pacc_g[:, lo0:1024], pp[ppar][:, lo0:1024]),
                        reads=[ppb] + pkeys, writes=pkeys)
                if both:
                    fw.op("tensor", lambda sb=sb, par=par, ppar=ppar, n_lo=n_lo:
                          nc.tensor.matmul(
                              nps[:, 0:512],
                              vh[par][:, 128 * sb:128 * (sb + 1)],
                              pp[ppar][:, 0:512],
                              start=(sb == 0), stop=(sb == n_lo - 1)),
                          reads=[vb, ppb], writes=["nps0"])
                fw.op("tensor", lambda sb=sb, par=par, ppar=ppar, n_hi=n_hi:
                      nc.tensor.matmul(
                          nps[:, 512:1024],
                          vh[par][:, 128 * sb:128 * (sb + 1)],
                          pp[ppar][:, 512:1024],
                          start=(sb == 0), stop=(sb == n_hi - 1)),
                      reads=[vb, ppb], writes=["nps1"])
                if sb == n_lo + 1:
                    finalize(h, lo, 0)
            finalize(h, hi, 1)


def kernel(L0_attn_x, L0_attn_Wq, L0_attn_Wk, L0_attn_Wv, L0_attn_Wo,
           L0_ffn_w_gate, L0_ffn_w_up, L0_ffn_w_down):
    x = np.asarray(L0_attn_x)
    wq = np.ascontiguousarray(np.asarray(L0_attn_Wq), dtype=np.float32)
    wk = np.ascontiguousarray(np.asarray(L0_attn_Wk), dtype=np.float32)
    wv = np.ascontiguousarray(np.asarray(L0_attn_Wv), dtype=np.float32)
    wo = np.ascontiguousarray(np.asarray(L0_attn_Wo), dtype=np.float32)
    wgt = np.ascontiguousarray(np.asarray(L0_ffn_w_gate).T, dtype=np.float32)
    wut = np.ascontiguousarray(np.asarray(L0_ffn_w_up).T, dtype=np.float32)
    wdt = np.ascontiguousarray(np.asarray(L0_ffn_w_down).T, dtype=np.float32)

    xT = [np.ascontiguousarray(x[b].T, dtype=np.float32) for b in range(B)]

    nc = build_program()
    in_maps = []
    for core in range(8):
        b, c = core // 4, core % 4
        xo = np.concatenate([xT[b][:, 1024 * c:1024 * (c + 1)],
                             xT[b][:, 1024 * (7 - c):1024 * (8 - c)]], axis=1)
        in_maps.append({
            "xT": xT[b],
            "x_ownT": np.ascontiguousarray(xo),
            "wq": wq, "wk": wk, "wv": wv, "wo": wo,
            "wgt": wgt, "wut": wut, "wdt": wdt,
            "cidx": np.array([[c]], dtype=np.int32),
        })
    res = run_bass_kernel_spmd(nc, in_maps, core_ids=list(range(8)))
    out = np.empty((B, T, D), dtype=np.float32)
    for core in range(8):
        b, c = core // 4, core % 4
        o = np.asarray(res.results[core]["out_fm"])
        out[b, 1024 * c:1024 * (c + 1), :] = o[:, 0:1024].T
        out[b, 1024 * (7 - c):1024 * (8 - c), :] = o[:, 1024:2048].T
    return out


# version 04
# speedup vs baseline: 1.0538x; 1.0538x over previous
import math
import os as _os
import numpy as np
import ml_dtypes
import concourse.bass as bass
import concourse.mybir as mybir
from concourse.expressions import RuntimeValue
from concourse.bass_utils import run_bass_kernel_spmd

F32 = mybir.dt.float32
BF16 = mybir.dt.bfloat16
I32 = mybir.dt.int32
AF = mybir.ActivationFunctionType
ALU = mybir.AluOpType
AX = mybir.AxisListType

D = 2048
T = 8192
B = 2
H = 16
DH = 128
HID = 8192
NT = 2048
SCALE = 1.0 / math.sqrt(T)
EPS = 1e-5
NDK = D // 128
NSB = T // 128

SET_LN_EXP = 6
SET_SILU = 18


def load_act_set(nc, set_id):
    return nc.scalar.add_instruction(mybir.InstLoadActFuncSet(
        name=nc.get_next_instruction_name(),
        act_func_set_id=set_id, ins=[], outs=[]))


class FW:
    ENG_SEM = {
        "tensor": ("T", None),
        "scalar": ("S", None),
        "vector": ("V", None),
        "gpsimd": ("PG", "DG"),
        "sync": ("SY", "DQ"),
    }
    ALL_SEMS = ["T", "S", "V", "PG", "SY", "DG", "DQ"]

    def __init__(self, nc):
        self.nc = nc
        self.sem = {}
        self.scope = ""
        self.cnt = {}
        self.known = {}
        self.bufs = {}
        for s in self.ALL_SEMS:
            for suf in ("", "b"):
                k = s + suf
                self.sem[k] = nc.alloc_semaphore(name="sem_" + k)
                self.cnt[k] = 0

    def semkey(self, engine, dma):
        c, d = self.ENG_SEM[engine]
        base = d if dma else c
        if self.scope and base in ("T", "S", "V", "PG", "SY", "DG", "DQ"):
            return base + self.scope
        return base

    def enter_body(self):
        self.scope = "b"
        for s in self.ALL_SEMS:
            self.cnt[s + "b"] = 0
        self.known = {k: v for k, v in self.known.items() if not k[1].endswith("b")}
        self._saved_bufs = {k: {"w": dict(v["w"]), "r": dict(v["r"])}
                            for k, v in self.bufs.items()}
        self._saved_known = dict(self.known)
        self._saved_cnt = dict(self.cnt)

    def exit_body(self):
        nc = self.nc
        finals = {s + "b": self.cnt[s + "b"] for s in self.ALL_SEMS}
        for ename in self.ENG_SEM:
            eng = getattr(nc, ename)
            for sk, v in finals.items():
                if v > 0:
                    eng.wait_ge(self.sem[sk], v)
            gk = self.ENG_SEM[ename][0]
            eng.drain().then_inc(self.sem[gk], 1)
        self.scope = ""

    def body_reset(self):
        self.scope = "b"
        self.bufs = {k: {"w": dict(v["w"]), "r": dict(v["r"])}
                     for k, v in self._saved_bufs.items()}
        self.known = dict(self._saved_known)
        self.cnt = dict(self._saved_cnt)
        for s in self.ALL_SEMS:
            self.cnt[s + "b"] = 0

    def finish_switch(self):
        self.scope = ""
        for ename in self.ENG_SEM:
            gk = self.ENG_SEM[ename][0]
            self.cnt[gk] += 1

    def wrote(self, buf, engine):
        gk = self.ENG_SEM[engine][0]
        self.bufs[buf] = {"w": {gk: self.cnt[gk]}, "r": {}}

    def _waits(self, ename, deps):
        eng = getattr(self.nc, ename)
        for sk, v in sorted(deps.items()):
            if v <= 0:
                continue
            kk = (ename, sk)
            if self.known.get(kk, 0) >= v:
                continue
            eng.wait_ge(self.sem[sk], v)
            self.known[kk] = v

    def op(self, ename, fn, reads=(), writes=(), dma=False, inc=True):
        deps = {}
        my_sk = self.semkey(ename, dma)
        my_csk = self.semkey(ename, False)

        def merge(d):
            for sk, v in d.items():
                if sk == my_csk and not dma and ename == "tensor":
                    continue
                if deps.get(sk, 0) < v:
                    deps[sk] = v

        for b in reads:
            st = self.bufs.get(b)
            if st:
                merge(st["w"])
        for b in writes:
            st = self.bufs.get(b)
            if st:
                merge(st["w"])
                merge(st["r"])
        self._waits(ename, deps)
        inst = fn()
        if inc:
            amt = 16 if dma else 1
            inst.then_inc(self.sem[my_sk], amt)
            self.cnt[my_sk] += amt
        tok = (my_sk, self.cnt[my_sk])
        for b in writes:
            self.bufs[b] = {"w": {tok[0]: tok[1]}, "r": {}}
        for b in reads:
            st = self.bufs.setdefault(b, {"w": {}, "r": {}})
            if st["r"].get(tok[0], 0) < tok[1]:
                st["r"][tok[0]] = tok[1]
        return tok

    def barrier(self):
        assert self.scope == ""
        nc = self.nc
        for ename in self.ENG_SEM:
            gk = self.ENG_SEM[ename][0]
            getattr(nc, ename).drain().then_inc(self.sem[gk], 1)
            self.cnt[gk] += 1
        targets = {s: self.cnt[s] for s in self.ALL_SEMS}
        for ename in self.ENG_SEM:
            eng = getattr(nc, ename)
            for s, v in targets.items():
                if v > 0:
                    eng.wait_ge(self.sem[s], v)
            for s, v in targets.items():
                self.known[(ename, s)] = v
        for b in self.bufs.values():
            pass


def build_program(stop_after=None, dbg=()):
    KSC = int(_os.environ.get("KDBG_SC", "16"))
    KSKIP = set(_os.environ.get("KDBG_SKIP", "").split(","))
    """stop_after in {None,'P1k','P1v','P2','A','O','F1'}; dbg: iterable of
    scratch names to expose as ExternalOutput."""
    nc = bass.Bass("TRN2", target_bir_lowering=False)

    def kind_of(name):
        return "ExternalOutput" if name in dbg else "Internal"

    xT = nc.dram_tensor("xT", [D, T], F32, kind="ExternalInput").ap()
    x_ownT = nc.dram_tensor("x_ownT", [D, NT], F32, kind="ExternalInput").ap()
    wq = nc.dram_tensor("wq", [D, D], F32, kind="ExternalInput").ap()
    wk = nc.dram_tensor("wk", [D, D], F32, kind="ExternalInput").ap()
    wv = nc.dram_tensor("wv", [D, D], F32, kind="ExternalInput").ap()
    wo = nc.dram_tensor("wo", [D, D], F32, kind="ExternalInput").ap()
    wgt = nc.dram_tensor("wgt", [D, HID], F32, kind="ExternalInput").ap()
    wut = nc.dram_tensor("wut", [D, HID], F32, kind="ExternalInput").ap()
    wdt = nc.dram_tensor("wdt", [HID, D], F32, kind="ExternalInput").ap()
    cidx = nc.dram_tensor("cidx", [1, 1], I32, kind="ExternalInput").ap()
    out_fm = nc.dram_tensor("out_fm", [D, NT], F32, kind="ExternalOutput").ap()
    k_scr = nc.dram_tensor("k_scr", [H, 128, T], BF16, kind=kind_of("k_scr")).ap()
    v_scr = nc.dram_tensor("v_scr", [NSB, 128, D], BF16, kind=kind_of("v_scr")).ap()
    q_scr = nc.dram_tensor("q_scr", [H, 128, NT], BF16, kind=kind_of("q_scr")).ap()
    a_scr = nc.dram_tensor("a_scr", [H, 128, NT], BF16, kind=kind_of("a_scr")).ap()
    y1f_scr = nc.dram_tensor("y1f_scr", [NDK, 128, NT], F32, kind=kind_of("y1f_scr")).ap()
    y1b_scr = nc.dram_tensor("y1b_scr", [NDK, 128, NT], BF16, kind=kind_of("y1b_scr")).ap()
    hid_scr = nc.dram_tensor("hid_scr", [HID // 128, 128, NT], BF16, kind=kind_of("hid_scr")).ap()
    dn_scr = nc.dram_tensor("dn_scr", [NDK, 128, NT], F32, kind=kind_of("dn_scr")).ap()
    r2_scr = nc.dram_tensor("r2_scr", [NDK, 128, NT], F32, kind=kind_of("r2_scr")).ap()

    fw = FW(nc)
    from contextlib import ExitStack

    with ExitStack() as outer:
        c_sb = outer.enter_context(nc.sbuf_tensor("c_sb", [1, 1], I32))
        ones_f = outer.enter_context(nc.sbuf_tensor("ones_f", [128, 1], F32))
        ones_b = outer.enter_context(nc.sbuf_tensor("ones_b", [128, 1], BF16))
        onesr_f = outer.enter_context(nc.sbuf_tensor("onesr_f", [1, 128], F32))
        masks = outer.enter_context(nc.sbuf_tensor("masks", [128, 4 * 512], F32))
        iotat = outer.enter_context(nc.sbuf_tensor("iotat", [128, 512], F32))
        sufA = outer.enter_context(nc.sbuf_tensor("sufA", [128, 65 * NDK], F32))
        sufB = outer.enter_context(nc.sbuf_tensor("sufB", [128, 65 * NDK], F32))
        sufXb = outer.enter_context(nc.sbuf_tensor("sufXb", [128, NDK * 65], BF16))
        sufV = outer.enter_context(nc.sbuf_tensor("sufV", [128, H * 65], F32))

        fw.op("sync", lambda: nc.sync.dma_start(c_sb[:, :], cidx),
              writes=["c_sb"], dma=True)
        if "init" not in KSKIP:
            fw.op("vector", lambda: nc.vector.memset(ones_f[:, :], 1.0), writes=["ones_f"])
            fw.op("vector", lambda: nc.vector.memset(ones_b[:, :], 1.0), writes=["ones_b"])
            fw.op("vector", lambda: nc.vector.memset(onesr_f[:, :], 1.0), writes=["onesr_f"])
            fw.op("scalar", lambda: load_act_set(nc, SET_LN_EXP), inc=False)
        for k in range(4 if "init" not in KSKIP else 0):
            fw.op("gpsimd",
                  lambda k=k: nc.gpsimd.iota(
                      iotat[:, :], [[1, 512]], base=-(128 * k),
                      channel_multiplier=-1,
                      allow_small_or_imprecise_dtypes=True),
                  writes=["iotat"])
            fw.op("vector",
                  lambda k=k: nc.vector.tensor_single_scalar(
                      masks[:, 512 * k:512 * (k + 1)], iotat[:, :], 0.0, ALU.is_ge),
                  reads=["iotat"], writes=["masks"])

        with ExitStack() as st:
            wk_sb = st.enter_context(nc.sbuf_tensor("wk_sb", [128, NDK * D], BF16))
            xc = [st.enter_context(nc.sbuf_tensor(f"xc{i}", [128, NDK * 512], BF16))
                  for i in range(2)]
            kst = [st.enter_context(nc.sbuf_tensor(f"kst{i}", [128, 512], BF16))
                   for i in range(2)]
            kps = st.enter_context(nc.psum_tensor("kps", [128, 4 * 512], F32))

            fw.op("gpsimd", lambda: nc.gpsimd.dma_start(
                wk_sb[:, :].rearrange("p (a n) -> p a n", a=NDK),
                wk.rearrange("(a p) n -> p a n", p=128)),
                writes=["wk_sb"], dma=True)

            for sc in range(KSC):
                xb = f"xc{sc % 2}"
                fw.op("gpsimd", lambda sc=sc: nc.gpsimd.dma_start(
                    xc[sc % 2][:, :].rearrange("p (a s) -> p a s", a=NDK),
                    xT.rearrange("(a p) s -> p a s", p=128)[:, :, 512 * sc:512 * (sc + 1)]),
                    writes=[xb], dma=True)
                for sb in range(4 if "bsum" not in KSKIP else 0):
                    j = sc * 4 + sb
                    fw.op("vector", lambda sc=sc, sb=sb, j=j: nc.vector.reduce_sum(
                        sufA[:, 16 * j:16 * (j + 1)].rearrange("p (a o) -> p a o", o=1),
                        xc[sc % 2][:, :].rearrange("p (a s) -> p a s", a=NDK)[:, :, 128 * sb:128 * (sb + 1)],
                        axis=AX.X),
                        reads=[xb], writes=["sufA"])
                for dcol in range(int(_os.environ.get("KDBG_DCOL", "16")) if "kproj" not in KSKIP else 0):
                    pb = f"kps{dcol % 4}"
                    for dk in range(NDK):
                        fw.op("tensor", lambda sc=sc, dcol=dcol, dk=dk: nc.tensor.matmul(
                            kps[:, 512 * (dcol % 4):512 * (dcol % 4 + 1)],
                            wk_sb[:, D * dk + 128 * dcol: D * dk + 128 * (dcol + 1)],
                            xc[sc % 2][:, 512 * dk:512 * (dk + 1)],
                            start=(dk == 0), stop=(dk == NDK - 1)),
                            reads=["wk_sb", xb], writes=[pb])
                    ks = f"kst{dcol % 2}"
                    if "evac" not in KSKIP:
                        fw.op("scalar", lambda dcol=dcol: nc.scalar.copy(
                            kst[dcol % 2][:, :], kps[:, 512 * (dcol % 4):512 * (dcol % 4 + 1)]),
                            reads=[pb], writes=[ks])
                    if "kwr" not in KSKIP:
                        fw.op("sync", lambda sc=sc, dcol=dcol: nc.sync.dma_start(
                            k_scr[dcol, :, 512 * sc:512 * (sc + 1)], kst[dcol % 2][:, :]),
                            reads=[ks], writes=[f"k_scr{dcol}"], dma=True)

            if "suf" in KSKIP:
                fw.barrier()
                if stop_after == "P1k":
                    return nc
            fw.op("vector", lambda: nc.vector.memset(sufA[:, 64 * 16:65 * 16], 0.0),
                  writes=["sufA"])
            cur, oth = sufA, sufB
            curb, othb = "sufA", "sufB"
            for lvl in range(6):
                sh = 1 << lvl
                w = (65 - sh) * 16
                fw.op("vector", lambda cur=cur, oth=oth, sh=sh, w=w: nc.vector.tensor_add(
                    oth[:, 0:w], cur[:, 0:w], cur[:, 16 * sh:16 * sh + w]),
                    reads=[curb], writes=[othb])
                fw.op("vector", lambda cur=cur, oth=oth, w=w: nc.vector.tensor_copy(
                    oth[:, w:65 * 16], cur[:, w:65 * 16]),
                    reads=[curb], writes=[othb])
                cur, oth = oth, cur
                curb, othb = othb, curb
            fw.op("scalar", lambda cur=cur: nc.scalar.copy(
                sufXb[:, :].rearrange("p (a j) -> p a j", a=NDK),
                cur[:, :].rearrange("p (j a) -> p a j", a=NDK)),
                reads=[curb], writes=["sufXb"])
        fw.barrier()
        if stop_after == "P1k":
            return nc

        with ExitStack() as st:
            wv_sb = st.enter_context(nc.sbuf_tensor("wv_sb", [128, NDK * D], BF16))
            xc = [st.enter_context(nc.sbuf_tensor(f"xv{i}", [128, NDK * 512], BF16))
                  for i in range(2)]
            vst = [st.enter_context(nc.sbuf_tensor(f"vst{i}", [128, D], BF16))
                   for i in range(2)]
            vps = [st.enter_context(nc.psum_tensor(f"vps{i}", [128, D], F32))
                   for i in range(2)]
            svps = vps[0]

            fw.op("gpsimd", lambda: nc.gpsimd.dma_start(
                wv_sb[:, :].rearrange("p (a n) -> p a n", a=NDK),
                wv.rearrange("(a p) n -> p a n", p=128)),
                writes=["wv_sb"], dma=True)

            for sc in range(16):
                xb = f"xv{sc % 2}"
                fw.op("gpsimd", lambda sc=sc: nc.gpsimd.dma_start(
                    xc[sc % 2][:, :].rearrange("p (a s) -> p a s", a=NDK),
                    xT.rearrange("(a p) s -> p a s", p=128)[:, :, 512 * sc:512 * (sc + 1)]),
                    writes=[xb], dma=True)
                for sblk in range(4):
                    sb = sc * 4 + sblk
                    pb = f"vps{sb % 2}"
                    for dcb in range(4):
                        for dk in range(NDK):
                            fw.op("tensor", lambda sc=sc, sblk=sblk, sb=sb, dcb=dcb, dk=dk:
                                  nc.tensor.matmul(
                                      vps[sb % 2][:, 512 * dcb:512 * (dcb + 1)],
                                      xc[sc % 2][:, 512 * dk + 128 * sblk:512 * dk + 128 * (sblk + 1)],
                                      wv_sb[:, D * dk + 512 * dcb: D * dk + 512 * (dcb + 1)],
                                      start=(dk == 0), stop=(dk == NDK - 1)),
                                  reads=["wv_sb", xb], writes=[pb])
                    vs = f"vst{sb % 2}"
                    fw.op("vector", lambda sb=sb: nc.vector.tensor_copy(
                        vst[sb % 2][:, :], vps[sb % 2][:, :]),
                        reads=[pb], writes=[vs])
                    fw.op("sync", lambda sb=sb: nc.sync.dma_start(
                        v_scr[sb, :, :], vst[sb % 2][:, :]),
                        reads=[vs], writes=[f"v_scr{sb}"], dma=True)

            for h in range(H):
                for dk in range(NDK):
                    fw.op("tensor", lambda h=h, dk=dk: nc.tensor.matmul(
                        svps[:, 0:65],
                        wv_sb[:, D * dk + 128 * h: D * dk + 128 * (h + 1)],
                        sufXb[:, 65 * dk:65 * (dk + 1)],
                        start=(dk == 0), stop=(dk == NDK - 1)),
                        reads=["wv_sb", "sufXb"], writes=["vps0"])
                fw.op("scalar", lambda h=h: nc.scalar.copy(
                    sufV[:, 65 * h:65 * (h + 1)], svps[:, 0:65]),
                    reads=["vps0"], writes=["sufV"])
        fw.barrier()
        if stop_after == "P1v":
            return nc

        with ExitStack() as st:
            wq_sb = st.enter_context(nc.sbuf_tensor("wq_sb", [128, NDK * D], BF16))
            xc = [st.enter_context(nc.sbuf_tensor(f"xq{i}", [128, NDK * 512], BF16))
                  for i in range(2)]
            qst = [st.enter_context(nc.sbuf_tensor(f"qst{i}", [128, 512], BF16))
                   for i in range(2)]
            qps = st.enter_context(nc.psum_tensor("qps", [128, 4 * 512], F32))

            fw.op("gpsimd", lambda: nc.gpsimd.dma_start(
                wq_sb[:, :].rearrange("p (a n) -> p a n", a=NDK),
                wq.rearrange("(a p) n -> p a n", p=128)),
                writes=["wq_sb"], dma=True)
            for sc in range(4):
                xb = f"xq{sc % 2}"
                fw.op("gpsimd", lambda sc=sc: nc.gpsimd.dma_start(
                    xc[sc % 2][:, :].rearrange("p (a s) -> p a s", a=NDK),
                    x_ownT.rearrange("(a p) s -> p a s", p=128)[:, :, 512 * sc:512 * (sc + 1)]),
                    writes=[xb], dma=True)
                for dcol in range(16):
                    pb = f"qps{dcol % 4}"
                    for dk in range(NDK):
                        fw.op("tensor", lambda sc=sc, dcol=dcol, dk=dk: nc.tensor.matmul(
                            qps[:, 512 * (dcol % 4):512 * (dcol % 4 + 1)],
                            wq_sb[:, D * dk + 128 * dcol: D * dk + 128 * (dcol + 1)],
                            xc[sc % 2][:, 512 * dk:512 * (dk + 1)],
                            start=(dk == 0), stop=(dk == NDK - 1)),
                            reads=["wq_sb", xb], writes=[pb])
                    qs = f"qst{dcol % 2}"
                    fw.op("scalar", lambda dcol=dcol: nc.scalar.copy(
                        qst[dcol % 2][:, :], qps[:, 512 * (dcol % 4):512 * (dcol % 4 + 1)]),
                        reads=[pb], writes=[qs])
                    fw.op("sync", lambda sc=sc, dcol=dcol: nc.sync.dma_start(
                        q_scr[dcol, :, 512 * sc:512 * (sc + 1)], qst[dcol % 2][:, :]),
                        reads=[qs], writes=[f"q_scr{dcol}"], dma=True)
        fw.barrier()
        if stop_after == "P2":
            return nc

        with ExitStack() as st:
            qh = [st.enter_context(nc.sbuf_tensor(f"qh{i}", [128, NT], BF16))
                  for i in range(2)]
            kh = [st.enter_context(nc.sbuf_tensor(f"kh{i}", [128, T], BF16))
                  for i in range(2)]
            vh = [st.enter_context(nc.sbuf_tensor(f"vh{i}", [128, T], BF16))
                  for i in range(2)]
            pp = [st.enter_context(nc.sbuf_tensor(f"pp{i}", [128, 1024], BF16))
                  for i in range(4)]
            ptmp = [st.enter_context(nc.sbuf_tensor(f"ptmp{i}", [128, 1024], BF16))
                    for i in range(2)]
            pacc_v = st.enter_context(nc.sbuf_tensor("pacc_v", [128, 1024], F32))
            zc_sb = st.enter_context(nc.sbuf_tensor("zc_sb", [1, 512], F32))
            zi_sb = st.enter_context(nc.sbuf_tensor("zi_sb", [1, 512], F32))
            zb_sb = st.enter_context(nc.sbuf_tensor("zb_sb", [128, 512], F32))
            ast = [st.enter_context(nc.sbuf_tensor(f"ast{i}", [128, 512], BF16))
                   for i in range(2)]
            nps = st.enter_context(nc.psum_tensor("nps", [128, 2 * 512], F32))
            stp = [st.enter_context(nc.psum_tensor(f"stp{i}", [128, 1024], F32))
                   for i in range(2)]
            zps = st.enter_context(nc.psum_tensor("zps", [128, 512], F32))
            zbp = st.enter_context(nc.psum_tensor("zbp", [128, 512], F32))

            engines = [nc.tensor, nc.scalar, nc.vector, nc.gpsimd, nc.sync]
            idxs = []
            for eng, ename in zip(engines, ["tensor", "scalar", "vector", "gpsimd", "sync"]):
                fw._waits(ename, {"DQ": 16})
                r = eng.alloc_register()
                eng.reg_load(r, c_sb[0:1, 0:1])
                idxs.append(RuntimeValue(r, min_val=0, max_val=3))

            first_body = True
            for c in nc.Switch(engines=engines, index=idxs, n=4):
                if first_body:
                    fw.enter_body()
                    first_body = False
                else:
                    fw.body_reset()
                emit_attention(nc, fw, c, qh, kh, vh, pp, ptmp, pacc_v, zc_sb, zi_sb,
                               zb_sb, ast, nps, stp, zps, zbp,
                               masks, sufV, ones_f, onesr_f,
                               q_scr, k_scr, v_scr, a_scr)
                fw.exit_body()
            fw.finish_switch()
            for h in range(H):
                fw.wrote(f"a_scr{h}", "gpsimd")
        fw.barrier()
        if stop_after == "A":
            return nc

        with ExitStack() as st:
            wo_sb = st.enter_context(nc.sbuf_tensor("wo_sb", [128, NDK * D], BF16))
            atb = [st.enter_context(nc.sbuf_tensor(f"atb{i}", [128, NDK * 512], BF16))
                   for i in range(2)]
            xres = [st.enter_context(nc.sbuf_tensor(f"xres{i}", [128, 512], F32))
                    for i in range(4)]
            r1 = st.enter_context(nc.sbuf_tensor("r1", [128, NDK * 512], F32))
            r1b = [st.enter_context(nc.sbuf_tensor(f"r1b{i}", [128, 512], BF16))
                   for i in range(2)]
            sqb = [st.enter_context(nc.sbuf_tensor(f"sqb{i}", [128, 512], BF16))
                   for i in range(2)]
            trow = st.enter_context(nc.sbuf_tensor("trow", [1, 512], F32))
            mrow = st.enter_context(nc.sbuf_tensor("mrow", [1, 512], F32))
            vrow = st.enter_context(nc.sbuf_tensor("vrow", [1, 512], F32))
            irow = st.enter_context(nc.sbuf_tensor("irow", [1, 512], F32))
            brow = st.enter_context(nc.sbuf_tensor("brow", [1, 512], F32))
            y1fs = [st.enter_context(nc.sbuf_tensor(f"y1fs{i}", [128, 512], F32))
                    for i in range(2)]
            abc_sb = st.enter_context(nc.sbuf_tensor("abc_sb", [128, 512], F32))
            bbc_sb = st.enter_context(nc.sbuf_tensor("bbc_sb", [128, 512], F32))
            y1bs = [st.enter_context(nc.sbuf_tensor(f"y1bs{i}", [128, 512], BF16))
                    for i in range(2)]
            ops_ = st.enter_context(nc.psum_tensor("ops", [128, 4 * 512], F32))
            sumps = st.enter_context(nc.psum_tensor("sumps", [128, 512], F32))
            sqps = st.enter_context(nc.psum_tensor("sqps", [128, 512], F32))
            abc = st.enter_context(nc.psum_tensor("abc", [128, 512], F32))
            bbc = st.enter_context(nc.psum_tensor("bbc", [128, 512], F32))

            fw.op("gpsimd", lambda: nc.gpsimd.dma_start(
                wo_sb[:, :].rearrange("p (a n) -> p a n", a=NDK),
                wo.rearrange("(a p) n -> p a n", p=128)),
                writes=["wo_sb"], dma=True)

            for tb in range(4):
                ab = f"atb{tb % 2}"
                fw.op("sync", lambda tb=tb: nc.sync.dma_start(
                    atb[tb % 2][:, :].rearrange("p (a s) -> p a s", a=NDK),
                    a_scr.rearrange("h p t -> p h t")[:, :, 512 * tb:512 * (tb + 1)]),
                    reads=[f"a_scr{h}" for h in range(H)], writes=[ab], dma=True)
                for dcol in range(16):
                    pb = f"ops{dcol % 4}"
                    xr = f"xres{dcol % 4}"
                    fw.op("sync", lambda tb=tb, dcol=dcol: nc.sync.dma_start(
                        xres[dcol % 4][:, :],
                        x_ownT[128 * dcol:128 * (dcol + 1), 512 * tb:512 * (tb + 1)]),
                        writes=[xr], dma=True)
                    for dk in range(NDK):
                        fw.op("tensor", lambda tb=tb, dcol=dcol, dk=dk: nc.tensor.matmul(
                            ops_[:, 512 * (dcol % 4):512 * (dcol % 4 + 1)],
                            wo_sb[:, D * dk + 128 * dcol: D * dk + 128 * (dcol + 1)],
                            atb[tb % 2][:, 512 * dk:512 * (dk + 1)],
                            start=(dk == 0), stop=(dk == NDK - 1)),
                            reads=["wo_sb", ab], writes=[pb])
                    fw.op("vector", lambda dcol=dcol: nc.vector.tensor_add(
                        r1[:, 512 * dcol:512 * (dcol + 1)],
                        xres[dcol % 4][:, :],
                        ops_[:, 512 * (dcol % 4):512 * (dcol % 4 + 1)]),
                        reads=[xr, pb], writes=["r1"])
                    fw.op("scalar", lambda dcol=dcol: nc.scalar.copy(
                        r1b[dcol % 2][:, :], r1[:, 512 * dcol:512 * (dcol + 1)]),
                        reads=["r1"], writes=[f"r1b{dcol % 2}"])
                    fw.op("scalar", lambda dcol=dcol: nc.scalar.activation(
                        sqb[dcol % 2][:, :], r1[:, 512 * dcol:512 * (dcol + 1)], AF.Square),
                        reads=["r1"], writes=[f"sqb{dcol % 2}"])
                    fw.op("tensor", lambda dcol=dcol: nc.tensor.matmul(
                        sumps[0:1, :], ones_b[:, :], r1b[dcol % 2][:, :],
                        start=(dcol == 0), stop=(dcol == 15)),
                        reads=[f"r1b{dcol % 2}", "ones_b"], writes=["sumps"])
                    fw.op("tensor", lambda dcol=dcol: nc.tensor.matmul(
                        sqps[0:1, :], ones_b[:, :], sqb[dcol % 2][:, :],
                        start=(dcol == 0), stop=(dcol == 15)),
                        reads=[f"sqb{dcol % 2}", "ones_b"], writes=["sqps"])
                fw.op("scalar", lambda: nc.scalar.activation(mrow[:, :], sumps[0:1, :], AF.Identity, scale=1.0 / D),
                      reads=["sumps"], writes=["mrow"])
                fw.op("scalar", lambda: nc.scalar.activation(vrow[:, :], sqps[0:1, :], AF.Identity, scale=1.0 / D),
                      reads=["sqps"], writes=["vrow"])
                fw.op("vector", lambda: nc.vector.tensor_mul(trow[:, :], mrow[:, :], mrow[:, :]),
                      reads=["mrow"], writes=["trow"])
                fw.op("vector", lambda: nc.vector.tensor_sub(vrow[:, :], vrow[:, :], trow[:, :]),
                      reads=["vrow", "trow"], writes=["vrow"])
                fw.op("vector", lambda: nc.vector.tensor_scalar(vrow[:, :], vrow[:, :], EPS, None, ALU.add),
                      reads=["vrow"], writes=["vrow"])
                fw.op("scalar", lambda: nc.scalar.activation(trow[:, :], vrow[:, :], AF.Ln),
                      reads=["vrow"], writes=["trow"])
                fw.op("scalar", lambda: nc.scalar.activation(irow[:, :], trow[:, :], AF.Exp, scale=-0.5),
                      reads=["trow"], writes=["irow"])
                fw.op("vector", lambda: nc.vector.tensor_mul(brow[:, :], mrow[:, :], irow[:, :]),
                      reads=["mrow", "irow"], writes=["brow"])
                fw.op("vector", lambda: nc.vector.tensor_scalar_mul(brow[:, :], brow[:, :], -1.0),
                      reads=["brow"], writes=["brow"])
                fw.op("tensor", lambda: nc.tensor.matmul(
                    abc[:, :], onesr_f[:, :], irow[:, :], start=True, stop=True),
                    reads=["irow", "onesr_f"], writes=["abc"])
                fw.op("tensor", lambda: nc.tensor.matmul(
                    bbc[:, :], onesr_f[:, :], brow[:, :], start=True, stop=True),
                    reads=["brow", "onesr_f"], writes=["bbc"])
                fw.op("scalar", lambda: nc.scalar.copy(abc_sb[:, :], abc[:, :]),
                      reads=["abc", "abc_sb"], writes=["abc_sb"])
                fw.op("scalar", lambda: nc.scalar.copy(bbc_sb[:, :], bbc[:, :]),
                      reads=["bbc", "bbc_sb"], writes=["bbc_sb"])
                for dcol in range(16):
                    yf = f"y1fs{dcol % 2}"
                    yb = f"y1bs{dcol % 2}"
                    fw.op("vector", lambda dcol=dcol: nc.vector.tensor_mul(
                        y1fs[dcol % 2][:, :], r1[:, 512 * dcol:512 * (dcol + 1)], abc_sb[:, :]),
                        reads=["r1", "abc_sb"], writes=[yf])
                    fw.op("vector", lambda dcol=dcol: nc.vector.tensor_add(
                        y1fs[dcol % 2][:, :], y1fs[dcol % 2][:, :], bbc_sb[:, :]),
                        reads=[yf, "bbc_sb"], writes=[yf])
                    fw.op("scalar", lambda dcol=dcol: nc.scalar.copy(
                        y1bs[dcol % 2][:, :], y1fs[dcol % 2][:, :]),
                        reads=[yf], writes=[yb])
                    fw.op("gpsimd", lambda tb=tb, dcol=dcol: nc.gpsimd.dma_start(
                        y1f_scr[dcol, :, 512 * tb:512 * (tb + 1)], y1fs[dcol % 2][:, :]),
                        reads=[yf], writes=[f"y1f_scr{dcol}"], dma=True)
                    fw.op("gpsimd", lambda tb=tb, dcol=dcol: nc.gpsimd.dma_start(
                        y1b_scr[dcol, :, 512 * tb:512 * (tb + 1)], y1bs[dcol % 2][:, :]),
                        reads=[yb], writes=["y1b_scr"], dma=True)
        fw.barrier()
        if stop_after == "O":
            return nc

        with ExitStack() as st:
            y1 = st.enter_context(nc.sbuf_tensor("y1", [128, NDK * NT], BF16))
            wgu = [st.enter_context(nc.sbuf_tensor(f"wgu{i}", [128, 2 * NDK * 128], BF16))
                   for i in range(2)]
            sg = [st.enter_context(nc.sbuf_tensor(f"sg{i}", [128, 1024], F32))
                  for i in range(2)]
            hst = [st.enter_context(nc.sbuf_tensor(f"hst{i}", [128, 1024], BF16))
                   for i in range(2)]
            gps = [st.enter_context(nc.psum_tensor(f"gps{i}", [128, 1024], F32))
                   for i in range(2)]
            ups = [st.enter_context(nc.psum_tensor(f"ups{i}", [128, 1024], F32))
                   for i in range(2)]

            fw.op("scalar", lambda: load_act_set(nc, SET_SILU), inc=False)
            fw.op("sync", lambda: nc.sync.dma_start(
                y1[:, :].rearrange("p (a t) -> p a t", a=NDK),
                y1b_scr.rearrange("a p t -> p a t")),
                reads=["y1b_scr"], writes=["y1"], dma=True)

            for ht in range(HID // 128):
                wb = f"wgu{ht % 2}"
                fw.op("gpsimd", lambda ht=ht: nc.gpsimd.dma_start(
                    wgu[ht % 2][:, 0:NDK * 128].rearrange("p (a n) -> p a n", a=NDK),
                    wgt.rearrange("(a p) n -> p a n", p=128)[:, :, 128 * ht:128 * (ht + 1)]),
                    writes=[wb], dma=True)
                fw.op("gpsimd", lambda ht=ht: nc.gpsimd.dma_start(
                    wgu[ht % 2][:, NDK * 128:2 * NDK * 128].rearrange("p (a n) -> p a n", a=NDK),
                    wut.rearrange("(a p) n -> p a n", p=128)[:, :, 128 * ht:128 * (ht + 1)]),
                    writes=[wb], dma=True)
                for half in range(2):
                    gb, ub = f"gps{half}", f"ups{half}"
                    for dk in range(NDK):
                        for q in range(2):
                            fw.op("tensor", lambda ht=ht, half=half, dk=dk, q=q: nc.tensor.matmul(
                                gps[half][:, 512 * q:512 * (q + 1)],
                                wgu[ht % 2][:, 128 * dk:128 * (dk + 1)],
                                y1[:, NT * dk + 1024 * half + 512 * q: NT * dk + 1024 * half + 512 * (q + 1)],
                                start=(dk == 0), stop=(dk == NDK - 1)),
                                reads=[wb, "y1"], writes=[gb])
                        for q in range(2):
                            fw.op("tensor", lambda ht=ht, half=half, dk=dk, q=q: nc.tensor.matmul(
                                ups[half][:, 512 * q:512 * (q + 1)],
                                wgu[ht % 2][:, NDK * 128 + 128 * dk:NDK * 128 + 128 * (dk + 1)],
                                y1[:, NT * dk + 1024 * half + 512 * q: NT * dk + 1024 * half + 512 * (q + 1)],
                                start=(dk == 0), stop=(dk == NDK - 1)),
                                reads=[wb, "y1"], writes=[ub])
                    sb_ = f"sg{half}"
                    hb = f"hst{half}"
                    fw.op("scalar", lambda half=half: nc.scalar.activation(
                        sg[half][:, :], gps[half][:, :], AF.Silu),
                        reads=[gb], writes=[sb_])
                    fw.op("vector", lambda half=half: nc.vector.tensor_mul(
                        hst[half][:, :], sg[half][:, :], ups[half][:, :]),
                        reads=[sb_, ub], writes=[hb])
                    fw.op("sync", lambda ht=ht, half=half: nc.sync.dma_start(
                        hid_scr[ht, :, 1024 * half:1024 * (half + 1)], hst[half][:, :]),
                        reads=[hb], writes=["hid_scr"], dma=True)
        fw.barrier()
        if stop_after == "F1":
            return nc

        with ExitStack() as st:
            hidh = st.enter_context(nc.sbuf_tensor("hidh", [128, 32 * 1024], BF16))
            wd = [st.enter_context(nc.sbuf_tensor(f"wd{i}", [128, 32 * 128], BF16))
                  for i in range(2)]
            acc = st.enter_context(nc.sbuf_tensor("acc", [128, NDK * 1024], F32))
            y1r = [st.enter_context(nc.sbuf_tensor(f"y1r{i}", [128, 1024], F32))
                   for i in range(2)]
            r2b = [st.enter_context(nc.sbuf_tensor(f"r2b{i}", [128, 1024], BF16))
                   for i in range(2)]
            sq2 = [st.enter_context(nc.sbuf_tensor(f"sq2{i}", [128, 1024], BF16))
                   for i in range(2)]
            trow = st.enter_context(nc.sbuf_tensor("trow2", [1, 1024], F32))
            mrow = st.enter_context(nc.sbuf_tensor("mrow2", [1, 1024], F32))
            vrow = st.enter_context(nc.sbuf_tensor("vrow2", [1, 1024], F32))
            irow = st.enter_context(nc.sbuf_tensor("irow2", [1, 1024], F32))
            brow = st.enter_context(nc.sbuf_tensor("brow2", [1, 1024], F32))
            dps = [st.enter_context(nc.psum_tensor(f"dps{i}", [128, 1024], F32))
                   for i in range(2)]
            sum2 = st.enter_context(nc.psum_tensor("sum2", [128, 1024], F32))
            sq2p = st.enter_context(nc.psum_tensor("sq2p", [128, 1024], F32))

            fw.op("scalar", lambda: load_act_set(nc, SET_LN_EXP), inc=False)

            for tp in range(2):
                for half in range(2):
                    fw.op("sync", lambda tp=tp, half=half: nc.sync.dma_start(
                        hidh[:, :].rearrange("p (a t) -> p a t", a=32),
                        hid_scr.rearrange("a p t -> p a t")[:, 32 * half:32 * (half + 1),
                                                           1024 * tp:1024 * (tp + 1)]),
                        reads=["hid_scr"], writes=["hidh"], dma=True)
                    for dcol in range(16):
                        wb = f"wd{dcol % 2}"
                        fw.op("gpsimd", lambda tp=tp, half=half, dcol=dcol: nc.gpsimd.dma_start(
                            wd[dcol % 2][:, :].rearrange("p (a n) -> p a n", a=32),
                            wdt[4096 * half:4096 * (half + 1),
                                128 * dcol:128 * (dcol + 1)].rearrange("(a p) n -> p a n", p=128)),
                            writes=[wb], dma=True)
                        pb = f"dps{dcol % 2}"
                        for hb in range(32):
                            for q in range(2):
                                fw.op("tensor", lambda half=half, dcol=dcol, hb=hb, q=q:
                                      nc.tensor.matmul(
                                          dps[dcol % 2][:, 512 * q:512 * (q + 1)],
                                          wd[dcol % 2][:, 128 * hb:128 * (hb + 1)],
                                          hidh[:, 1024 * hb + 512 * q:1024 * hb + 512 * (q + 1)],
                                          start=(hb == 0), stop=(hb == 31)),
                                      reads=[wb, "hidh"], writes=[pb])
                        if half == 0:
                            fw.op("vector", lambda dcol=dcol: nc.vector.tensor_copy(
                                acc[:, 1024 * dcol:1024 * (dcol + 1)], dps[dcol % 2][:, :]),
                                reads=[pb], writes=["acc"])
                        else:
                            fw.op("vector", lambda dcol=dcol: nc.vector.tensor_add(
                                acc[:, 1024 * dcol:1024 * (dcol + 1)],
                                acc[:, 1024 * dcol:1024 * (dcol + 1)], dps[dcol % 2][:, :]),
                                reads=[pb, "acc"], writes=["acc"])
                            if "dn_scr" in dbg:
                                fw.op("sync", lambda tp=tp, dcol=dcol: nc.sync.dma_start(
                                    dn_scr[dcol, :, 1024 * tp:1024 * (tp + 1)],
                                    acc[:, 1024 * dcol:1024 * (dcol + 1)]),
                                    reads=["acc"], writes=["dn_scr"], dma=True)
                for dcol in range(16):
                    yb = f"y1r{dcol % 2}"
                    fw.op("sync", lambda tp=tp, dcol=dcol: nc.sync.dma_start(
                        y1r[dcol % 2][:, :],
                        y1f_scr[dcol, :, 1024 * tp:1024 * (tp + 1)]),
                        reads=[f"y1f_scr{dcol}"], writes=[yb], dma=True)
                    fw.op("vector", lambda dcol=dcol: nc.vector.tensor_add(
                        acc[:, 1024 * dcol:1024 * (dcol + 1)],
                        acc[:, 1024 * dcol:1024 * (dcol + 1)], y1r[dcol % 2][:, :]),
                        reads=["acc", yb], writes=["acc"])
                    if "r2_scr" in dbg:
                        fw.op("sync", lambda tp=tp, dcol=dcol: nc.sync.dma_start(
                            r2_scr[dcol, :, 1024 * tp:1024 * (tp + 1)],
                            acc[:, 1024 * dcol:1024 * (dcol + 1)]),
                            reads=["acc"], writes=["r2_scr"], dma=True)
                    fw.op("scalar", lambda dcol=dcol: nc.scalar.copy(
                        r2b[dcol % 2][:, :], acc[:, 1024 * dcol:1024 * (dcol + 1)]),
                        reads=["acc"], writes=[f"r2b{dcol % 2}"])
                    fw.op("scalar", lambda dcol=dcol: nc.scalar.activation(
                        sq2[dcol % 2][:, :], acc[:, 1024 * dcol:1024 * (dcol + 1)], AF.Square),
                        reads=["acc"], writes=[f"sq2{dcol % 2}"])
                    for q in range(2):
                        fw.op("tensor", lambda dcol=dcol, q=q: nc.tensor.matmul(
                            sum2[0:1, 512 * q:512 * (q + 1)], ones_b[:, :],
                            r2b[dcol % 2][:, 512 * q:512 * (q + 1)],
                            start=(dcol == 0), stop=(dcol == 15)),
                            reads=[f"r2b{dcol % 2}", "ones_b"], writes=["sum2"])
                        fw.op("tensor", lambda dcol=dcol, q=q: nc.tensor.matmul(
                            sq2p[0:1, 512 * q:512 * (q + 1)], ones_b[:, :],
                            sq2[dcol % 2][:, 512 * q:512 * (q + 1)],
                            start=(dcol == 0), stop=(dcol == 15)),
                            reads=[f"sq2{dcol % 2}", "ones_b"], writes=["sq2p"])
                fw.op("scalar", lambda: nc.scalar.activation(mrow[:, :], sum2[0:1, :], AF.Identity, scale=1.0 / D),
                      reads=["sum2"], writes=["mrow2"])
                fw.op("scalar", lambda: nc.scalar.activation(vrow[:, :], sq2p[0:1, :], AF.Identity, scale=1.0 / D),
                      reads=["sq2p"], writes=["vrow2"])
                fw.op("vector", lambda: nc.vector.tensor_mul(trow[:, :], mrow[:, :], mrow[:, :]),
                      reads=["mrow2"], writes=["trow2"])
                fw.op("vector", lambda: nc.vector.tensor_sub(vrow[:, :], vrow[:, :], trow[:, :]),
                      reads=["vrow2", "trow2"], writes=["vrow2"])
                fw.op("vector", lambda: nc.vector.tensor_scalar(vrow[:, :], vrow[:, :], EPS, None, ALU.add),
                      reads=["vrow2"], writes=["vrow2"])
                fw.op("scalar", lambda: nc.scalar.activation(trow[:, :], vrow[:, :], AF.Ln),
                      reads=["vrow2"], writes=["trow2"])
                fw.op("scalar", lambda: nc.scalar.activation(irow[:, :], trow[:, :], AF.Exp, scale=-0.5),
                      reads=["trow2"], writes=["irow2"])
                fw.op("vector", lambda: nc.vector.tensor_mul(brow[:, :], mrow[:, :], irow[:, :]),
                      reads=["mrow2", "irow2"], writes=["brow2"])
                fw.op("vector", lambda: nc.vector.tensor_scalar_mul(brow[:, :], brow[:, :], -1.0),
                      reads=["brow2"], writes=["brow2"])
                for q in range(2):
                    fw.op("tensor", lambda q=q: nc.tensor.matmul(
                        dps[0][:, 512 * q:512 * (q + 1)], onesr_f[:, :],
                        irow[0:1, 512 * q:512 * (q + 1)], start=True, stop=True),
                        reads=["irow2"], writes=["dps0"])
                    fw.op("tensor", lambda q=q: nc.tensor.matmul(
                        dps[1][:, 512 * q:512 * (q + 1)], onesr_f[:, :],
                        brow[0:1, 512 * q:512 * (q + 1)], start=True, stop=True),
                        reads=["brow2"], writes=["dps1"])
                fw.op("scalar", lambda: nc.scalar.copy(y1r[0][:, :], dps[0][:, :]),
                      reads=["dps0", "y1r0"], writes=["y1r0"])
                fw.op("scalar", lambda: nc.scalar.copy(y1r[1][:, :], dps[1][:, :]),
                      reads=["dps1", "y1r1"], writes=["y1r1"])
                for dcol in range(16):
                    fw.op("vector", lambda dcol=dcol: nc.vector.tensor_mul(
                        acc[:, 1024 * dcol:1024 * (dcol + 1)],
                        acc[:, 1024 * dcol:1024 * (dcol + 1)], y1r[0][:, :]),
                        reads=["acc", "y1r0"], writes=["acc"])
                    fw.op("vector", lambda dcol=dcol: nc.vector.tensor_add(
                        acc[:, 1024 * dcol:1024 * (dcol + 1)],
                        acc[:, 1024 * dcol:1024 * (dcol + 1)], y1r[1][:, :]),
                        reads=["acc", "y1r1"], writes=["acc"])
                    fw.op("sync", lambda tp=tp, dcol=dcol: nc.sync.dma_start(
                        out_fm[128 * dcol:128 * (dcol + 1), 1024 * tp:1024 * (tp + 1)],
                        acc[:, 1024 * dcol:1024 * (dcol + 1)]),
                        reads=["acc"], writes=["out_fm"], dma=True)
        fw.barrier()
    return nc


def emit_attention(nc, fw, c, qh, kh, vh, pp, ptmp, pacc_v, zc_sb, zi_sb,
                   zb_sb, ast, nps, stp, zps, zbp, masks, sufV, ones_f, onesr_f,
                   q_scr, k_scr, v_scr, a_scr):
    ns = [8 * c + 4, 8 * c + 8, 60 - 8 * c, 64 - 8 * c]
    counts = [float(T - 128 * n) for n in ns]
    span_sb = ns[3]
    vchunk = span_sb // 8

    def load_head(h):
        par = h % 2
        qb, kb, vb = f"qh{par}", f"kh{par}", f"vh{par}"
        fw.op("sync", lambda h=h: nc.sync.dma_start(
            qh[h % 2][:, :], q_scr[h, :, :]), writes=[qb], dma=True)
        fw.op("sync", lambda h=h: nc.sync.dma_start(
            kh[h % 2][:, 0:128 * span_sb], k_scr[h, :, 0:128 * span_sb]),
            writes=[kb], dma=True)
        for i in range(8):
            fw.op("gpsimd", lambda h=h, i=i: nc.gpsimd.dma_start(
                vh[h % 2][:, 128 * vchunk * i:128 * vchunk * (i + 1)]
                  .rearrange("p (a d) -> p a d", a=vchunk),
                v_scr[vchunk * i:vchunk * (i + 1), :, 128 * h:128 * (h + 1)]
                  .rearrange("a p d -> p a d")),
                writes=[vb], dma=True)

    def finalize(h, tb, ti):
        prow = 0 if ti == 0 else 32
        fw.op("tensor", lambda ti=ti, prow=prow: nc.tensor.matmul(
            zps[prow:prow + 1, :], ones_f[:, :],
            pacc_v[:, 512 * ti:512 * (ti + 1)], start=True, stop=True),
            reads=[f"paccv{ti}", "ones_f"], writes=["zps"])
        fw.op("vector", lambda tb=tb, prow=prow: nc.vector.tensor_scalar(
            zc_sb[:, :], zps[prow:prow + 1, :], counts[tb], None, ALU.add),
            reads=["zps"], writes=["zc_sb"])
        fw.op("vector", lambda: nc.vector.reciprocal(zi_sb[:, :], zc_sb[:, :]),
              reads=["zc_sb"], writes=["zi_sb"])
        fw.op("tensor", lambda: nc.tensor.matmul(
            zbp[:, :], onesr_f[:, :], zi_sb[:, :], start=True, stop=True),
            reads=["zi_sb", "onesr_f"], writes=["zbp"])
        fw.op("scalar", lambda: nc.scalar.copy(zb_sb[:, :], zbp[:, :]),
              reads=["zbp", "zb_sb"], writes=["zb_sb"])
        apar = tb % 2
        fw.op("vector", lambda h=h, tb=tb, ti=ti, apar=apar: nc.vector.scalar_tensor_tensor(
            ast[apar][:, :], nps[:, 512 * ti:512 * (ti + 1)],
            sufV[:, 65 * h + ns[tb]:65 * h + ns[tb] + 1],
            zb_sb[:, :], ALU.add, ALU.mult),
            reads=[f"nps{ti}", "sufV", "zb_sb"], writes=[f"ast{apar}"])
        fw.op("gpsimd", lambda h=h, tb=tb, apar=apar: nc.gpsimd.dma_start(
            a_scr[h, :, 512 * tb:512 * (tb + 1)], ast[apar][:, :]),
            reads=[f"ast{apar}"], writes=[f"a_scr{h}"], dma=True)

    load_head(0)
    for h in range(H):
        par = h % 2
        qb, kb, vb = f"qh{par}", f"kh{par}", f"vh{par}"
        if h + 1 < H:
            load_head(h + 1)
        for p in range(2):
            lo, hi = 2 * p, 2 * p + 1
            n_lo, n_hi = ns[lo], ns[hi]
            for sb in range(n_hi):
                both = sb < n_lo
                ppar = sb % 4
                spar = sb % 2
                ppb = f"pp{ppar}"
                stL, stH = f"st{spar}L", f"st{spar}H"
                lo0 = 0 if both else 512
                if both:
                    fw.op("tensor", lambda sb=sb, lo=lo, par=par, spar=spar: nc.tensor.matmul(
                        stp[spar][:, 0:512],
                        kh[par][:, 128 * sb:128 * (sb + 1)],
                        qh[par][:, 512 * lo:512 * (lo + 1)],
                        start=True, stop=True),
                        reads=[kb, qb], writes=[stL])
                fw.op("tensor", lambda sb=sb, hi=hi, par=par, spar=spar: nc.tensor.matmul(
                    stp[spar][:, 512:1024],
                    kh[par][:, 128 * sb:128 * (sb + 1)],
                    qh[par][:, 512 * hi:512 * (hi + 1)],
                    start=True, stop=True),
                    reads=[kb, qb], writes=[stH])
                if both and n_lo - 4 <= sb < n_lo:
                    k = sb - (n_lo - 4)
                    fw.op("vector", lambda k=k, spar=spar: nc.vector.tensor_mul(
                        stp[spar][:, 0:512], stp[spar][:, 0:512],
                        masks[:, 512 * k:512 * (k + 1)]),
                        reads=[stL, "masks"], writes=[stL])
                if n_hi - 4 <= sb < n_hi:
                    k = sb - (n_hi - 4)
                    fw.op("vector", lambda k=k, spar=spar: nc.vector.tensor_mul(
                        stp[spar][:, 512:1024], stp[spar][:, 512:1024],
                        masks[:, 512 * k:512 * (k + 1)]),
                        reads=[stH, "masks"], writes=[stH])
                st_reads = ([stL, stH] if both else [stH])
                fw.op("scalar", lambda ppar=ppar, lo0=lo0, spar=spar: nc.scalar.activation(
                    pp[ppar][:, lo0:1024], stp[spar][:, lo0:1024], AF.Exp, scale=SCALE),
                    reads=st_reads, writes=[ppb])
                if sb % 2 == 1:
                    tpar = (sb // 2) % 2
                    tpb = f"ptmp{tpar}"
                    ppb_prev = f"pp{(sb - 1) % 4}"
                    plo0 = 0 if (sb - 1) < n_lo else 512
                    pkeys = ["paccv0", "paccv1"] if plo0 == 0 else ["paccv1"]
                    fw.op("vector", lambda ppar=ppar, sb=sb, plo0=plo0, tpar=tpar:
                          nc.vector.tensor_add(
                              ptmp[tpar][:, plo0:1024],
                              pp[(sb - 1) % 4][:, plo0:1024], pp[ppar][:, plo0:1024]),
                          reads=[ppb, ppb_prev, tpb], writes=[tpb])
                    if sb == 1:
                        fw.op("vector", lambda tpar=tpar: nc.vector.tensor_copy(
                            pacc_v[:, :], ptmp[tpar][:, :]),
                            reads=[tpb], writes=["paccv0", "paccv1"])
                    else:
                        fw.op("vector", lambda tpar=tpar, plo0=plo0: nc.vector.tensor_add(
                            pacc_v[:, plo0:1024], pacc_v[:, plo0:1024], ptmp[tpar][:, plo0:1024]),
                            reads=[tpb] + pkeys, writes=pkeys)
                if both:
                    fw.op("tensor", lambda sb=sb, par=par, ppar=ppar, n_lo=n_lo:
                          nc.tensor.matmul(
                              nps[:, 0:512],
                              vh[par][:, 128 * sb:128 * (sb + 1)],
                              pp[ppar][:, 0:512],
                              start=(sb == 0), stop=(sb == n_lo - 1)),
                          reads=[vb, ppb], writes=["nps0"])
                fw.op("tensor", lambda sb=sb, par=par, ppar=ppar, n_hi=n_hi:
                      nc.tensor.matmul(
                          nps[:, 512:1024],
                          vh[par][:, 128 * sb:128 * (sb + 1)],
                          pp[ppar][:, 512:1024],
                          start=(sb == 0), stop=(sb == n_hi - 1)),
                      reads=[vb, ppb], writes=["nps1"])
                if sb == n_lo + 1:
                    finalize(h, lo, 0)
            finalize(h, hi, 1)


def kernel(L0_attn_x, L0_attn_Wq, L0_attn_Wk, L0_attn_Wv, L0_attn_Wo,
           L0_ffn_w_gate, L0_ffn_w_up, L0_ffn_w_down):
    x = np.asarray(L0_attn_x)
    wq = np.ascontiguousarray(np.asarray(L0_attn_Wq), dtype=np.float32)
    wk = np.ascontiguousarray(np.asarray(L0_attn_Wk), dtype=np.float32)
    wv = np.ascontiguousarray(np.asarray(L0_attn_Wv), dtype=np.float32)
    wo = np.ascontiguousarray(np.asarray(L0_attn_Wo), dtype=np.float32)
    wgt = np.ascontiguousarray(np.asarray(L0_ffn_w_gate).T, dtype=np.float32)
    wut = np.ascontiguousarray(np.asarray(L0_ffn_w_up).T, dtype=np.float32)
    wdt = np.ascontiguousarray(np.asarray(L0_ffn_w_down).T, dtype=np.float32)

    xT = [np.ascontiguousarray(x[b].T, dtype=np.float32) for b in range(B)]

    nc = build_program()
    in_maps = []
    for core in range(8):
        b, c = core // 4, core % 4
        xo = np.concatenate([xT[b][:, 1024 * c:1024 * (c + 1)],
                             xT[b][:, 1024 * (7 - c):1024 * (8 - c)]], axis=1)
        in_maps.append({
            "xT": xT[b],
            "x_ownT": np.ascontiguousarray(xo),
            "wq": wq, "wk": wk, "wv": wv, "wo": wo,
            "wgt": wgt, "wut": wut, "wdt": wdt,
            "cidx": np.array([[c]], dtype=np.int32),
        })
    res = run_bass_kernel_spmd(nc, in_maps, core_ids=list(range(8)))
    out = np.empty((B, T, D), dtype=np.float32)
    for core in range(8):
        b, c = core // 4, core % 4
        o = np.asarray(res.results[core]["out_fm"])
        out[b, 1024 * c:1024 * (c + 1), :] = o[:, 0:1024].T
        out[b, 1024 * (7 - c):1024 * (8 - c), :] = o[:, 1024:2048].T
    return out


# version 05
# speedup vs baseline: 1.1444x; 1.0859x over previous
import math
import os as _os
import numpy as np
import ml_dtypes
import concourse.bass as bass
import concourse.mybir as mybir
from concourse.expressions import RuntimeValue
from concourse.bass_utils import run_bass_kernel_spmd

F32 = mybir.dt.float32
BF16 = mybir.dt.bfloat16
I32 = mybir.dt.int32
AF = mybir.ActivationFunctionType
ALU = mybir.AluOpType
AX = mybir.AxisListType

D = 2048
T = 8192
B = 2
H = 16
DH = 128
HID = 8192
NT = 2048
SCALE = 1.0 / math.sqrt(T)
EPS = 1e-5
NDK = D // 128
NSB = T // 128

SET_LN_EXP = 6
SET_SILU = 18


def load_act_set(nc, set_id):
    return nc.scalar.add_instruction(mybir.InstLoadActFuncSet(
        name=nc.get_next_instruction_name(),
        act_func_set_id=set_id, ins=[], outs=[]))


class FW:
    ENG_SEM = {
        "tensor": ("T", None),
        "scalar": ("S", None),
        "vector": ("V", None),
        "gpsimd": ("PG", "DG"),
        "sync": ("SY", "DQ"),
    }
    ALL_SEMS = ["T", "S", "V", "PG", "SY", "DG", "DQ"]

    def __init__(self, nc):
        self.nc = nc
        self.sem = {}
        self.scope = ""
        self.cnt = {}
        self.known = {}
        self.bufs = {}
        for s in self.ALL_SEMS:
            for suf in ("", "b"):
                k = s + suf
                self.sem[k] = nc.alloc_semaphore(name="sem_" + k)
                self.cnt[k] = 0

    def semkey(self, engine, dma):
        c, d = self.ENG_SEM[engine]
        base = d if dma else c
        if self.scope and base in ("T", "S", "V", "PG", "SY", "DG", "DQ"):
            return base + self.scope
        return base

    def enter_body(self):
        self.scope = "b"
        for s in self.ALL_SEMS:
            self.cnt[s + "b"] = 0
        self.known = {k: v for k, v in self.known.items() if not k[1].endswith("b")}
        self._saved_bufs = {k: {"w": dict(v["w"]), "r": dict(v["r"])}
                            for k, v in self.bufs.items()}
        self._saved_known = dict(self.known)
        self._saved_cnt = dict(self.cnt)

    def exit_body(self):
        nc = self.nc
        finals = {s + "b": self.cnt[s + "b"] for s in self.ALL_SEMS}
        for ename in self.ENG_SEM:
            eng = getattr(nc, ename)
            for sk, v in finals.items():
                if v > 0:
                    eng.wait_ge(self.sem[sk], v)
            gk = self.ENG_SEM[ename][0]
            eng.drain().then_inc(self.sem[gk], 1)
        self.scope = ""

    def body_reset(self):
        self.scope = "b"
        self.bufs = {k: {"w": dict(v["w"]), "r": dict(v["r"])}
                     for k, v in self._saved_bufs.items()}
        self.known = dict(self._saved_known)
        self.cnt = dict(self._saved_cnt)
        for s in self.ALL_SEMS:
            self.cnt[s + "b"] = 0

    def finish_switch(self):
        self.scope = ""
        for ename in self.ENG_SEM:
            gk = self.ENG_SEM[ename][0]
            self.cnt[gk] += 1

    def wrote(self, buf, engine):
        gk = self.ENG_SEM[engine][0]
        self.bufs[buf] = {"w": {gk: self.cnt[gk]}, "r": {}}

    def _waits(self, ename, deps):
        eng = getattr(self.nc, ename)
        for sk, v in sorted(deps.items()):
            if v <= 0:
                continue
            kk = (ename, sk)
            if self.known.get(kk, 0) >= v:
                continue
            eng.wait_ge(self.sem[sk], v)
            self.known[kk] = v

    def op(self, ename, fn, reads=(), writes=(), dma=False, inc=True):
        deps = {}
        my_sk = self.semkey(ename, dma)
        my_csk = self.semkey(ename, False)

        def merge(d):
            for sk, v in d.items():
                if sk == my_csk and not dma and ename == "tensor":
                    continue
                if deps.get(sk, 0) < v:
                    deps[sk] = v

        for b in reads:
            st = self.bufs.get(b)
            if st:
                merge(st["w"])
        for b in writes:
            st = self.bufs.get(b)
            if st:
                merge(st["w"])
                merge(st["r"])
        self._waits(ename, deps)
        inst = fn()
        if inc:
            amt = 16 if dma else 1
            inst.then_inc(self.sem[my_sk], amt)
            self.cnt[my_sk] += amt
        tok = (my_sk, self.cnt[my_sk])
        for b in writes:
            self.bufs[b] = {"w": {tok[0]: tok[1]}, "r": {}}
        for b in reads:
            st = self.bufs.setdefault(b, {"w": {}, "r": {}})
            if st["r"].get(tok[0], 0) < tok[1]:
                st["r"][tok[0]] = tok[1]
        return tok

    def barrier(self):
        assert self.scope == ""
        nc = self.nc
        for ename in self.ENG_SEM:
            gk = self.ENG_SEM[ename][0]
            getattr(nc, ename).drain().then_inc(self.sem[gk], 1)
            self.cnt[gk] += 1
        targets = {s: self.cnt[s] for s in self.ALL_SEMS}
        for ename in self.ENG_SEM:
            eng = getattr(nc, ename)
            for s, v in targets.items():
                if v > 0:
                    eng.wait_ge(self.sem[s], v)
            for s, v in targets.items():
                self.known[(ename, s)] = v
        for b in self.bufs.values():
            pass


def build_program(stop_after=None, dbg=()):
    KSC = int(_os.environ.get("KDBG_SC", "16"))
    KSKIP = set(_os.environ.get("KDBG_SKIP", "").split(","))
    """stop_after in {None,'P1k','P1v','P2','A','O','F1'}; dbg: iterable of
    scratch names to expose as ExternalOutput."""
    nc = bass.Bass("TRN2", target_bir_lowering=False)

    def kind_of(name):
        return "ExternalOutput" if name in dbg else "Internal"

    xT = nc.dram_tensor("xT", [D, T], F32, kind="ExternalInput").ap()
    x_ownT = nc.dram_tensor("x_ownT", [D, NT], F32, kind="ExternalInput").ap()
    wq = nc.dram_tensor("wq", [D, D], F32, kind="ExternalInput").ap()
    wk = nc.dram_tensor("wk", [D, D], F32, kind="ExternalInput").ap()
    wv = nc.dram_tensor("wv", [D, D], F32, kind="ExternalInput").ap()
    wo = nc.dram_tensor("wo", [D, D], F32, kind="ExternalInput").ap()
    wgt = nc.dram_tensor("wgt", [D, HID], F32, kind="ExternalInput").ap()
    wut = nc.dram_tensor("wut", [D, HID], F32, kind="ExternalInput").ap()
    wdt = nc.dram_tensor("wdt", [HID, D], F32, kind="ExternalInput").ap()
    cidx = nc.dram_tensor("cidx", [1, 1], I32, kind="ExternalInput").ap()
    out_fm = nc.dram_tensor("out_fm", [D, NT], F32, kind="ExternalOutput").ap()
    k_scr = nc.dram_tensor("k_scr", [H, 128, T], BF16, kind=kind_of("k_scr")).ap()
    v_scr = nc.dram_tensor("v_scr", [NSB, 128, D], BF16, kind=kind_of("v_scr")).ap()
    q_scr = nc.dram_tensor("q_scr", [H, 128, NT], BF16, kind=kind_of("q_scr")).ap()
    a_scr = nc.dram_tensor("a_scr", [H, 128, NT], BF16, kind=kind_of("a_scr")).ap()
    y1f_scr = nc.dram_tensor("y1f_scr", [NDK, 128, NT], F32, kind=kind_of("y1f_scr")).ap()
    y1b_scr = nc.dram_tensor("y1b_scr", [NDK, 128, NT], BF16, kind=kind_of("y1b_scr")).ap()
    hid_scr = nc.dram_tensor("hid_scr", [HID // 128, 128, NT], BF16, kind=kind_of("hid_scr")).ap()
    dn_scr = nc.dram_tensor("dn_scr", [NDK, 128, NT], F32, kind=kind_of("dn_scr")).ap()
    r2_scr = nc.dram_tensor("r2_scr", [NDK, 128, NT], F32, kind=kind_of("r2_scr")).ap()

    fw = FW(nc)
    from contextlib import ExitStack

    with ExitStack() as outer:
        c_sb = outer.enter_context(nc.sbuf_tensor("c_sb", [1, 1], I32))
        ones_f = outer.enter_context(nc.sbuf_tensor("ones_f", [128, 1], F32))
        ones_b = outer.enter_context(nc.sbuf_tensor("ones_b", [128, 1], BF16))
        onesr_f = outer.enter_context(nc.sbuf_tensor("onesr_f", [1, 128], F32))
        masks = outer.enter_context(nc.sbuf_tensor("masks", [128, 4 * 512], F32))
        iotat = outer.enter_context(nc.sbuf_tensor("iotat", [128, 512], F32))
        sufA = outer.enter_context(nc.sbuf_tensor("sufA", [128, 65 * NDK], F32))
        sufB = outer.enter_context(nc.sbuf_tensor("sufB", [128, 65 * NDK], F32))
        sufXb = outer.enter_context(nc.sbuf_tensor("sufXb", [128, NDK * 65], BF16))
        sufV = outer.enter_context(nc.sbuf_tensor("sufV", [128, H * 65], F32))

        fw.op("sync", lambda: nc.sync.dma_start(c_sb[:, :], cidx),
              writes=["c_sb"], dma=True)
        if "init" not in KSKIP:
            fw.op("vector", lambda: nc.vector.memset(ones_f[:, :], 1.0), writes=["ones_f"])
            fw.op("vector", lambda: nc.vector.memset(ones_b[:, :], 1.0), writes=["ones_b"])
            fw.op("vector", lambda: nc.vector.memset(onesr_f[:, :], 1.0), writes=["onesr_f"])
            fw.op("scalar", lambda: load_act_set(nc, SET_LN_EXP), inc=False)
        for k in range(4 if "init" not in KSKIP else 0):
            fw.op("gpsimd",
                  lambda k=k: nc.gpsimd.iota(
                      iotat[:, :], [[1, 512]], base=-(128 * k),
                      channel_multiplier=-1,
                      allow_small_or_imprecise_dtypes=True),
                  writes=["iotat"])
            fw.op("vector",
                  lambda k=k: nc.vector.tensor_single_scalar(
                      masks[:, 512 * k:512 * (k + 1)], iotat[:, :], 0.0, ALU.is_ge),
                  reads=["iotat"], writes=["masks"])

        with ExitStack() as st:
            wk_sb = st.enter_context(nc.sbuf_tensor("wk_sb", [128, NDK * D], BF16))
            xc = [st.enter_context(nc.sbuf_tensor(f"xc{i}", [128, NDK * 512], BF16))
                  for i in range(2)]
            kst = [st.enter_context(nc.sbuf_tensor(f"kst{i}", [128, 512], BF16))
                   for i in range(2)]
            kps = st.enter_context(nc.psum_tensor("kps", [128, 4 * 512], F32))

            fw.op("gpsimd", lambda: nc.gpsimd.dma_start(
                wk_sb[:, :].rearrange("p (a n) -> p a n", a=NDK),
                wk.rearrange("(a p) n -> p a n", p=128)),
                writes=["wk_sb"], dma=True)

            for sc in range(KSC):
                xb = f"xc{sc % 2}"
                fw.op("gpsimd", lambda sc=sc: nc.gpsimd.dma_start(
                    xc[sc % 2][:, :].rearrange("p (a s) -> p a s", a=NDK),
                    xT.rearrange("(a p) s -> p a s", p=128)[:, :, 512 * sc:512 * (sc + 1)]),
                    writes=[xb], dma=True)
                for sb in range(4 if "bsum" not in KSKIP else 0):
                    j = sc * 4 + sb
                    fw.op("vector", lambda sc=sc, sb=sb, j=j: nc.vector.reduce_sum(
                        sufA[:, 16 * j:16 * (j + 1)].rearrange("p (a o) -> p a o", o=1),
                        xc[sc % 2][:, :].rearrange("p (a s) -> p a s", a=NDK)[:, :, 128 * sb:128 * (sb + 1)],
                        axis=AX.X),
                        reads=[xb], writes=["sufA"])
                for dcol in range(int(_os.environ.get("KDBG_DCOL", "16")) if "kproj" not in KSKIP else 0):
                    pb = f"kps{dcol % 4}"
                    for dk in range(NDK):
                        fw.op("tensor", lambda sc=sc, dcol=dcol, dk=dk: nc.tensor.matmul(
                            kps[:, 512 * (dcol % 4):512 * (dcol % 4 + 1)],
                            wk_sb[:, D * dk + 128 * dcol: D * dk + 128 * (dcol + 1)],
                            xc[sc % 2][:, 512 * dk:512 * (dk + 1)],
                            start=(dk == 0), stop=(dk == NDK - 1)),
                            reads=["wk_sb", xb], writes=[pb])
                    ks = f"kst{dcol % 2}"
                    if "evac" not in KSKIP:
                        fw.op("scalar", lambda dcol=dcol: nc.scalar.copy(
                            kst[dcol % 2][:, :], kps[:, 512 * (dcol % 4):512 * (dcol % 4 + 1)]),
                            reads=[pb], writes=[ks])
                    if "kwr" not in KSKIP:
                        fw.op("sync", lambda sc=sc, dcol=dcol: nc.sync.dma_start(
                            k_scr[dcol, :, 512 * sc:512 * (sc + 1)], kst[dcol % 2][:, :]),
                            reads=[ks], writes=[f"k_scr{dcol}"], dma=True)

            if "suf" in KSKIP:
                fw.barrier()
                if stop_after == "P1k":
                    return nc
            fw.op("vector", lambda: nc.vector.memset(sufA[:, 64 * 16:65 * 16], 0.0),
                  writes=["sufA"])
            cur, oth = sufA, sufB
            curb, othb = "sufA", "sufB"
            for lvl in range(6):
                sh = 1 << lvl
                w = (65 - sh) * 16
                fw.op("vector", lambda cur=cur, oth=oth, sh=sh, w=w: nc.vector.tensor_add(
                    oth[:, 0:w], cur[:, 0:w], cur[:, 16 * sh:16 * sh + w]),
                    reads=[curb], writes=[othb])
                fw.op("vector", lambda cur=cur, oth=oth, w=w: nc.vector.tensor_copy(
                    oth[:, w:65 * 16], cur[:, w:65 * 16]),
                    reads=[curb], writes=[othb])
                cur, oth = oth, cur
                curb, othb = othb, curb
            fw.op("scalar", lambda cur=cur: nc.scalar.copy(
                sufXb[:, :].rearrange("p (a j) -> p a j", a=NDK),
                cur[:, :].rearrange("p (j a) -> p a j", a=NDK)),
                reads=[curb], writes=["sufXb"])
        fw.barrier()
        if stop_after == "P1k":
            return nc

        with ExitStack() as st:
            wv_sb = st.enter_context(nc.sbuf_tensor("wv_sb", [128, NDK * D], BF16))
            xc = [st.enter_context(nc.sbuf_tensor(f"xv{i}", [128, NDK * 512], BF16))
                  for i in range(2)]
            vst = [st.enter_context(nc.sbuf_tensor(f"vst{i}", [128, D], BF16))
                   for i in range(2)]
            vps = [st.enter_context(nc.psum_tensor(f"vps{i}", [128, D], F32))
                   for i in range(2)]
            svps = vps[0]

            fw.op("gpsimd", lambda: nc.gpsimd.dma_start(
                wv_sb[:, :].rearrange("p (a n) -> p a n", a=NDK),
                wv.rearrange("(a p) n -> p a n", p=128)),
                writes=["wv_sb"], dma=True)

            for sc in range(16):
                xb = f"xv{sc % 2}"
                fw.op("gpsimd", lambda sc=sc: nc.gpsimd.dma_start(
                    xc[sc % 2][:, :].rearrange("p (a s) -> p a s", a=NDK),
                    xT.rearrange("(a p) s -> p a s", p=128)[:, :, 512 * sc:512 * (sc + 1)]),
                    writes=[xb], dma=True)
                for sblk in range(4):
                    sb = sc * 4 + sblk
                    pb = f"vps{sb % 2}"
                    for dcb in range(4):
                        for dk in range(NDK):
                            fw.op("tensor", lambda sc=sc, sblk=sblk, sb=sb, dcb=dcb, dk=dk:
                                  nc.tensor.matmul(
                                      vps[sb % 2][:, 512 * dcb:512 * (dcb + 1)],
                                      xc[sc % 2][:, 512 * dk + 128 * sblk:512 * dk + 128 * (sblk + 1)],
                                      wv_sb[:, D * dk + 512 * dcb: D * dk + 512 * (dcb + 1)],
                                      start=(dk == 0), stop=(dk == NDK - 1)),
                                  reads=["wv_sb", xb], writes=[pb])
                    vs = f"vst{sb % 2}"
                    fw.op("vector", lambda sb=sb: nc.vector.tensor_copy(
                        vst[sb % 2][:, :], vps[sb % 2][:, :]),
                        reads=[pb], writes=[vs])
                    fw.op("sync", lambda sb=sb: nc.sync.dma_start(
                        v_scr[sb, :, :], vst[sb % 2][:, :]),
                        reads=[vs], writes=[f"v_scr{sb}"], dma=True)

            for h in range(H):
                for dk in range(NDK):
                    fw.op("tensor", lambda h=h, dk=dk: nc.tensor.matmul(
                        svps[:, 0:65],
                        wv_sb[:, D * dk + 128 * h: D * dk + 128 * (h + 1)],
                        sufXb[:, 65 * dk:65 * (dk + 1)],
                        start=(dk == 0), stop=(dk == NDK - 1)),
                        reads=["wv_sb", "sufXb"], writes=["vps0"])
                fw.op("scalar", lambda h=h: nc.scalar.copy(
                    sufV[:, 65 * h:65 * (h + 1)], svps[:, 0:65]),
                    reads=["vps0"], writes=["sufV"])
        fw.barrier()
        if stop_after == "P1v":
            return nc

        with ExitStack() as st:
            wq_sb = st.enter_context(nc.sbuf_tensor("wq_sb", [128, NDK * D], BF16))
            xc = [st.enter_context(nc.sbuf_tensor(f"xq{i}", [128, NDK * 512], BF16))
                  for i in range(2)]
            qst = [st.enter_context(nc.sbuf_tensor(f"qst{i}", [128, 512], BF16))
                   for i in range(2)]
            qps = st.enter_context(nc.psum_tensor("qps", [128, 4 * 512], F32))

            fw.op("gpsimd", lambda: nc.gpsimd.dma_start(
                wq_sb[:, :].rearrange("p (a n) -> p a n", a=NDK),
                wq.rearrange("(a p) n -> p a n", p=128)),
                writes=["wq_sb"], dma=True)
            for sc in range(4):
                xb = f"xq{sc % 2}"
                fw.op("gpsimd", lambda sc=sc: nc.gpsimd.dma_start(
                    xc[sc % 2][:, :].rearrange("p (a s) -> p a s", a=NDK),
                    x_ownT.rearrange("(a p) s -> p a s", p=128)[:, :, 512 * sc:512 * (sc + 1)]),
                    writes=[xb], dma=True)
                for dcol in range(16):
                    pb = f"qps{dcol % 4}"
                    for dk in range(NDK):
                        fw.op("tensor", lambda sc=sc, dcol=dcol, dk=dk: nc.tensor.matmul(
                            qps[:, 512 * (dcol % 4):512 * (dcol % 4 + 1)],
                            wq_sb[:, D * dk + 128 * dcol: D * dk + 128 * (dcol + 1)],
                            xc[sc % 2][:, 512 * dk:512 * (dk + 1)],
                            start=(dk == 0), stop=(dk == NDK - 1)),
                            reads=["wq_sb", xb], writes=[pb])
                    qs = f"qst{dcol % 2}"
                    fw.op("scalar", lambda dcol=dcol: nc.scalar.copy(
                        qst[dcol % 2][:, :], qps[:, 512 * (dcol % 4):512 * (dcol % 4 + 1)]),
                        reads=[pb], writes=[qs])
                    fw.op("sync", lambda sc=sc, dcol=dcol: nc.sync.dma_start(
                        q_scr[dcol, :, 512 * sc:512 * (sc + 1)], qst[dcol % 2][:, :]),
                        reads=[qs], writes=[f"q_scr{dcol}"], dma=True)
        fw.barrier()
        if stop_after == "P2":
            return nc

        with ExitStack() as st:
            qh = [st.enter_context(nc.sbuf_tensor(f"qh{i}", [128, NT], BF16))
                  for i in range(2)]
            kh = [st.enter_context(nc.sbuf_tensor(f"kh{i}", [128, T], BF16))
                  for i in range(2)]
            vh = [st.enter_context(nc.sbuf_tensor(f"vh{i}", [128, T], BF16))
                  for i in range(2)]
            pp = [st.enter_context(nc.sbuf_tensor(f"pp{i}", [128, 1024], BF16))
                  for i in range(4)]
            ptmp = [st.enter_context(nc.sbuf_tensor(f"ptmp{i}", [128, 1024], BF16))
                    for i in range(2)]
            pacc_v = st.enter_context(nc.sbuf_tensor("pacc_v", [128, 1024], F32))
            zc_sb = st.enter_context(nc.sbuf_tensor("zc_sb", [1, 512], F32))
            zi_sb = st.enter_context(nc.sbuf_tensor("zi_sb", [1, 512], F32))
            zb_sb = st.enter_context(nc.sbuf_tensor("zb_sb", [128, 512], F32))
            ast = [st.enter_context(nc.sbuf_tensor(f"ast{i}", [128, 512], BF16))
                   for i in range(2)]
            nps = st.enter_context(nc.psum_tensor("nps", [128, 2 * 512], F32))
            stp = [st.enter_context(nc.psum_tensor(f"stp{i}", [128, 1024], F32))
                   for i in range(2)]
            zps = st.enter_context(nc.psum_tensor("zps", [128, 512], F32))
            zbp = st.enter_context(nc.psum_tensor("zbp", [128, 512], F32))

            engines = [nc.tensor, nc.scalar, nc.vector, nc.gpsimd, nc.sync]
            idxs = []
            for eng, ename in zip(engines, ["tensor", "scalar", "vector", "gpsimd", "sync"]):
                fw._waits(ename, {"DQ": 16})
                r = eng.alloc_register()
                eng.reg_load(r, c_sb[0:1, 0:1])
                idxs.append(RuntimeValue(r, min_val=0, max_val=3))

            first_body = True
            for c in nc.Switch(engines=engines, index=idxs, n=4):
                if first_body:
                    fw.enter_body()
                    first_body = False
                else:
                    fw.body_reset()
                emit_attention(nc, fw, c, qh, kh, vh, pp, ptmp, pacc_v, zc_sb, zi_sb,
                               zb_sb, ast, nps, stp, zps, zbp,
                               masks, sufV, ones_f, onesr_f,
                               q_scr, k_scr, v_scr, a_scr)
                fw.exit_body()
            fw.finish_switch()
            for h in range(H):
                fw.wrote(f"a_scr{h}", "gpsimd")
        fw.barrier()
        if stop_after == "A":
            return nc

        with ExitStack() as st:
            wo_sb = st.enter_context(nc.sbuf_tensor("wo_sb", [128, NDK * D], BF16))
            atb = [st.enter_context(nc.sbuf_tensor(f"atb{i}", [128, NDK * 512], BF16))
                   for i in range(2)]
            xres = [st.enter_context(nc.sbuf_tensor(f"xres{i}", [128, 512], F32))
                    for i in range(4)]
            r1 = st.enter_context(nc.sbuf_tensor("r1", [128, NDK * 512], F32))
            r1b = [st.enter_context(nc.sbuf_tensor(f"r1b{i}", [128, 512], BF16))
                   for i in range(2)]
            sqb = [st.enter_context(nc.sbuf_tensor(f"sqb{i}", [128, 512], BF16))
                   for i in range(2)]
            trow = st.enter_context(nc.sbuf_tensor("trow", [1, 512], F32))
            mrow = st.enter_context(nc.sbuf_tensor("mrow", [1, 512], F32))
            vrow = st.enter_context(nc.sbuf_tensor("vrow", [1, 512], F32))
            irow = st.enter_context(nc.sbuf_tensor("irow", [1, 512], F32))
            brow = st.enter_context(nc.sbuf_tensor("brow", [1, 512], F32))
            y1fs = [st.enter_context(nc.sbuf_tensor(f"y1fs{i}", [128, 512], F32))
                    for i in range(2)]
            abc_sb = st.enter_context(nc.sbuf_tensor("abc_sb", [128, 512], F32))
            bbc_sb = st.enter_context(nc.sbuf_tensor("bbc_sb", [128, 512], F32))
            y1bs = [st.enter_context(nc.sbuf_tensor(f"y1bs{i}", [128, 512], BF16))
                    for i in range(2)]
            ops_ = st.enter_context(nc.psum_tensor("ops", [128, 4 * 512], F32))
            sumps = st.enter_context(nc.psum_tensor("sumps", [128, 512], F32))
            sqps = st.enter_context(nc.psum_tensor("sqps", [128, 512], F32))
            abc = st.enter_context(nc.psum_tensor("abc", [128, 512], F32))
            bbc = st.enter_context(nc.psum_tensor("bbc", [128, 512], F32))

            fw.op("gpsimd", lambda: nc.gpsimd.dma_start(
                wo_sb[:, :].rearrange("p (a n) -> p a n", a=NDK),
                wo.rearrange("(a p) n -> p a n", p=128)),
                writes=["wo_sb"], dma=True)

            for tb in range(4):
                ab = f"atb{tb % 2}"
                fw.op("sync", lambda tb=tb: nc.sync.dma_start(
                    atb[tb % 2][:, :].rearrange("p (a s) -> p a s", a=NDK),
                    a_scr.rearrange("h p t -> p h t")[:, :, 512 * tb:512 * (tb + 1)]),
                    reads=[f"a_scr{h}" for h in range(H)], writes=[ab], dma=True)
                for dcol in range(16):
                    pb = f"ops{dcol % 4}"
                    xr = f"xres{dcol % 4}"
                    fw.op("sync", lambda tb=tb, dcol=dcol: nc.sync.dma_start(
                        xres[dcol % 4][:, :],
                        x_ownT[128 * dcol:128 * (dcol + 1), 512 * tb:512 * (tb + 1)]),
                        writes=[xr], dma=True)
                    for dk in range(NDK):
                        fw.op("tensor", lambda tb=tb, dcol=dcol, dk=dk: nc.tensor.matmul(
                            ops_[:, 512 * (dcol % 4):512 * (dcol % 4 + 1)],
                            wo_sb[:, D * dk + 128 * dcol: D * dk + 128 * (dcol + 1)],
                            atb[tb % 2][:, 512 * dk:512 * (dk + 1)],
                            start=(dk == 0), stop=(dk == NDK - 1)),
                            reads=["wo_sb", ab], writes=[pb])
                    fw.op("vector", lambda dcol=dcol: nc.vector.tensor_add(
                        r1[:, 512 * dcol:512 * (dcol + 1)],
                        xres[dcol % 4][:, :],
                        ops_[:, 512 * (dcol % 4):512 * (dcol % 4 + 1)]),
                        reads=[xr, pb], writes=["r1"])
                    fw.op("scalar", lambda dcol=dcol: nc.scalar.copy(
                        r1b[dcol % 2][:, :], r1[:, 512 * dcol:512 * (dcol + 1)]),
                        reads=["r1"], writes=[f"r1b{dcol % 2}"])
                    fw.op("scalar", lambda dcol=dcol: nc.scalar.activation(
                        sqb[dcol % 2][:, :], r1[:, 512 * dcol:512 * (dcol + 1)], AF.Square),
                        reads=["r1"], writes=[f"sqb{dcol % 2}"])
                    fw.op("tensor", lambda dcol=dcol: nc.tensor.matmul(
                        sumps[0:1, :], ones_b[:, :], r1b[dcol % 2][:, :],
                        start=(dcol == 0), stop=(dcol == 15)),
                        reads=[f"r1b{dcol % 2}", "ones_b"], writes=["sumps"])
                    fw.op("tensor", lambda dcol=dcol: nc.tensor.matmul(
                        sqps[0:1, :], ones_b[:, :], sqb[dcol % 2][:, :],
                        start=(dcol == 0), stop=(dcol == 15)),
                        reads=[f"sqb{dcol % 2}", "ones_b"], writes=["sqps"])
                fw.op("scalar", lambda: nc.scalar.activation(mrow[:, :], sumps[0:1, :], AF.Identity, scale=1.0 / D),
                      reads=["sumps"], writes=["mrow"])
                fw.op("scalar", lambda: nc.scalar.activation(vrow[:, :], sqps[0:1, :], AF.Identity, scale=1.0 / D),
                      reads=["sqps"], writes=["vrow"])
                fw.op("vector", lambda: nc.vector.tensor_mul(trow[:, :], mrow[:, :], mrow[:, :]),
                      reads=["mrow"], writes=["trow"])
                fw.op("vector", lambda: nc.vector.tensor_sub(vrow[:, :], vrow[:, :], trow[:, :]),
                      reads=["vrow", "trow"], writes=["vrow"])
                fw.op("vector", lambda: nc.vector.tensor_scalar(vrow[:, :], vrow[:, :], EPS, None, ALU.add),
                      reads=["vrow"], writes=["vrow"])
                fw.op("scalar", lambda: nc.scalar.activation(trow[:, :], vrow[:, :], AF.Ln),
                      reads=["vrow"], writes=["trow"])
                fw.op("scalar", lambda: nc.scalar.activation(irow[:, :], trow[:, :], AF.Exp, scale=-0.5),
                      reads=["trow"], writes=["irow"])
                fw.op("vector", lambda: nc.vector.tensor_mul(brow[:, :], mrow[:, :], irow[:, :]),
                      reads=["mrow", "irow"], writes=["brow"])
                fw.op("vector", lambda: nc.vector.tensor_scalar_mul(brow[:, :], brow[:, :], -1.0),
                      reads=["brow"], writes=["brow"])
                fw.op("tensor", lambda: nc.tensor.matmul(
                    abc[:, :], onesr_f[:, :], irow[:, :], start=True, stop=True),
                    reads=["irow", "onesr_f"], writes=["abc"])
                fw.op("tensor", lambda: nc.tensor.matmul(
                    bbc[:, :], onesr_f[:, :], brow[:, :], start=True, stop=True),
                    reads=["brow", "onesr_f"], writes=["bbc"])
                fw.op("scalar", lambda: nc.scalar.copy(abc_sb[:, :], abc[:, :]),
                      reads=["abc", "abc_sb"], writes=["abc_sb"])
                fw.op("scalar", lambda: nc.scalar.copy(bbc_sb[:, :], bbc[:, :]),
                      reads=["bbc", "bbc_sb"], writes=["bbc_sb"])
                for dcol in range(16):
                    yf = f"y1fs{dcol % 2}"
                    yb = f"y1bs{dcol % 2}"
                    fw.op("vector", lambda dcol=dcol: nc.vector.tensor_mul(
                        y1fs[dcol % 2][:, :], r1[:, 512 * dcol:512 * (dcol + 1)], abc_sb[:, :]),
                        reads=["r1", "abc_sb"], writes=[yf])
                    fw.op("vector", lambda dcol=dcol: nc.vector.tensor_add(
                        y1fs[dcol % 2][:, :], y1fs[dcol % 2][:, :], bbc_sb[:, :]),
                        reads=[yf, "bbc_sb"], writes=[yf])
                    fw.op("scalar", lambda dcol=dcol: nc.scalar.copy(
                        y1bs[dcol % 2][:, :], y1fs[dcol % 2][:, :]),
                        reads=[yf], writes=[yb])
                    fw.op("gpsimd", lambda tb=tb, dcol=dcol: nc.gpsimd.dma_start(
                        y1f_scr[dcol, :, 512 * tb:512 * (tb + 1)], y1fs[dcol % 2][:, :]),
                        reads=[yf], writes=[f"y1f_scr{dcol}"], dma=True)
                    fw.op("gpsimd", lambda tb=tb, dcol=dcol: nc.gpsimd.dma_start(
                        y1b_scr[dcol, :, 512 * tb:512 * (tb + 1)], y1bs[dcol % 2][:, :]),
                        reads=[yb], writes=["y1b_scr"], dma=True)
        fw.barrier()
        if stop_after == "O":
            return nc

        with ExitStack() as st:
            y1 = st.enter_context(nc.sbuf_tensor("y1", [128, NDK * NT], BF16))
            wgu = [st.enter_context(nc.sbuf_tensor(f"wgu{i}", [128, 2 * NDK * 128], BF16))
                   for i in range(2)]
            sg = [st.enter_context(nc.sbuf_tensor(f"sg{i}", [128, 1024], F32))
                  for i in range(2)]
            hst = [st.enter_context(nc.sbuf_tensor(f"hst{i}", [128, 1024], BF16))
                   for i in range(2)]
            gps = [st.enter_context(nc.psum_tensor(f"gps{i}", [128, 1024], F32))
                   for i in range(2)]
            ups = [st.enter_context(nc.psum_tensor(f"ups{i}", [128, 1024], F32))
                   for i in range(2)]

            fw.op("scalar", lambda: load_act_set(nc, SET_SILU), inc=False)
            fw.op("sync", lambda: nc.sync.dma_start(
                y1[:, :].rearrange("p (a t) -> p a t", a=NDK),
                y1b_scr.rearrange("a p t -> p a t")),
                reads=["y1b_scr"], writes=["y1"], dma=True)

            for ht in range(HID // 128):
                wb = f"wgu{ht % 2}"
                fw.op("gpsimd", lambda ht=ht: nc.gpsimd.dma_start(
                    wgu[ht % 2][:, 0:NDK * 128].rearrange("p (a n) -> p a n", a=NDK),
                    wgt.rearrange("(a p) n -> p a n", p=128)[:, :, 128 * ht:128 * (ht + 1)]),
                    writes=[wb], dma=True)
                fw.op("gpsimd", lambda ht=ht: nc.gpsimd.dma_start(
                    wgu[ht % 2][:, NDK * 128:2 * NDK * 128].rearrange("p (a n) -> p a n", a=NDK),
                    wut.rearrange("(a p) n -> p a n", p=128)[:, :, 128 * ht:128 * (ht + 1)]),
                    writes=[wb], dma=True)
                for half in range(2):
                    gb, ub = f"gps{half}", f"ups{half}"
                    for dk in range(NDK):
                        for q in range(2):
                            fw.op("tensor", lambda ht=ht, half=half, dk=dk, q=q: nc.tensor.matmul(
                                gps[half][:, 512 * q:512 * (q + 1)],
                                wgu[ht % 2][:, 128 * dk:128 * (dk + 1)],
                                y1[:, NT * dk + 1024 * half + 512 * q: NT * dk + 1024 * half + 512 * (q + 1)],
                                start=(dk == 0), stop=(dk == NDK - 1)),
                                reads=[wb, "y1"], writes=[gb])
                        for q in range(2):
                            fw.op("tensor", lambda ht=ht, half=half, dk=dk, q=q: nc.tensor.matmul(
                                ups[half][:, 512 * q:512 * (q + 1)],
                                wgu[ht % 2][:, NDK * 128 + 128 * dk:NDK * 128 + 128 * (dk + 1)],
                                y1[:, NT * dk + 1024 * half + 512 * q: NT * dk + 1024 * half + 512 * (q + 1)],
                                start=(dk == 0), stop=(dk == NDK - 1)),
                                reads=[wb, "y1"], writes=[ub])
                    sb_ = f"sg{half}"
                    hb = f"hst{half}"
                    fw.op("scalar", lambda half=half: nc.scalar.activation(
                        sg[half][:, :], gps[half][:, :], AF.Silu),
                        reads=[gb], writes=[sb_])
                    fw.op("vector", lambda half=half: nc.vector.tensor_mul(
                        hst[half][:, :], sg[half][:, :], ups[half][:, :]),
                        reads=[sb_, ub], writes=[hb])
                    fw.op("sync", lambda ht=ht, half=half: nc.sync.dma_start(
                        hid_scr[ht, :, 1024 * half:1024 * (half + 1)], hst[half][:, :]),
                        reads=[hb], writes=["hid_scr"], dma=True)
        fw.barrier()
        if stop_after == "F1":
            return nc

        with ExitStack() as st:
            hidh = st.enter_context(nc.sbuf_tensor("hidh", [128, 32 * 1024], BF16))
            wd = [st.enter_context(nc.sbuf_tensor(f"wd{i}", [128, 32 * 128], BF16))
                  for i in range(2)]
            acc = st.enter_context(nc.sbuf_tensor("acc", [128, NDK * 1024], F32))
            y1r = [st.enter_context(nc.sbuf_tensor(f"y1r{i}", [128, 1024], F32))
                   for i in range(2)]
            r2b = [st.enter_context(nc.sbuf_tensor(f"r2b{i}", [128, 1024], BF16))
                   for i in range(2)]
            sq2 = [st.enter_context(nc.sbuf_tensor(f"sq2{i}", [128, 1024], BF16))
                   for i in range(2)]
            trow = st.enter_context(nc.sbuf_tensor("trow2", [1, 1024], F32))
            mrow = st.enter_context(nc.sbuf_tensor("mrow2", [1, 1024], F32))
            vrow = st.enter_context(nc.sbuf_tensor("vrow2", [1, 1024], F32))
            irow = st.enter_context(nc.sbuf_tensor("irow2", [1, 1024], F32))
            brow = st.enter_context(nc.sbuf_tensor("brow2", [1, 1024], F32))
            dps = [st.enter_context(nc.psum_tensor(f"dps{i}", [128, 1024], F32))
                   for i in range(2)]
            sum2 = st.enter_context(nc.psum_tensor("sum2", [128, 1024], F32))
            sq2p = st.enter_context(nc.psum_tensor("sq2p", [128, 1024], F32))

            fw.op("scalar", lambda: load_act_set(nc, SET_LN_EXP), inc=False)

            for tp in range(2):
                for half in range(2):
                    fw.op("sync", lambda tp=tp, half=half: nc.sync.dma_start(
                        hidh[:, :].rearrange("p (a t) -> p a t", a=32),
                        hid_scr.rearrange("a p t -> p a t")[:, 32 * half:32 * (half + 1),
                                                           1024 * tp:1024 * (tp + 1)]),
                        reads=["hid_scr"], writes=["hidh"], dma=True)
                    for dcol in range(16):
                        wb = f"wd{dcol % 2}"
                        fw.op("gpsimd", lambda tp=tp, half=half, dcol=dcol: nc.gpsimd.dma_start(
                            wd[dcol % 2][:, :].rearrange("p (a n) -> p a n", a=32),
                            wdt[4096 * half:4096 * (half + 1),
                                128 * dcol:128 * (dcol + 1)].rearrange("(a p) n -> p a n", p=128)),
                            writes=[wb], dma=True)
                        pb = f"dps{dcol % 2}"
                        for hb in range(32):
                            for q in range(2):
                                fw.op("tensor", lambda half=half, dcol=dcol, hb=hb, q=q:
                                      nc.tensor.matmul(
                                          dps[dcol % 2][:, 512 * q:512 * (q + 1)],
                                          wd[dcol % 2][:, 128 * hb:128 * (hb + 1)],
                                          hidh[:, 1024 * hb + 512 * q:1024 * hb + 512 * (q + 1)],
                                          start=(hb == 0), stop=(hb == 31)),
                                      reads=[wb, "hidh"], writes=[pb])
                        if half == 0:
                            fw.op("vector", lambda dcol=dcol: nc.vector.tensor_copy(
                                acc[:, 1024 * dcol:1024 * (dcol + 1)], dps[dcol % 2][:, :]),
                                reads=[pb], writes=["acc"])
                        else:
                            fw.op("vector", lambda dcol=dcol: nc.vector.tensor_add(
                                acc[:, 1024 * dcol:1024 * (dcol + 1)],
                                acc[:, 1024 * dcol:1024 * (dcol + 1)], dps[dcol % 2][:, :]),
                                reads=[pb, "acc"], writes=["acc"])
                            if "dn_scr" in dbg:
                                fw.op("sync", lambda tp=tp, dcol=dcol: nc.sync.dma_start(
                                    dn_scr[dcol, :, 1024 * tp:1024 * (tp + 1)],
                                    acc[:, 1024 * dcol:1024 * (dcol + 1)]),
                                    reads=["acc"], writes=["dn_scr"], dma=True)
                for dcol in range(16):
                    yb = f"y1r{dcol % 2}"
                    fw.op("sync", lambda tp=tp, dcol=dcol: nc.sync.dma_start(
                        y1r[dcol % 2][:, :],
                        y1f_scr[dcol, :, 1024 * tp:1024 * (tp + 1)]),
                        reads=[f"y1f_scr{dcol}"], writes=[yb], dma=True)
                    fw.op("vector", lambda dcol=dcol: nc.vector.tensor_add(
                        acc[:, 1024 * dcol:1024 * (dcol + 1)],
                        acc[:, 1024 * dcol:1024 * (dcol + 1)], y1r[dcol % 2][:, :]),
                        reads=["acc", yb], writes=["acc"])
                    if "r2_scr" in dbg:
                        fw.op("sync", lambda tp=tp, dcol=dcol: nc.sync.dma_start(
                            r2_scr[dcol, :, 1024 * tp:1024 * (tp + 1)],
                            acc[:, 1024 * dcol:1024 * (dcol + 1)]),
                            reads=["acc"], writes=["r2_scr"], dma=True)
                    fw.op("scalar", lambda dcol=dcol: nc.scalar.copy(
                        r2b[dcol % 2][:, :], acc[:, 1024 * dcol:1024 * (dcol + 1)]),
                        reads=["acc"], writes=[f"r2b{dcol % 2}"])
                    fw.op("scalar", lambda dcol=dcol: nc.scalar.activation(
                        sq2[dcol % 2][:, :], acc[:, 1024 * dcol:1024 * (dcol + 1)], AF.Square),
                        reads=["acc"], writes=[f"sq2{dcol % 2}"])
                    for q in range(2):
                        fw.op("tensor", lambda dcol=dcol, q=q: nc.tensor.matmul(
                            sum2[0:1, 512 * q:512 * (q + 1)], ones_b[:, :],
                            r2b[dcol % 2][:, 512 * q:512 * (q + 1)],
                            start=(dcol == 0), stop=(dcol == 15)),
                            reads=[f"r2b{dcol % 2}", "ones_b"], writes=["sum2"])
                        fw.op("tensor", lambda dcol=dcol, q=q: nc.tensor.matmul(
                            sq2p[0:1, 512 * q:512 * (q + 1)], ones_b[:, :],
                            sq2[dcol % 2][:, 512 * q:512 * (q + 1)],
                            start=(dcol == 0), stop=(dcol == 15)),
                            reads=[f"sq2{dcol % 2}", "ones_b"], writes=["sq2p"])
                fw.op("scalar", lambda: nc.scalar.activation(mrow[:, :], sum2[0:1, :], AF.Identity, scale=1.0 / D),
                      reads=["sum2"], writes=["mrow2"])
                fw.op("scalar", lambda: nc.scalar.activation(vrow[:, :], sq2p[0:1, :], AF.Identity, scale=1.0 / D),
                      reads=["sq2p"], writes=["vrow2"])
                fw.op("vector", lambda: nc.vector.tensor_mul(trow[:, :], mrow[:, :], mrow[:, :]),
                      reads=["mrow2"], writes=["trow2"])
                fw.op("vector", lambda: nc.vector.tensor_sub(vrow[:, :], vrow[:, :], trow[:, :]),
                      reads=["vrow2", "trow2"], writes=["vrow2"])
                fw.op("vector", lambda: nc.vector.tensor_scalar(vrow[:, :], vrow[:, :], EPS, None, ALU.add),
                      reads=["vrow2"], writes=["vrow2"])
                fw.op("scalar", lambda: nc.scalar.activation(trow[:, :], vrow[:, :], AF.Ln),
                      reads=["vrow2"], writes=["trow2"])
                fw.op("scalar", lambda: nc.scalar.activation(irow[:, :], trow[:, :], AF.Exp, scale=-0.5),
                      reads=["trow2"], writes=["irow2"])
                fw.op("vector", lambda: nc.vector.tensor_mul(brow[:, :], mrow[:, :], irow[:, :]),
                      reads=["mrow2", "irow2"], writes=["brow2"])
                fw.op("vector", lambda: nc.vector.tensor_scalar_mul(brow[:, :], brow[:, :], -1.0),
                      reads=["brow2"], writes=["brow2"])
                for q in range(2):
                    fw.op("tensor", lambda q=q: nc.tensor.matmul(
                        dps[0][:, 512 * q:512 * (q + 1)], onesr_f[:, :],
                        irow[0:1, 512 * q:512 * (q + 1)], start=True, stop=True),
                        reads=["irow2"], writes=["dps0"])
                    fw.op("tensor", lambda q=q: nc.tensor.matmul(
                        dps[1][:, 512 * q:512 * (q + 1)], onesr_f[:, :],
                        brow[0:1, 512 * q:512 * (q + 1)], start=True, stop=True),
                        reads=["brow2"], writes=["dps1"])
                fw.op("scalar", lambda: nc.scalar.copy(y1r[0][:, :], dps[0][:, :]),
                      reads=["dps0", "y1r0"], writes=["y1r0"])
                fw.op("scalar", lambda: nc.scalar.copy(y1r[1][:, :], dps[1][:, :]),
                      reads=["dps1", "y1r1"], writes=["y1r1"])
                for dcol in range(16):
                    fw.op("vector", lambda dcol=dcol: nc.vector.tensor_mul(
                        acc[:, 1024 * dcol:1024 * (dcol + 1)],
                        acc[:, 1024 * dcol:1024 * (dcol + 1)], y1r[0][:, :]),
                        reads=["acc", "y1r0"], writes=["acc"])
                    fw.op("vector", lambda dcol=dcol: nc.vector.tensor_add(
                        acc[:, 1024 * dcol:1024 * (dcol + 1)],
                        acc[:, 1024 * dcol:1024 * (dcol + 1)], y1r[1][:, :]),
                        reads=["acc", "y1r1"], writes=["acc"])
                    fw.op("sync", lambda tp=tp, dcol=dcol: nc.sync.dma_start(
                        out_fm[128 * dcol:128 * (dcol + 1), 1024 * tp:1024 * (tp + 1)],
                        acc[:, 1024 * dcol:1024 * (dcol + 1)]),
                        reads=["acc"], writes=["out_fm"], dma=True)
        fw.barrier()
    return nc


def emit_attention(nc, fw, c, qh, kh, vh, pp, ptmp, pacc_v, zc_sb, zi_sb,
                   zb_sb, ast, nps, stp, zps, zbp, masks, sufV, ones_f, onesr_f,
                   q_scr, k_scr, v_scr, a_scr):
    ns = [8 * c + 4, 8 * c + 8, 60 - 8 * c, 64 - 8 * c]
    counts = [float(T - 128 * n) for n in ns]
    span_sb = ns[3]
    vchunk = span_sb // 8

    def load_head(h):
        par = h % 2
        qb, kb, vb = f"qh{par}", f"kh{par}", f"vh{par}"
        fw.op("sync", lambda h=h: nc.sync.dma_start(
            qh[h % 2][:, :], q_scr[h, :, :]), writes=[qb], dma=True)
        fw.op("sync", lambda h=h: nc.sync.dma_start(
            kh[h % 2][:, 0:128 * span_sb], k_scr[h, :, 0:128 * span_sb]),
            writes=[kb], dma=True)
        for i in range(8):
            fw.op("gpsimd", lambda h=h, i=i: nc.gpsimd.dma_start(
                vh[h % 2][:, 128 * vchunk * i:128 * vchunk * (i + 1)]
                  .rearrange("p (a d) -> p a d", a=vchunk),
                v_scr[vchunk * i:vchunk * (i + 1), :, 128 * h:128 * (h + 1)]
                  .rearrange("a p d -> p a d")),
                writes=[vb], dma=True)

    def finalize(h, tb, ti):
        prow = 0 if ti == 0 else 32
        fw.op("tensor", lambda ti=ti, prow=prow: nc.tensor.matmul(
            zps[prow:prow + 1, :], ones_f[:, :],
            pacc_v[:, 512 * ti:512 * (ti + 1)], start=True, stop=True),
            reads=[f"paccv{ti}", "ones_f"], writes=["zps"])
        fw.op("vector", lambda tb=tb, prow=prow: nc.vector.tensor_scalar(
            zc_sb[:, :], zps[prow:prow + 1, :], counts[tb], None, ALU.add),
            reads=["zps"], writes=["zc_sb"])
        fw.op("vector", lambda: nc.vector.reciprocal(zi_sb[:, :], zc_sb[:, :]),
              reads=["zc_sb"], writes=["zi_sb"])
        fw.op("tensor", lambda: nc.tensor.matmul(
            zbp[:, :], onesr_f[:, :], zi_sb[:, :], start=True, stop=True),
            reads=["zi_sb", "onesr_f"], writes=["zbp"])
        fw.op("scalar", lambda: nc.scalar.copy(zb_sb[:, :], zbp[:, :]),
              reads=["zbp", "zb_sb"], writes=["zb_sb"])
        apar = tb % 2
        fw.op("vector", lambda h=h, tb=tb, ti=ti, apar=apar: nc.vector.scalar_tensor_tensor(
            ast[apar][:, :], nps[:, 512 * ti:512 * (ti + 1)],
            sufV[:, 65 * h + ns[tb]:65 * h + ns[tb] + 1],
            zb_sb[:, :], ALU.add, ALU.mult),
            reads=[f"nps{ti}", "sufV", "zb_sb"], writes=[f"ast{apar}"])
        fw.op("gpsimd", lambda h=h, tb=tb, apar=apar: nc.gpsimd.dma_start(
            a_scr[h, :, 512 * tb:512 * (tb + 1)], ast[apar][:, :]),
            reads=[f"ast{apar}"], writes=[f"a_scr{h}"], dma=True)

    load_head(0)
    for h in range(H):
        par = h % 2
        qb, kb, vb = f"qh{par}", f"kh{par}", f"vh{par}"
        if h + 1 < H:
            load_head(h + 1)
        for p in range(2):
            lo, hi = 2 * p, 2 * p + 1
            n_lo, n_hi = ns[lo], ns[hi]
            def av_emit(sb):
                both = sb < n_lo
                ppar = sb % 4
                ppb = f"pp{ppar}"
                if both:
                    fw.op("tensor", lambda sb=sb, par=par, ppar=ppar, n_lo=n_lo:
                          nc.tensor.matmul(
                              nps[:, 0:512],
                              vh[par][:, 128 * sb:128 * (sb + 1)],
                              pp[ppar][:, 0:512],
                              start=(sb == 0), stop=(sb == n_lo - 1)),
                          reads=[f"vh{par}", ppb], writes=["nps0"])
                fw.op("tensor", lambda sb=sb, par=par, ppar=ppar, n_hi=n_hi:
                      nc.tensor.matmul(
                          nps[:, 512:1024],
                          vh[par][:, 128 * sb:128 * (sb + 1)],
                          pp[ppar][:, 512:1024],
                          start=(sb == 0), stop=(sb == n_hi - 1)),
                      reads=[f"vh{par}", ppb], writes=["nps1"])

            LAG = 2
            for sb in range(n_hi):
                both = sb < n_lo
                ppar = sb % 4
                spar = sb % 2
                ppb = f"pp{ppar}"
                stL, stH = f"st{spar}L", f"st{spar}H"
                lo0 = 0 if both else 512
                if both:
                    fw.op("tensor", lambda sb=sb, lo=lo, par=par, spar=spar: nc.tensor.matmul(
                        stp[spar][:, 0:512],
                        kh[par][:, 128 * sb:128 * (sb + 1)],
                        qh[par][:, 512 * lo:512 * (lo + 1)],
                        start=True, stop=True),
                        reads=[kb, qb], writes=[stL])
                fw.op("tensor", lambda sb=sb, hi=hi, par=par, spar=spar: nc.tensor.matmul(
                    stp[spar][:, 512:1024],
                    kh[par][:, 128 * sb:128 * (sb + 1)],
                    qh[par][:, 512 * hi:512 * (hi + 1)],
                    start=True, stop=True),
                    reads=[kb, qb], writes=[stH])
                if both and n_lo - 4 <= sb < n_lo:
                    k = sb - (n_lo - 4)
                    fw.op("vector", lambda k=k, spar=spar: nc.vector.tensor_mul(
                        stp[spar][:, 0:512], stp[spar][:, 0:512],
                        masks[:, 512 * k:512 * (k + 1)]),
                        reads=[stL, "masks"], writes=[stL])
                if n_hi - 4 <= sb < n_hi:
                    k = sb - (n_hi - 4)
                    fw.op("vector", lambda k=k, spar=spar: nc.vector.tensor_mul(
                        stp[spar][:, 512:1024], stp[spar][:, 512:1024],
                        masks[:, 512 * k:512 * (k + 1)]),
                        reads=[stH, "masks"], writes=[stH])
                st_reads = ([stL, stH] if both else [stH])
                fw.op("scalar", lambda ppar=ppar, lo0=lo0, spar=spar: nc.scalar.activation(
                    pp[ppar][:, lo0:1024], stp[spar][:, lo0:1024], AF.Exp, scale=SCALE),
                    reads=st_reads, writes=[ppb])
                if sb % 2 == 1:
                    tpar = (sb // 2) % 2
                    tpb = f"ptmp{tpar}"
                    ppb_prev = f"pp{(sb - 1) % 4}"
                    plo0 = 0 if (sb - 1) < n_lo else 512
                    pkeys = ["paccv0", "paccv1"] if plo0 == 0 else ["paccv1"]
                    fw.op("vector", lambda ppar=ppar, sb=sb, plo0=plo0, tpar=tpar:
                          nc.vector.tensor_add(
                              ptmp[tpar][:, plo0:1024],
                              pp[(sb - 1) % 4][:, plo0:1024], pp[ppar][:, plo0:1024]),
                          reads=[ppb, ppb_prev, tpb], writes=[tpb])
                    if sb == 1:
                        fw.op("vector", lambda tpar=tpar: nc.vector.tensor_copy(
                            pacc_v[:, :], ptmp[tpar][:, :]),
                            reads=[tpb], writes=["paccv0", "paccv1"])
                    else:
                        fw.op("vector", lambda tpar=tpar, plo0=plo0: nc.vector.tensor_add(
                            pacc_v[:, plo0:1024], pacc_v[:, plo0:1024], ptmp[tpar][:, plo0:1024]),
                            reads=[tpb] + pkeys, writes=pkeys)
                if sb >= LAG:
                    av_emit(sb - LAG)
                if sb - LAG == n_lo + 1:
                    finalize(h, lo, 0)
            for sb in range(max(n_hi - LAG, 0), n_hi):
                av_emit(sb)
                if sb == n_lo + 1:
                    finalize(h, lo, 0)
            finalize(h, hi, 1)


def kernel(L0_attn_x, L0_attn_Wq, L0_attn_Wk, L0_attn_Wv, L0_attn_Wo,
           L0_ffn_w_gate, L0_ffn_w_up, L0_ffn_w_down):
    x = np.asarray(L0_attn_x)
    wq = np.ascontiguousarray(np.asarray(L0_attn_Wq), dtype=np.float32)
    wk = np.ascontiguousarray(np.asarray(L0_attn_Wk), dtype=np.float32)
    wv = np.ascontiguousarray(np.asarray(L0_attn_Wv), dtype=np.float32)
    wo = np.ascontiguousarray(np.asarray(L0_attn_Wo), dtype=np.float32)
    wgt = np.ascontiguousarray(np.asarray(L0_ffn_w_gate).T, dtype=np.float32)
    wut = np.ascontiguousarray(np.asarray(L0_ffn_w_up).T, dtype=np.float32)
    wdt = np.ascontiguousarray(np.asarray(L0_ffn_w_down).T, dtype=np.float32)

    xT = [np.ascontiguousarray(x[b].T, dtype=np.float32) for b in range(B)]

    nc = build_program()
    in_maps = []
    for core in range(8):
        b, c = core // 4, core % 4
        xo = np.concatenate([xT[b][:, 1024 * c:1024 * (c + 1)],
                             xT[b][:, 1024 * (7 - c):1024 * (8 - c)]], axis=1)
        in_maps.append({
            "xT": xT[b],
            "x_ownT": np.ascontiguousarray(xo),
            "wq": wq, "wk": wk, "wv": wv, "wo": wo,
            "wgt": wgt, "wut": wut, "wdt": wdt,
            "cidx": np.array([[c]], dtype=np.int32),
        })
    res = run_bass_kernel_spmd(nc, in_maps, core_ids=list(range(8)))
    out = np.empty((B, T, D), dtype=np.float32)
    for core in range(8):
        b, c = core // 4, core % 4
        o = np.asarray(res.results[core]["out_fm"])
        out[b, 1024 * c:1024 * (c + 1), :] = o[:, 0:1024].T
        out[b, 1024 * (7 - c):1024 * (8 - c), :] = o[:, 1024:2048].T
    return out


# version 06
# speedup vs baseline: 1.3230x; 1.1561x over previous
import math
import os as _os
import numpy as np
import ml_dtypes
import concourse.bass as bass
import concourse.mybir as mybir
from concourse.expressions import RuntimeValue
from concourse.bass_utils import run_bass_kernel_spmd

F32 = mybir.dt.float32
BF16 = mybir.dt.bfloat16
I32 = mybir.dt.int32
AF = mybir.ActivationFunctionType
ALU = mybir.AluOpType
AX = mybir.AxisListType

D = 2048
T = 8192
B = 2
H = 16
DH = 128
HID = 8192
NT = 2048
SCALE = 1.0 / math.sqrt(T)
EPS = 1e-5
NDK = D // 128
NSB = T // 128

SET_LN_EXP = 6
SET_SILU = 18


def load_act_set(nc, set_id):
    return nc.scalar.add_instruction(mybir.InstLoadActFuncSet(
        name=nc.get_next_instruction_name(),
        act_func_set_id=set_id, ins=[], outs=[]))


class FW:
    ENG_SEM = {
        "tensor": ("T", None),
        "scalar": ("S", None),
        "vector": ("V", None),
        "gpsimd": ("PG", "DG"),
        "sync": ("SY", "DQ"),
    }
    ALL_SEMS = ["T", "S", "V", "PG", "SY", "DG", "DQ"]

    def __init__(self, nc):
        self.nc = nc
        self.sem = {}
        self.scope = ""
        self.cnt = {}
        self.known = {}
        self.bufs = {}
        for s in self.ALL_SEMS:
            for suf in ("", "b"):
                k = s + suf
                self.sem[k] = nc.alloc_semaphore(name="sem_" + k)
                self.cnt[k] = 0

    def semkey(self, engine, dma):
        c, d = self.ENG_SEM[engine]
        base = d if dma else c
        if self.scope and base in ("T", "S", "V", "PG", "SY", "DG", "DQ"):
            return base + self.scope
        return base

    def enter_body(self):
        self.scope = "b"
        for s in self.ALL_SEMS:
            self.cnt[s + "b"] = 0
        self.known = {k: v for k, v in self.known.items() if not k[1].endswith("b")}
        self._saved_bufs = {k: {"w": dict(v["w"]), "r": dict(v["r"])}
                            for k, v in self.bufs.items()}
        self._saved_known = dict(self.known)
        self._saved_cnt = dict(self.cnt)

    def exit_body(self):
        nc = self.nc
        finals = {s + "b": self.cnt[s + "b"] for s in self.ALL_SEMS}
        for ename in self.ENG_SEM:
            eng = getattr(nc, ename)
            for sk, v in finals.items():
                if v > 0:
                    eng.wait_ge(self.sem[sk], v)
            gk = self.ENG_SEM[ename][0]
            eng.drain().then_inc(self.sem[gk], 1)
        self.scope = ""

    def body_reset(self):
        self.scope = "b"
        self.bufs = {k: {"w": dict(v["w"]), "r": dict(v["r"])}
                     for k, v in self._saved_bufs.items()}
        self.known = dict(self._saved_known)
        self.cnt = dict(self._saved_cnt)
        for s in self.ALL_SEMS:
            self.cnt[s + "b"] = 0

    def finish_switch(self):
        self.scope = ""
        for ename in self.ENG_SEM:
            gk = self.ENG_SEM[ename][0]
            self.cnt[gk] += 1

    def wrote(self, buf, engine):
        gk = self.ENG_SEM[engine][0]
        self.bufs[buf] = {"w": {gk: self.cnt[gk]}, "r": {}}

    def _waits(self, ename, deps):
        eng = getattr(self.nc, ename)
        for sk, v in sorted(deps.items()):
            if v <= 0:
                continue
            kk = (ename, sk)
            if self.known.get(kk, 0) >= v:
                continue
            eng.wait_ge(self.sem[sk], v)
            self.known[kk] = v

    def op(self, ename, fn, reads=(), writes=(), dma=False, inc=True):
        deps = {}
        my_sk = self.semkey(ename, dma)
        my_csk = self.semkey(ename, False)

        def merge(d):
            for sk, v in d.items():
                if sk == my_csk and not dma and ename == "tensor":
                    continue
                if deps.get(sk, 0) < v:
                    deps[sk] = v

        for b in reads:
            st = self.bufs.get(b)
            if st:
                merge(st["w"])
        for b in writes:
            st = self.bufs.get(b)
            if st:
                merge(st["w"])
                merge(st["r"])
        self._waits(ename, deps)
        inst = fn()
        if inc:
            amt = 16 if dma else 1
            inst.then_inc(self.sem[my_sk], amt)
            self.cnt[my_sk] += amt
        tok = (my_sk, self.cnt[my_sk])
        for b in writes:
            self.bufs[b] = {"w": {tok[0]: tok[1]}, "r": {}}
        for b in reads:
            st = self.bufs.setdefault(b, {"w": {}, "r": {}})
            if st["r"].get(tok[0], 0) < tok[1]:
                st["r"][tok[0]] = tok[1]
        return tok

    def barrier(self):
        assert self.scope == ""
        nc = self.nc
        for ename in self.ENG_SEM:
            gk = self.ENG_SEM[ename][0]
            getattr(nc, ename).drain().then_inc(self.sem[gk], 1)
            self.cnt[gk] += 1
        targets = {s: self.cnt[s] for s in self.ALL_SEMS}
        for ename in self.ENG_SEM:
            eng = getattr(nc, ename)
            for s, v in targets.items():
                if v > 0:
                    eng.wait_ge(self.sem[s], v)
            for s, v in targets.items():
                self.known[(ename, s)] = v
        for b in self.bufs.values():
            pass


def build_program(stop_after=None, dbg=()):
    KSC = int(_os.environ.get("KDBG_SC", "16"))
    KSKIP = set(_os.environ.get("KDBG_SKIP", "").split(","))
    """stop_after in {None,'P1k','P1v','P2','A','O','F1'}; dbg: iterable of
    scratch names to expose as ExternalOutput."""
    nc = bass.Bass("TRN2", target_bir_lowering=False)

    def kind_of(name):
        return "ExternalOutput" if name in dbg else "Internal"

    xT = nc.dram_tensor("xT", [D, T], F32, kind="ExternalInput").ap()
    x_ownT = nc.dram_tensor("x_ownT", [D, NT], F32, kind="ExternalInput").ap()
    wq = nc.dram_tensor("wq", [D, D], F32, kind="ExternalInput").ap()
    wk = nc.dram_tensor("wk", [D, D], F32, kind="ExternalInput").ap()
    wv = nc.dram_tensor("wv", [D, D], F32, kind="ExternalInput").ap()
    wo = nc.dram_tensor("wo", [D, D], F32, kind="ExternalInput").ap()
    wgt = nc.dram_tensor("wgt", [D, HID], F32, kind="ExternalInput").ap()
    wut = nc.dram_tensor("wut", [D, HID], F32, kind="ExternalInput").ap()
    wdt = nc.dram_tensor("wdt", [HID, D], F32, kind="ExternalInput").ap()
    cidx = nc.dram_tensor("cidx", [1, 1], I32, kind="ExternalInput").ap()
    out_fm = nc.dram_tensor("out_fm", [D, NT], F32, kind="ExternalOutput").ap()
    k_own = nc.dram_tensor("k_own", [D, NT], BF16, kind="Internal")
    v_own = nc.dram_tensor("v_own", [NT, D], BF16, kind="Internal")
    k_ag = nc.dram_tensor("k_ag", [4 * D, NT], BF16, kind=kind_of("k_ag"))
    v_ag = nc.dram_tensor("v_ag", [4 * NT, D], BF16, kind=kind_of("v_ag"))
    q_scr = nc.dram_tensor("q_scr", [H, 128, NT], BF16, kind=kind_of("q_scr")).ap()
    a_scr = nc.dram_tensor("a_scr", [H, 128, NT], BF16, kind=kind_of("a_scr")).ap()
    y1f_scr = nc.dram_tensor("y1f_scr", [NDK, 128, NT], F32, kind=kind_of("y1f_scr")).ap()
    y1b_scr = nc.dram_tensor("y1b_scr", [NDK, 128, NT], BF16, kind=kind_of("y1b_scr")).ap()
    hid_scr = nc.dram_tensor("hid_scr", [HID // 128, 128, NT], BF16, kind=kind_of("hid_scr")).ap()
    dn_scr = nc.dram_tensor("dn_scr", [NDK, 128, NT], F32, kind=kind_of("dn_scr")).ap()
    r2_scr = nc.dram_tensor("r2_scr", [NDK, 128, NT], F32, kind=kind_of("r2_scr")).ap()

    fw = FW(nc)
    from contextlib import ExitStack

    with ExitStack() as outer:
        c_sb = outer.enter_context(nc.sbuf_tensor("c_sb", [1, 1], I32))
        ones_f = outer.enter_context(nc.sbuf_tensor("ones_f", [128, 1], F32))
        ones_b = outer.enter_context(nc.sbuf_tensor("ones_b", [128, 1], BF16))
        onesr_f = outer.enter_context(nc.sbuf_tensor("onesr_f", [1, 128], F32))
        masks = outer.enter_context(nc.sbuf_tensor("masks", [128, 4 * 512], F32))
        iotat = outer.enter_context(nc.sbuf_tensor("iotat", [128, 512], F32))
        sufA = outer.enter_context(nc.sbuf_tensor("sufA", [128, 65 * NDK], F32))
        sufB = outer.enter_context(nc.sbuf_tensor("sufB", [128, 65 * NDK], F32))
        sufXb = outer.enter_context(nc.sbuf_tensor("sufXb", [128, NDK * 65], BF16))
        sufV = outer.enter_context(nc.sbuf_tensor("sufV", [128, H * 65], F32))

        fw.op("sync", lambda: nc.sync.dma_start(c_sb[:, :], cidx),
              writes=["c_sb"], dma=True)
        if "init" not in KSKIP:
            fw.op("vector", lambda: nc.vector.memset(ones_f[:, :], 1.0), writes=["ones_f"])
            fw.op("vector", lambda: nc.vector.memset(ones_b[:, :], 1.0), writes=["ones_b"])
            fw.op("vector", lambda: nc.vector.memset(onesr_f[:, :], 1.0), writes=["onesr_f"])
            fw.op("scalar", lambda: load_act_set(nc, SET_LN_EXP), inc=False)
        for k in range(4 if "init" not in KSKIP else 0):
            fw.op("gpsimd",
                  lambda k=k: nc.gpsimd.iota(
                      iotat[:, :], [[1, 512]], base=-(128 * k),
                      channel_multiplier=-1,
                      allow_small_or_imprecise_dtypes=True),
                  writes=["iotat"])
            fw.op("vector",
                  lambda k=k: nc.vector.tensor_single_scalar(
                      masks[:, 512 * k:512 * (k + 1)], iotat[:, :], 0.0, ALU.is_ge),
                  reads=["iotat"], writes=["masks"])

        with ExitStack() as st:
            wk_sb = st.enter_context(nc.sbuf_tensor("wk_sb", [128, NDK * D], BF16))
            xc = [st.enter_context(nc.sbuf_tensor(f"xc{i}", [128, NDK * 512], BF16))
                  for i in range(2)]
            kst = [st.enter_context(nc.sbuf_tensor(f"kst{i}", [128, 512], BF16))
                   for i in range(2)]
            kps = st.enter_context(nc.psum_tensor("kps", [128, 4 * 512], F32))

            xs = [st.enter_context(nc.sbuf_tensor(f"xs{i}", [128, NDK * 256], F32))
                  for i in range(2)]
            fw.op("gpsimd", lambda: nc.gpsimd.dma_start(
                wk_sb[:, :].rearrange("p (a n) -> p a n", a=NDK),
                wk.rearrange("(a p) n -> p a n", p=128)),
                writes=["wk_sb"], dma=True)

            for sc in range(4):
                xb = f"xc{sc % 2}"
                fw.op("gpsimd", lambda sc=sc: nc.gpsimd.dma_start(
                    xc[sc % 2][:, :].rearrange("p (a s) -> p a s", a=NDK),
                    x_ownT.rearrange("(a p) s -> p a s", p=128)[:, :, 512 * sc:512 * (sc + 1)]),
                    writes=[xb], dma=True)
                for dcol in range(16):
                    pb = f"kps{dcol % 4}"
                    for dk in range(NDK):
                        fw.op("tensor", lambda sc=sc, dcol=dcol, dk=dk: nc.tensor.matmul(
                            kps[:, 512 * (dcol % 4):512 * (dcol % 4 + 1)],
                            wk_sb[:, D * dk + 128 * dcol: D * dk + 128 * (dcol + 1)],
                            xc[sc % 2][:, 512 * dk:512 * (dk + 1)],
                            start=(dk == 0), stop=(dk == NDK - 1)),
                            reads=["wk_sb", xb], writes=[pb])
                    ks = f"kst{dcol % 2}"
                    fw.op("scalar", lambda dcol=dcol: nc.scalar.copy(
                        kst[dcol % 2][:, :], kps[:, 512 * (dcol % 4):512 * (dcol % 4 + 1)]),
                        reads=[pb, ks], writes=[ks])
                    fw.op("sync", lambda sc=sc, dcol=dcol: nc.sync.dma_start(
                        k_own[128 * dcol:128 * (dcol + 1), 512 * sc:512 * (sc + 1)],
                        kst[dcol % 2][:, :]),
                        reads=[ks], writes=["k_own"], dma=True)
            for sc in range(32):
                xb = f"xs{sc % 2}"
                fw.op("sync", lambda sc=sc: nc.sync.dma_start(
                    xs[sc % 2][:, :].rearrange("p (a s) -> p a s", a=NDK),
                    xT.rearrange("(a p) s -> p a s", p=128)[:, :, 256 * sc:256 * (sc + 1)]),
                    writes=[xb], dma=True)
                for sb in range(2):
                    j = sc * 2 + sb
                    fw.op("vector", lambda sc=sc, sb=sb, j=j: nc.vector.reduce_sum(
                        sufA[:, 16 * j:16 * (j + 1)].rearrange("p (a o) -> p a o", o=1),
                        xs[sc % 2][:, :].rearrange("p (a s) -> p a s", a=NDK)[:, :, 128 * sb:128 * (sb + 1)],
                        axis=AX.X),
                        reads=[xb], writes=["sufA"])

            if "suf" in KSKIP:
                fw.barrier()
                if stop_after == "P1k":
                    return nc
            fw.op("vector", lambda: nc.vector.memset(sufA[:, 64 * 16:65 * 16], 0.0),
                  writes=["sufA"])
            cur, oth = sufA, sufB
            curb, othb = "sufA", "sufB"
            for lvl in range(6):
                sh = 1 << lvl
                w = (65 - sh) * 16
                fw.op("vector", lambda cur=cur, oth=oth, sh=sh, w=w: nc.vector.tensor_add(
                    oth[:, 0:w], cur[:, 0:w], cur[:, 16 * sh:16 * sh + w]),
                    reads=[curb], writes=[othb])
                fw.op("vector", lambda cur=cur, oth=oth, w=w: nc.vector.tensor_copy(
                    oth[:, w:65 * 16], cur[:, w:65 * 16]),
                    reads=[curb], writes=[othb])
                cur, oth = oth, cur
                curb, othb = othb, curb
            fw.op("scalar", lambda cur=cur: nc.scalar.copy(
                sufXb[:, :].rearrange("p (a j) -> p a j", a=NDK),
                cur[:, :].rearrange("p (j a) -> p a j", a=NDK)),
                reads=[curb], writes=["sufXb"])
        fw.barrier()
        if stop_after == "P1k":
            return nc

        with ExitStack() as st:
            wv_sb = st.enter_context(nc.sbuf_tensor("wv_sb", [128, NDK * D], BF16))
            xc = [st.enter_context(nc.sbuf_tensor(f"xv{i}", [128, NDK * 512], BF16))
                  for i in range(2)]
            vst = [st.enter_context(nc.sbuf_tensor(f"vst{i}", [128, D], BF16))
                   for i in range(2)]
            vps = [st.enter_context(nc.psum_tensor(f"vps{i}", [128, D], F32))
                   for i in range(2)]
            svps = vps[0]

            fw.op("gpsimd", lambda: nc.gpsimd.dma_start(
                wv_sb[:, :].rearrange("p (a n) -> p a n", a=NDK),
                wv.rearrange("(a p) n -> p a n", p=128)),
                writes=["wv_sb"], dma=True)

            for sc in range(4):
                xb = f"xv{sc % 2}"
                fw.op("gpsimd", lambda sc=sc: nc.gpsimd.dma_start(
                    xc[sc % 2][:, :].rearrange("p (a s) -> p a s", a=NDK),
                    x_ownT.rearrange("(a p) s -> p a s", p=128)[:, :, 512 * sc:512 * (sc + 1)]),
                    writes=[xb], dma=True)
                for sblk in range(4):
                    sb = sc * 4 + sblk
                    pb = f"vps{sb % 2}"
                    for dcb in range(4):
                        for dk in range(NDK):
                            fw.op("tensor", lambda sc=sc, sblk=sblk, sb=sb, dcb=dcb, dk=dk:
                                  nc.tensor.matmul(
                                      vps[sb % 2][:, 512 * dcb:512 * (dcb + 1)],
                                      xc[sc % 2][:, 512 * dk + 128 * sblk:512 * dk + 128 * (sblk + 1)],
                                      wv_sb[:, D * dk + 512 * dcb: D * dk + 512 * (dcb + 1)],
                                      start=(dk == 0), stop=(dk == NDK - 1)),
                                  reads=["wv_sb", xb], writes=[pb])
                    vs = f"vst{sb % 2}"
                    fw.op("vector", lambda sb=sb: nc.vector.tensor_copy(
                        vst[sb % 2][:, :], vps[sb % 2][:, :]),
                        reads=[pb], writes=[vs])
                    fw.op("sync", lambda sb=sb: nc.sync.dma_start(
                        v_own[128 * sb:128 * (sb + 1), :], vst[sb % 2][:, :]),
                        reads=[vs], writes=["v_own"], dma=True)

            for h in range(H):
                for dk in range(NDK):
                    fw.op("tensor", lambda h=h, dk=dk: nc.tensor.matmul(
                        svps[:, 0:65],
                        wv_sb[:, D * dk + 128 * h: D * dk + 128 * (h + 1)],
                        sufXb[:, 65 * dk:65 * (dk + 1)],
                        start=(dk == 0), stop=(dk == NDK - 1)),
                        reads=["wv_sb", "sufXb"], writes=["vps0"])
                fw.op("scalar", lambda h=h: nc.scalar.copy(
                    sufV[:, 65 * h:65 * (h + 1)], svps[:, 0:65]),
                    reads=["vps0"], writes=["sufV"])
        fw.barrier()
        if stop_after == "P1v":
            return nc

        with ExitStack() as st:
            wq_sb = st.enter_context(nc.sbuf_tensor("wq_sb", [128, NDK * D], BF16))
            xc = [st.enter_context(nc.sbuf_tensor(f"xq{i}", [128, NDK * 512], BF16))
                  for i in range(2)]
            qst = [st.enter_context(nc.sbuf_tensor(f"qst{i}", [128, 512], BF16))
                   for i in range(2)]
            qps = st.enter_context(nc.psum_tensor("qps", [128, 4 * 512], F32))

            fw.op("gpsimd", lambda: nc.gpsimd.collective_compute(
                "AllGather", ALU.bypass,
                replica_groups=[[0, 1, 2, 3], [4, 5, 6, 7]],
                ins=[k_own.ap().opt()], outs=[k_ag.ap().opt()]),
                reads=["k_own"], writes=["k_ag"])
            fw.op("gpsimd", lambda: nc.gpsimd.collective_compute(
                "AllGather", ALU.bypass,
                replica_groups=[[0, 1, 2, 3], [4, 5, 6, 7]],
                ins=[v_own.ap().opt()], outs=[v_ag.ap().opt()]),
                reads=["v_own"], writes=["v_ag"])
            fw.op("gpsimd", lambda: nc.gpsimd.dma_start(
                wq_sb[:, :].rearrange("p (a n) -> p a n", a=NDK),
                wq.rearrange("(a p) n -> p a n", p=128)),
                writes=["wq_sb"], dma=True)
            for sc in range(4):
                xb = f"xq{sc % 2}"
                fw.op("gpsimd", lambda sc=sc: nc.gpsimd.dma_start(
                    xc[sc % 2][:, :].rearrange("p (a s) -> p a s", a=NDK),
                    x_ownT.rearrange("(a p) s -> p a s", p=128)[:, :, 512 * sc:512 * (sc + 1)]),
                    writes=[xb], dma=True)
                for dcol in range(16):
                    pb = f"qps{dcol % 4}"
                    for dk in range(NDK):
                        fw.op("tensor", lambda sc=sc, dcol=dcol, dk=dk: nc.tensor.matmul(
                            qps[:, 512 * (dcol % 4):512 * (dcol % 4 + 1)],
                            wq_sb[:, D * dk + 128 * dcol: D * dk + 128 * (dcol + 1)],
                            xc[sc % 2][:, 512 * dk:512 * (dk + 1)],
                            start=(dk == 0), stop=(dk == NDK - 1)),
                            reads=["wq_sb", xb], writes=[pb])
                    qs = f"qst{dcol % 2}"
                    fw.op("scalar", lambda dcol=dcol: nc.scalar.copy(
                        qst[dcol % 2][:, :], qps[:, 512 * (dcol % 4):512 * (dcol % 4 + 1)]),
                        reads=[pb], writes=[qs])
                    fw.op("sync", lambda sc=sc, dcol=dcol: nc.sync.dma_start(
                        q_scr[dcol, :, 512 * sc:512 * (sc + 1)], qst[dcol % 2][:, :]),
                        reads=[qs], writes=[f"q_scr{dcol}"], dma=True)
        fw.barrier()
        if stop_after == "P2":
            return nc

        with ExitStack() as st:
            qh = [st.enter_context(nc.sbuf_tensor(f"qh{i}", [128, NT], BF16))
                  for i in range(2)]
            kh = [st.enter_context(nc.sbuf_tensor(f"kh{i}", [128, T], BF16))
                  for i in range(2)]
            vh = [st.enter_context(nc.sbuf_tensor(f"vh{i}", [128, T], BF16))
                  for i in range(2)]
            pp = [st.enter_context(nc.sbuf_tensor(f"pp{i}", [128, 1024], BF16))
                  for i in range(4)]
            ptmp = [st.enter_context(nc.sbuf_tensor(f"ptmp{i}", [128, 1024], BF16))
                    for i in range(2)]
            pacc_v = st.enter_context(nc.sbuf_tensor("pacc_v", [128, 1024], F32))
            zc_sb = st.enter_context(nc.sbuf_tensor("zc_sb", [1, 512], F32))
            zi_sb = st.enter_context(nc.sbuf_tensor("zi_sb", [1, 512], F32))
            zb_sb = st.enter_context(nc.sbuf_tensor("zb_sb", [128, 512], F32))
            ast = [st.enter_context(nc.sbuf_tensor(f"ast{i}", [128, 512], BF16))
                   for i in range(2)]
            nps = st.enter_context(nc.psum_tensor("nps", [128, 2 * 512], F32))
            stp = [st.enter_context(nc.psum_tensor(f"stp{i}", [128, 1024], F32))
                   for i in range(2)]
            zps = st.enter_context(nc.psum_tensor("zps", [128, 512], F32))
            zbp = st.enter_context(nc.psum_tensor("zbp", [128, 512], F32))

            engines = [nc.tensor, nc.scalar, nc.vector, nc.gpsimd, nc.sync]
            idxs = []
            for eng, ename in zip(engines, ["tensor", "scalar", "vector", "gpsimd", "sync"]):
                fw._waits(ename, {"DQ": 16})
                r = eng.alloc_register()
                eng.reg_load(r, c_sb[0:1, 0:1])
                idxs.append(RuntimeValue(r, min_val=0, max_val=3))

            first_body = True
            for c in nc.Switch(engines=engines, index=idxs, n=4):
                if first_body:
                    fw.enter_body()
                    first_body = False
                else:
                    fw.body_reset()
                emit_attention(nc, fw, c, qh, kh, vh, pp, ptmp, pacc_v, zc_sb, zi_sb,
                               zb_sb, ast, nps, stp, zps, zbp,
                               masks, sufV, ones_f, onesr_f,
                               q_scr, k_ag, v_ag, a_scr)
                fw.exit_body()
            fw.finish_switch()
            for h in range(H):
                fw.wrote(f"a_scr{h}", "gpsimd")
        fw.barrier()
        if stop_after == "A":
            return nc

        with ExitStack() as st:
            wo_sb = st.enter_context(nc.sbuf_tensor("wo_sb", [128, NDK * D], BF16))
            atb = [st.enter_context(nc.sbuf_tensor(f"atb{i}", [128, NDK * 512], BF16))
                   for i in range(2)]
            xres = [st.enter_context(nc.sbuf_tensor(f"xres{i}", [128, 512], F32))
                    for i in range(4)]
            r1 = st.enter_context(nc.sbuf_tensor("r1", [128, NDK * 512], F32))
            r1b = [st.enter_context(nc.sbuf_tensor(f"r1b{i}", [128, 512], BF16))
                   for i in range(2)]
            sqb = [st.enter_context(nc.sbuf_tensor(f"sqb{i}", [128, 512], BF16))
                   for i in range(2)]
            trow = st.enter_context(nc.sbuf_tensor("trow", [1, 512], F32))
            mrow = st.enter_context(nc.sbuf_tensor("mrow", [1, 512], F32))
            vrow = st.enter_context(nc.sbuf_tensor("vrow", [1, 512], F32))
            irow = st.enter_context(nc.sbuf_tensor("irow", [1, 512], F32))
            brow = st.enter_context(nc.sbuf_tensor("brow", [1, 512], F32))
            y1fs = [st.enter_context(nc.sbuf_tensor(f"y1fs{i}", [128, 512], F32))
                    for i in range(2)]
            abc_sb = st.enter_context(nc.sbuf_tensor("abc_sb", [128, 512], F32))
            bbc_sb = st.enter_context(nc.sbuf_tensor("bbc_sb", [128, 512], F32))
            y1bs = [st.enter_context(nc.sbuf_tensor(f"y1bs{i}", [128, 512], BF16))
                    for i in range(2)]
            ops_ = st.enter_context(nc.psum_tensor("ops", [128, 4 * 512], F32))
            sumps = st.enter_context(nc.psum_tensor("sumps", [128, 512], F32))
            sqps = st.enter_context(nc.psum_tensor("sqps", [128, 512], F32))
            abc = st.enter_context(nc.psum_tensor("abc", [128, 512], F32))
            bbc = st.enter_context(nc.psum_tensor("bbc", [128, 512], F32))

            fw.op("gpsimd", lambda: nc.gpsimd.dma_start(
                wo_sb[:, :].rearrange("p (a n) -> p a n", a=NDK),
                wo.rearrange("(a p) n -> p a n", p=128)),
                writes=["wo_sb"], dma=True)

            for tb in range(4):
                ab = f"atb{tb % 2}"
                fw.op("sync", lambda tb=tb: nc.sync.dma_start(
                    atb[tb % 2][:, :].rearrange("p (a s) -> p a s", a=NDK),
                    a_scr.rearrange("h p t -> p h t")[:, :, 512 * tb:512 * (tb + 1)]),
                    reads=[f"a_scr{h}" for h in range(H)], writes=[ab], dma=True)
                for dcol in range(16):
                    pb = f"ops{dcol % 4}"
                    xr = f"xres{dcol % 4}"
                    fw.op("sync", lambda tb=tb, dcol=dcol: nc.sync.dma_start(
                        xres[dcol % 4][:, :],
                        x_ownT[128 * dcol:128 * (dcol + 1), 512 * tb:512 * (tb + 1)]),
                        writes=[xr], dma=True)
                    for dk in range(NDK):
                        fw.op("tensor", lambda tb=tb, dcol=dcol, dk=dk: nc.tensor.matmul(
                            ops_[:, 512 * (dcol % 4):512 * (dcol % 4 + 1)],
                            wo_sb[:, D * dk + 128 * dcol: D * dk + 128 * (dcol + 1)],
                            atb[tb % 2][:, 512 * dk:512 * (dk + 1)],
                            start=(dk == 0), stop=(dk == NDK - 1)),
                            reads=["wo_sb", ab], writes=[pb])
                    fw.op("vector", lambda dcol=dcol: nc.vector.tensor_add(
                        r1[:, 512 * dcol:512 * (dcol + 1)],
                        xres[dcol % 4][:, :],
                        ops_[:, 512 * (dcol % 4):512 * (dcol % 4 + 1)]),
                        reads=[xr, pb], writes=["r1"])
                    fw.op("scalar", lambda dcol=dcol: nc.scalar.copy(
                        r1b[dcol % 2][:, :], r1[:, 512 * dcol:512 * (dcol + 1)]),
                        reads=["r1"], writes=[f"r1b{dcol % 2}"])
                    fw.op("scalar", lambda dcol=dcol: nc.scalar.activation(
                        sqb[dcol % 2][:, :], r1[:, 512 * dcol:512 * (dcol + 1)], AF.Square),
                        reads=["r1"], writes=[f"sqb{dcol % 2}"])
                    fw.op("tensor", lambda dcol=dcol: nc.tensor.matmul(
                        sumps[0:1, :], ones_b[:, :], r1b[dcol % 2][:, :],
                        start=(dcol == 0), stop=(dcol == 15)),
                        reads=[f"r1b{dcol % 2}", "ones_b"], writes=["sumps"])
                    fw.op("tensor", lambda dcol=dcol: nc.tensor.matmul(
                        sqps[0:1, :], ones_b[:, :], sqb[dcol % 2][:, :],
                        start=(dcol == 0), stop=(dcol == 15)),
                        reads=[f"sqb{dcol % 2}", "ones_b"], writes=["sqps"])
                fw.op("scalar", lambda: nc.scalar.activation(mrow[:, :], sumps[0:1, :], AF.Identity, scale=1.0 / D),
                      reads=["sumps"], writes=["mrow"])
                fw.op("scalar", lambda: nc.scalar.activation(vrow[:, :], sqps[0:1, :], AF.Identity, scale=1.0 / D),
                      reads=["sqps"], writes=["vrow"])
                fw.op("vector", lambda: nc.vector.tensor_mul(trow[:, :], mrow[:, :], mrow[:, :]),
                      reads=["mrow"], writes=["trow"])
                fw.op("vector", lambda: nc.vector.tensor_sub(vrow[:, :], vrow[:, :], trow[:, :]),
                      reads=["vrow", "trow"], writes=["vrow"])
                fw.op("vector", lambda: nc.vector.tensor_scalar(vrow[:, :], vrow[:, :], EPS, None, ALU.add),
                      reads=["vrow"], writes=["vrow"])
                fw.op("scalar", lambda: nc.scalar.activation(trow[:, :], vrow[:, :], AF.Ln),
                      reads=["vrow"], writes=["trow"])
                fw.op("scalar", lambda: nc.scalar.activation(irow[:, :], trow[:, :], AF.Exp, scale=-0.5),
                      reads=["trow"], writes=["irow"])
                fw.op("vector", lambda: nc.vector.tensor_mul(brow[:, :], mrow[:, :], irow[:, :]),
                      reads=["mrow", "irow"], writes=["brow"])
                fw.op("vector", lambda: nc.vector.tensor_scalar_mul(brow[:, :], brow[:, :], -1.0),
                      reads=["brow"], writes=["brow"])
                fw.op("tensor", lambda: nc.tensor.matmul(
                    abc[:, :], onesr_f[:, :], irow[:, :], start=True, stop=True),
                    reads=["irow", "onesr_f"], writes=["abc"])
                fw.op("tensor", lambda: nc.tensor.matmul(
                    bbc[:, :], onesr_f[:, :], brow[:, :], start=True, stop=True),
                    reads=["brow", "onesr_f"], writes=["bbc"])
                fw.op("scalar", lambda: nc.scalar.copy(abc_sb[:, :], abc[:, :]),
                      reads=["abc", "abc_sb"], writes=["abc_sb"])
                fw.op("scalar", lambda: nc.scalar.copy(bbc_sb[:, :], bbc[:, :]),
                      reads=["bbc", "bbc_sb"], writes=["bbc_sb"])
                for dcol in range(16):
                    yf = f"y1fs{dcol % 2}"
                    yb = f"y1bs{dcol % 2}"
                    fw.op("vector", lambda dcol=dcol: nc.vector.tensor_mul(
                        y1fs[dcol % 2][:, :], r1[:, 512 * dcol:512 * (dcol + 1)], abc_sb[:, :]),
                        reads=["r1", "abc_sb"], writes=[yf])
                    fw.op("vector", lambda dcol=dcol: nc.vector.tensor_add(
                        y1fs[dcol % 2][:, :], y1fs[dcol % 2][:, :], bbc_sb[:, :]),
                        reads=[yf, "bbc_sb"], writes=[yf])
                    fw.op("scalar", lambda dcol=dcol: nc.scalar.copy(
                        y1bs[dcol % 2][:, :], y1fs[dcol % 2][:, :]),
                        reads=[yf], writes=[yb])
                    fw.op("gpsimd", lambda tb=tb, dcol=dcol: nc.gpsimd.dma_start(
                        y1f_scr[dcol, :, 512 * tb:512 * (tb + 1)], y1fs[dcol % 2][:, :]),
                        reads=[yf], writes=[f"y1f_scr{dcol}"], dma=True)
                    fw.op("gpsimd", lambda tb=tb, dcol=dcol: nc.gpsimd.dma_start(
                        y1b_scr[dcol, :, 512 * tb:512 * (tb + 1)], y1bs[dcol % 2][:, :]),
                        reads=[yb], writes=["y1b_scr"], dma=True)
        fw.barrier()
        if stop_after == "O":
            return nc

        with ExitStack() as st:
            y1 = st.enter_context(nc.sbuf_tensor("y1", [128, NDK * NT], BF16))
            wgu = [st.enter_context(nc.sbuf_tensor(f"wgu{i}", [128, 2 * NDK * 128], BF16))
                   for i in range(2)]
            sg = [st.enter_context(nc.sbuf_tensor(f"sg{i}", [128, 1024], F32))
                  for i in range(2)]
            hst = [st.enter_context(nc.sbuf_tensor(f"hst{i}", [128, 1024], BF16))
                   for i in range(2)]
            gps = [st.enter_context(nc.psum_tensor(f"gps{i}", [128, 1024], F32))
                   for i in range(2)]
            ups = [st.enter_context(nc.psum_tensor(f"ups{i}", [128, 1024], F32))
                   for i in range(2)]

            fw.op("scalar", lambda: load_act_set(nc, SET_SILU), inc=False)
            fw.op("sync", lambda: nc.sync.dma_start(
                y1[:, :].rearrange("p (a t) -> p a t", a=NDK),
                y1b_scr.rearrange("a p t -> p a t")),
                reads=["y1b_scr"], writes=["y1"], dma=True)

            for ht in range(HID // 128):
                wb = f"wgu{ht % 2}"
                fw.op("gpsimd", lambda ht=ht: nc.gpsimd.dma_start(
                    wgu[ht % 2][:, 0:NDK * 128].rearrange("p (a n) -> p a n", a=NDK),
                    wgt.rearrange("(a p) n -> p a n", p=128)[:, :, 128 * ht:128 * (ht + 1)]),
                    writes=[wb], dma=True)
                fw.op("gpsimd", lambda ht=ht: nc.gpsimd.dma_start(
                    wgu[ht % 2][:, NDK * 128:2 * NDK * 128].rearrange("p (a n) -> p a n", a=NDK),
                    wut.rearrange("(a p) n -> p a n", p=128)[:, :, 128 * ht:128 * (ht + 1)]),
                    writes=[wb], dma=True)
                for half in range(2):
                    gb, ub = f"gps{half}", f"ups{half}"
                    for dk in range(NDK):
                        for q in range(2):
                            fw.op("tensor", lambda ht=ht, half=half, dk=dk, q=q: nc.tensor.matmul(
                                gps[half][:, 512 * q:512 * (q + 1)],
                                wgu[ht % 2][:, 128 * dk:128 * (dk + 1)],
                                y1[:, NT * dk + 1024 * half + 512 * q: NT * dk + 1024 * half + 512 * (q + 1)],
                                start=(dk == 0), stop=(dk == NDK - 1)),
                                reads=[wb, "y1"], writes=[gb])
                        for q in range(2):
                            fw.op("tensor", lambda ht=ht, half=half, dk=dk, q=q: nc.tensor.matmul(
                                ups[half][:, 512 * q:512 * (q + 1)],
                                wgu[ht % 2][:, NDK * 128 + 128 * dk:NDK * 128 + 128 * (dk + 1)],
                                y1[:, NT * dk + 1024 * half + 512 * q: NT * dk + 1024 * half + 512 * (q + 1)],
                                start=(dk == 0), stop=(dk == NDK - 1)),
                                reads=[wb, "y1"], writes=[ub])
                    sb_ = f"sg{half}"
                    hb = f"hst{half}"
                    fw.op("scalar", lambda half=half: nc.scalar.activation(
                        sg[half][:, :], gps[half][:, :], AF.Silu),
                        reads=[gb], writes=[sb_])
                    fw.op("vector", lambda half=half: nc.vector.tensor_mul(
                        hst[half][:, :], sg[half][:, :], ups[half][:, :]),
                        reads=[sb_, ub], writes=[hb])
                    fw.op("sync", lambda ht=ht, half=half: nc.sync.dma_start(
                        hid_scr[ht, :, 1024 * half:1024 * (half + 1)], hst[half][:, :]),
                        reads=[hb], writes=["hid_scr"], dma=True)
        fw.barrier()
        if stop_after == "F1":
            return nc

        with ExitStack() as st:
            hidh = st.enter_context(nc.sbuf_tensor("hidh", [128, 32 * 1024], BF16))
            wd = [st.enter_context(nc.sbuf_tensor(f"wd{i}", [128, 32 * 128], BF16))
                  for i in range(2)]
            acc = st.enter_context(nc.sbuf_tensor("acc", [128, NDK * 1024], F32))
            y1r = [st.enter_context(nc.sbuf_tensor(f"y1r{i}", [128, 1024], F32))
                   for i in range(2)]
            r2b = [st.enter_context(nc.sbuf_tensor(f"r2b{i}", [128, 1024], BF16))
                   for i in range(2)]
            sq2 = [st.enter_context(nc.sbuf_tensor(f"sq2{i}", [128, 1024], BF16))
                   for i in range(2)]
            trow = st.enter_context(nc.sbuf_tensor("trow2", [1, 1024], F32))
            mrow = st.enter_context(nc.sbuf_tensor("mrow2", [1, 1024], F32))
            vrow = st.enter_context(nc.sbuf_tensor("vrow2", [1, 1024], F32))
            irow = st.enter_context(nc.sbuf_tensor("irow2", [1, 1024], F32))
            brow = st.enter_context(nc.sbuf_tensor("brow2", [1, 1024], F32))
            dps = [st.enter_context(nc.psum_tensor(f"dps{i}", [128, 1024], F32))
                   for i in range(2)]
            sum2 = st.enter_context(nc.psum_tensor("sum2", [128, 1024], F32))
            sq2p = st.enter_context(nc.psum_tensor("sq2p", [128, 1024], F32))

            fw.op("scalar", lambda: load_act_set(nc, SET_LN_EXP), inc=False)

            for tp in range(2):
                for half in range(2):
                    fw.op("sync", lambda tp=tp, half=half: nc.sync.dma_start(
                        hidh[:, :].rearrange("p (a t) -> p a t", a=32),
                        hid_scr.rearrange("a p t -> p a t")[:, 32 * half:32 * (half + 1),
                                                           1024 * tp:1024 * (tp + 1)]),
                        reads=["hid_scr"], writes=["hidh"], dma=True)
                    for dcol in range(16):
                        wb = f"wd{dcol % 2}"
                        fw.op("gpsimd", lambda tp=tp, half=half, dcol=dcol: nc.gpsimd.dma_start(
                            wd[dcol % 2][:, :].rearrange("p (a n) -> p a n", a=32),
                            wdt[4096 * half:4096 * (half + 1),
                                128 * dcol:128 * (dcol + 1)].rearrange("(a p) n -> p a n", p=128)),
                            writes=[wb], dma=True)
                        pb = f"dps{dcol % 2}"
                        for hb in range(32):
                            for q in range(2):
                                fw.op("tensor", lambda half=half, dcol=dcol, hb=hb, q=q:
                                      nc.tensor.matmul(
                                          dps[dcol % 2][:, 512 * q:512 * (q + 1)],
                                          wd[dcol % 2][:, 128 * hb:128 * (hb + 1)],
                                          hidh[:, 1024 * hb + 512 * q:1024 * hb + 512 * (q + 1)],
                                          start=(hb == 0), stop=(hb == 31)),
                                      reads=[wb, "hidh"], writes=[pb])
                        if half == 0:
                            fw.op("vector", lambda dcol=dcol: nc.vector.tensor_copy(
                                acc[:, 1024 * dcol:1024 * (dcol + 1)], dps[dcol % 2][:, :]),
                                reads=[pb], writes=["acc"])
                        else:
                            fw.op("vector", lambda dcol=dcol: nc.vector.tensor_add(
                                acc[:, 1024 * dcol:1024 * (dcol + 1)],
                                acc[:, 1024 * dcol:1024 * (dcol + 1)], dps[dcol % 2][:, :]),
                                reads=[pb, "acc"], writes=["acc"])
                            if "dn_scr" in dbg:
                                fw.op("sync", lambda tp=tp, dcol=dcol: nc.sync.dma_start(
                                    dn_scr[dcol, :, 1024 * tp:1024 * (tp + 1)],
                                    acc[:, 1024 * dcol:1024 * (dcol + 1)]),
                                    reads=["acc"], writes=["dn_scr"], dma=True)
                for dcol in range(16):
                    yb = f"y1r{dcol % 2}"
                    fw.op("sync", lambda tp=tp, dcol=dcol: nc.sync.dma_start(
                        y1r[dcol % 2][:, :],
                        y1f_scr[dcol, :, 1024 * tp:1024 * (tp + 1)]),
                        reads=[f"y1f_scr{dcol}"], writes=[yb], dma=True)
                    fw.op("vector", lambda dcol=dcol: nc.vector.tensor_add(
                        acc[:, 1024 * dcol:1024 * (dcol + 1)],
                        acc[:, 1024 * dcol:1024 * (dcol + 1)], y1r[dcol % 2][:, :]),
                        reads=["acc", yb], writes=["acc"])
                    if "r2_scr" in dbg:
                        fw.op("sync", lambda tp=tp, dcol=dcol: nc.sync.dma_start(
                            r2_scr[dcol, :, 1024 * tp:1024 * (tp + 1)],
                            acc[:, 1024 * dcol:1024 * (dcol + 1)]),
                            reads=["acc"], writes=["r2_scr"], dma=True)
                    fw.op("scalar", lambda dcol=dcol: nc.scalar.copy(
                        r2b[dcol % 2][:, :], acc[:, 1024 * dcol:1024 * (dcol + 1)]),
                        reads=["acc"], writes=[f"r2b{dcol % 2}"])
                    fw.op("scalar", lambda dcol=dcol: nc.scalar.activation(
                        sq2[dcol % 2][:, :], acc[:, 1024 * dcol:1024 * (dcol + 1)], AF.Square),
                        reads=["acc"], writes=[f"sq2{dcol % 2}"])
                    for q in range(2):
                        fw.op("tensor", lambda dcol=dcol, q=q: nc.tensor.matmul(
                            sum2[0:1, 512 * q:512 * (q + 1)], ones_b[:, :],
                            r2b[dcol % 2][:, 512 * q:512 * (q + 1)],
                            start=(dcol == 0), stop=(dcol == 15)),
                            reads=[f"r2b{dcol % 2}", "ones_b"], writes=["sum2"])
                        fw.op("tensor", lambda dcol=dcol, q=q: nc.tensor.matmul(
                            sq2p[0:1, 512 * q:512 * (q + 1)], ones_b[:, :],
                            sq2[dcol % 2][:, 512 * q:512 * (q + 1)],
                            start=(dcol == 0), stop=(dcol == 15)),
                            reads=[f"sq2{dcol % 2}", "ones_b"], writes=["sq2p"])
                fw.op("scalar", lambda: nc.scalar.activation(mrow[:, :], sum2[0:1, :], AF.Identity, scale=1.0 / D),
                      reads=["sum2"], writes=["mrow2"])
                fw.op("scalar", lambda: nc.scalar.activation(vrow[:, :], sq2p[0:1, :], AF.Identity, scale=1.0 / D),
                      reads=["sq2p"], writes=["vrow2"])
                fw.op("vector", lambda: nc.vector.tensor_mul(trow[:, :], mrow[:, :], mrow[:, :]),
                      reads=["mrow2"], writes=["trow2"])
                fw.op("vector", lambda: nc.vector.tensor_sub(vrow[:, :], vrow[:, :], trow[:, :]),
                      reads=["vrow2", "trow2"], writes=["vrow2"])
                fw.op("vector", lambda: nc.vector.tensor_scalar(vrow[:, :], vrow[:, :], EPS, None, ALU.add),
                      reads=["vrow2"], writes=["vrow2"])
                fw.op("scalar", lambda: nc.scalar.activation(trow[:, :], vrow[:, :], AF.Ln),
                      reads=["vrow2"], writes=["trow2"])
                fw.op("scalar", lambda: nc.scalar.activation(irow[:, :], trow[:, :], AF.Exp, scale=-0.5),
                      reads=["trow2"], writes=["irow2"])
                fw.op("vector", lambda: nc.vector.tensor_mul(brow[:, :], mrow[:, :], irow[:, :]),
                      reads=["mrow2", "irow2"], writes=["brow2"])
                fw.op("vector", lambda: nc.vector.tensor_scalar_mul(brow[:, :], brow[:, :], -1.0),
                      reads=["brow2"], writes=["brow2"])
                for q in range(2):
                    fw.op("tensor", lambda q=q: nc.tensor.matmul(
                        dps[0][:, 512 * q:512 * (q + 1)], onesr_f[:, :],
                        irow[0:1, 512 * q:512 * (q + 1)], start=True, stop=True),
                        reads=["irow2"], writes=["dps0"])
                    fw.op("tensor", lambda q=q: nc.tensor.matmul(
                        dps[1][:, 512 * q:512 * (q + 1)], onesr_f[:, :],
                        brow[0:1, 512 * q:512 * (q + 1)], start=True, stop=True),
                        reads=["brow2"], writes=["dps1"])
                fw.op("scalar", lambda: nc.scalar.copy(y1r[0][:, :], dps[0][:, :]),
                      reads=["dps0", "y1r0"], writes=["y1r0"])
                fw.op("scalar", lambda: nc.scalar.copy(y1r[1][:, :], dps[1][:, :]),
                      reads=["dps1", "y1r1"], writes=["y1r1"])
                for dcol in range(16):
                    fw.op("vector", lambda dcol=dcol: nc.vector.tensor_mul(
                        acc[:, 1024 * dcol:1024 * (dcol + 1)],
                        acc[:, 1024 * dcol:1024 * (dcol + 1)], y1r[0][:, :]),
                        reads=["acc", "y1r0"], writes=["acc"])
                    fw.op("vector", lambda dcol=dcol: nc.vector.tensor_add(
                        acc[:, 1024 * dcol:1024 * (dcol + 1)],
                        acc[:, 1024 * dcol:1024 * (dcol + 1)], y1r[1][:, :]),
                        reads=["acc", "y1r1"], writes=["acc"])
                    fw.op("sync", lambda tp=tp, dcol=dcol: nc.sync.dma_start(
                        out_fm[128 * dcol:128 * (dcol + 1), 1024 * tp:1024 * (tp + 1)],
                        acc[:, 1024 * dcol:1024 * (dcol + 1)]),
                        reads=["acc"], writes=["out_fm"], dma=True)
        fw.barrier()
    return nc


def emit_attention(nc, fw, c, qh, kh, vh, pp, ptmp, pacc_v, zc_sb, zi_sb,
                   zb_sb, ast, nps, stp, zps, zbp, masks, sufV, ones_f, onesr_f,
                   q_scr, k_ag, v_ag, a_scr):
    ns = [8 * c + 4, 8 * c + 8, 60 - 8 * c, 64 - 8 * c]
    counts = [float(T - 128 * n) for n in ns]
    span_sb = ns[3]
    vchunk = span_sb // 8

    span_chunks = span_sb // 8
    kag = k_ag.ap().rearrange("(r d) s -> r d s", r=4)
    vag = v_ag.ap().rearrange("(r s) d -> r s d", r=4)

    def load_head(h):
        par = h % 2
        qb, kb, vb = f"qh{par}", f"kh{par}", f"vh{par}"
        fw.op("sync", lambda h=h: nc.sync.dma_start(
            qh[h % 2][:, :], q_scr[h, :, :]), writes=[qb], dma=True)
        for g in range(span_chunks):
            r, half = (g, 0) if g < 4 else (7 - g, 1)
            fw.op("sync", lambda h=h, g=g, r=r, half=half: nc.sync.dma_start(
                kh[h % 2][:, 1024 * g:1024 * (g + 1)],
                kag[r, 128 * h:128 * (h + 1), 1024 * half:1024 * (half + 1)]),
                writes=[kb], dma=True)
            fw.op("gpsimd", lambda h=h, g=g, r=r, half=half: nc.gpsimd.dma_start(
                vh[h % 2][:, 1024 * g:1024 * (g + 1)]
                  .rearrange("p (a d) -> p a d", a=8),
                vag[r, 1024 * half:1024 * (half + 1), 128 * h:128 * (h + 1)]
                  .rearrange("(a p) d -> p a d", p=128)),
                writes=[vb], dma=True)

    def finalize(h, tb, ti):
        prow = 0 if ti == 0 else 32
        fw.op("tensor", lambda ti=ti, prow=prow: nc.tensor.matmul(
            zps[prow:prow + 1, :], ones_f[:, :],
            pacc_v[:, 512 * ti:512 * (ti + 1)], start=True, stop=True),
            reads=[f"paccv{ti}", "ones_f"], writes=["zps"])
        fw.op("vector", lambda tb=tb, prow=prow: nc.vector.tensor_scalar(
            zc_sb[:, :], zps[prow:prow + 1, :], counts[tb], None, ALU.add),
            reads=["zps"], writes=["zc_sb"])
        fw.op("vector", lambda: nc.vector.reciprocal(zi_sb[:, :], zc_sb[:, :]),
              reads=["zc_sb"], writes=["zi_sb"])
        fw.op("tensor", lambda: nc.tensor.matmul(
            zbp[:, :], onesr_f[:, :], zi_sb[:, :], start=True, stop=True),
            reads=["zi_sb", "onesr_f"], writes=["zbp"])
        fw.op("scalar", lambda: nc.scalar.copy(zb_sb[:, :], zbp[:, :]),
              reads=["zbp", "zb_sb"], writes=["zb_sb"])
        apar = tb % 2
        fw.op("vector", lambda h=h, tb=tb, ti=ti, apar=apar: nc.vector.scalar_tensor_tensor(
            ast[apar][:, :], nps[:, 512 * ti:512 * (ti + 1)],
            sufV[:, 65 * h + ns[tb]:65 * h + ns[tb] + 1],
            zb_sb[:, :], ALU.add, ALU.mult),
            reads=[f"nps{ti}", "sufV", "zb_sb"], writes=[f"ast{apar}"])
        fw.op("gpsimd", lambda h=h, tb=tb, apar=apar: nc.gpsimd.dma_start(
            a_scr[h, :, 512 * tb:512 * (tb + 1)], ast[apar][:, :]),
            reads=[f"ast{apar}"], writes=[f"a_scr{h}"], dma=True)

    load_head(0)
    for h in range(H):
        par = h % 2
        qb, kb, vb = f"qh{par}", f"kh{par}", f"vh{par}"
        if h + 1 < H:
            load_head(h + 1)
        for p in range(2):
            lo, hi = 2 * p, 2 * p + 1
            n_lo, n_hi = ns[lo], ns[hi]
            def av_emit(sb):
                both = sb < n_lo
                ppar = sb % 4
                ppb = f"pp{ppar}"
                if both:
                    fw.op("tensor", lambda sb=sb, par=par, ppar=ppar, n_lo=n_lo:
                          nc.tensor.matmul(
                              nps[:, 0:512],
                              vh[par][:, 128 * sb:128 * (sb + 1)],
                              pp[ppar][:, 0:512],
                              start=(sb == 0), stop=(sb == n_lo - 1)),
                          reads=[f"vh{par}", ppb], writes=["nps0"])
                fw.op("tensor", lambda sb=sb, par=par, ppar=ppar, n_hi=n_hi:
                      nc.tensor.matmul(
                          nps[:, 512:1024],
                          vh[par][:, 128 * sb:128 * (sb + 1)],
                          pp[ppar][:, 512:1024],
                          start=(sb == 0), stop=(sb == n_hi - 1)),
                      reads=[f"vh{par}", ppb], writes=["nps1"])

            LAG = 2
            for sb in range(n_hi):
                both = sb < n_lo
                ppar = sb % 4
                spar = sb % 2
                ppb = f"pp{ppar}"
                stL, stH = f"st{spar}L", f"st{spar}H"
                lo0 = 0 if both else 512
                if both:
                    fw.op("tensor", lambda sb=sb, lo=lo, par=par, spar=spar: nc.tensor.matmul(
                        stp[spar][:, 0:512],
                        kh[par][:, 128 * sb:128 * (sb + 1)],
                        qh[par][:, 512 * lo:512 * (lo + 1)],
                        start=True, stop=True),
                        reads=[kb, qb], writes=[stL])
                fw.op("tensor", lambda sb=sb, hi=hi, par=par, spar=spar: nc.tensor.matmul(
                    stp[spar][:, 512:1024],
                    kh[par][:, 128 * sb:128 * (sb + 1)],
                    qh[par][:, 512 * hi:512 * (hi + 1)],
                    start=True, stop=True),
                    reads=[kb, qb], writes=[stH])
                if both and n_lo - 4 <= sb < n_lo:
                    k = sb - (n_lo - 4)
                    fw.op("vector", lambda k=k, spar=spar: nc.vector.tensor_mul(
                        stp[spar][:, 0:512], stp[spar][:, 0:512],
                        masks[:, 512 * k:512 * (k + 1)]),
                        reads=[stL, "masks"], writes=[stL])
                if n_hi - 4 <= sb < n_hi:
                    k = sb - (n_hi - 4)
                    fw.op("vector", lambda k=k, spar=spar: nc.vector.tensor_mul(
                        stp[spar][:, 512:1024], stp[spar][:, 512:1024],
                        masks[:, 512 * k:512 * (k + 1)]),
                        reads=[stH, "masks"], writes=[stH])
                st_reads = ([stL, stH] if both else [stH])
                fw.op("scalar", lambda ppar=ppar, lo0=lo0, spar=spar: nc.scalar.activation(
                    pp[ppar][:, lo0:1024], stp[spar][:, lo0:1024], AF.Exp, scale=SCALE),
                    reads=st_reads, writes=[ppb])
                if sb % 2 == 1:
                    tpar = (sb // 2) % 2
                    tpb = f"ptmp{tpar}"
                    ppb_prev = f"pp{(sb - 1) % 4}"
                    plo0 = 0 if (sb - 1) < n_lo else 512
                    pkeys = ["paccv0", "paccv1"] if plo0 == 0 else ["paccv1"]
                    fw.op("vector", lambda ppar=ppar, sb=sb, plo0=plo0, tpar=tpar:
                          nc.vector.tensor_add(
                              ptmp[tpar][:, plo0:1024],
                              pp[(sb - 1) % 4][:, plo0:1024], pp[ppar][:, plo0:1024]),
                          reads=[ppb, ppb_prev, tpb], writes=[tpb])
                    if sb == 1:
                        fw.op("vector", lambda tpar=tpar: nc.vector.tensor_copy(
                            pacc_v[:, :], ptmp[tpar][:, :]),
                            reads=[tpb], writes=["paccv0", "paccv1"])
                    else:
                        fw.op("vector", lambda tpar=tpar, plo0=plo0: nc.vector.tensor_add(
                            pacc_v[:, plo0:1024], pacc_v[:, plo0:1024], ptmp[tpar][:, plo0:1024]),
                            reads=[tpb] + pkeys, writes=pkeys)
                if sb >= LAG:
                    av_emit(sb - LAG)
                if sb - LAG == n_lo + 1:
                    finalize(h, lo, 0)
            for sb in range(max(n_hi - LAG, 0), n_hi):
                av_emit(sb)
                if sb == n_lo + 1:
                    finalize(h, lo, 0)
            finalize(h, hi, 1)


def kernel(L0_attn_x, L0_attn_Wq, L0_attn_Wk, L0_attn_Wv, L0_attn_Wo,
           L0_ffn_w_gate, L0_ffn_w_up, L0_ffn_w_down):
    x = np.asarray(L0_attn_x)
    wq = np.ascontiguousarray(np.asarray(L0_attn_Wq), dtype=np.float32)
    wk = np.ascontiguousarray(np.asarray(L0_attn_Wk), dtype=np.float32)
    wv = np.ascontiguousarray(np.asarray(L0_attn_Wv), dtype=np.float32)
    wo = np.ascontiguousarray(np.asarray(L0_attn_Wo), dtype=np.float32)
    wgt = np.ascontiguousarray(np.asarray(L0_ffn_w_gate).T, dtype=np.float32)
    wut = np.ascontiguousarray(np.asarray(L0_ffn_w_up).T, dtype=np.float32)
    wdt = np.ascontiguousarray(np.asarray(L0_ffn_w_down).T, dtype=np.float32)

    xT = [np.ascontiguousarray(x[b].T, dtype=np.float32) for b in range(B)]

    nc = build_program()
    in_maps = []
    for core in range(8):
        b, c = core // 4, core % 4
        xo = np.concatenate([xT[b][:, 1024 * c:1024 * (c + 1)],
                             xT[b][:, 1024 * (7 - c):1024 * (8 - c)]], axis=1)
        in_maps.append({
            "xT": xT[b],
            "x_ownT": np.ascontiguousarray(xo),
            "wq": wq, "wk": wk, "wv": wv, "wo": wo,
            "wgt": wgt, "wut": wut, "wdt": wdt,
            "cidx": np.array([[c]], dtype=np.int32),
        })
    res = run_bass_kernel_spmd(nc, in_maps, core_ids=list(range(8)))
    out = np.empty((B, T, D), dtype=np.float32)
    for core in range(8):
        b, c = core // 4, core % 4
        o = np.asarray(res.results[core]["out_fm"])
        out[b, 1024 * c:1024 * (c + 1), :] = o[:, 0:1024].T
        out[b, 1024 * (7 - c):1024 * (8 - c), :] = o[:, 1024:2048].T
    return out


# version 09
# speedup vs baseline: 1.3516x; 1.0216x over previous
import math
import os as _os
import numpy as np
import ml_dtypes
import concourse.bass as bass
import concourse.mybir as mybir
from concourse.expressions import RuntimeValue
from concourse.bass_utils import run_bass_kernel_spmd

F32 = mybir.dt.float32
BF16 = mybir.dt.bfloat16
I32 = mybir.dt.int32
AF = mybir.ActivationFunctionType
ALU = mybir.AluOpType
AX = mybir.AxisListType

D = 2048
T = 8192
B = 2
H = 16
DH = 128
HID = 8192
NT = 2048
SCALE = 1.0 / math.sqrt(T)
EPS = 1e-5
NDK = D // 128
NSB = T // 128

SET_LN_EXP = 6
SET_SILU = 18


def load_act_set(nc, set_id):
    return nc.scalar.add_instruction(mybir.InstLoadActFuncSet(
        name=nc.get_next_instruction_name(),
        act_func_set_id=set_id, ins=[], outs=[]))


class FW:
    ENG_SEM = {
        "tensor": ("T", None),
        "scalar": ("S", None),
        "vector": ("V", None),
        "gpsimd": ("PG", "DG"),
        "sync": ("SY", "DQ"),
    }
    ALL_SEMS = ["T", "S", "V", "PG", "SY", "DG", "DQ"]

    def __init__(self, nc):
        self.nc = nc
        self.sem = {}
        self.scope = ""
        self.cnt = {}
        self.known = {}
        self.bufs = {}
        for s in self.ALL_SEMS:
            for suf in ("", "b"):
                k = s + suf
                self.sem[k] = nc.alloc_semaphore(name="sem_" + k)
                self.cnt[k] = 0

    def semkey(self, engine, dma):
        c, d = self.ENG_SEM[engine]
        base = d if dma else c
        if self.scope and base in ("T", "S", "V", "PG", "SY", "DG", "DQ"):
            return base + self.scope
        return base

    def enter_body(self):
        self.scope = "b"
        for s in self.ALL_SEMS:
            self.cnt[s + "b"] = 0
        self.known = {k: v for k, v in self.known.items() if not k[1].endswith("b")}
        self._saved_bufs = {k: {"w": dict(v["w"]), "r": dict(v["r"])}
                            for k, v in self.bufs.items()}
        self._saved_known = dict(self.known)
        self._saved_cnt = dict(self.cnt)

    def exit_body(self):
        nc = self.nc
        finals = {s + "b": self.cnt[s + "b"] for s in self.ALL_SEMS}
        for ename in self.ENG_SEM:
            eng = getattr(nc, ename)
            for sk, v in finals.items():
                if v > 0:
                    eng.wait_ge(self.sem[sk], v)
            gk = self.ENG_SEM[ename][0]
            eng.drain().then_inc(self.sem[gk], 1)
        self.scope = ""

    def body_reset(self):
        self.scope = "b"
        self.bufs = {k: {"w": dict(v["w"]), "r": dict(v["r"])}
                     for k, v in self._saved_bufs.items()}
        self.known = dict(self._saved_known)
        self.cnt = dict(self._saved_cnt)
        for s in self.ALL_SEMS:
            self.cnt[s + "b"] = 0

    def finish_switch(self):
        self.scope = ""
        for ename in self.ENG_SEM:
            gk = self.ENG_SEM[ename][0]
            self.cnt[gk] += 1

    def wrote(self, buf, engine):
        gk = self.ENG_SEM[engine][0]
        self.bufs[buf] = {"w": {gk: self.cnt[gk]}, "r": {}}

    def _waits(self, ename, deps):
        eng = getattr(self.nc, ename)
        for sk, v in sorted(deps.items()):
            if v <= 0:
                continue
            kk = (ename, sk)
            if self.known.get(kk, 0) >= v:
                continue
            eng.wait_ge(self.sem[sk], v)
            self.known[kk] = v

    def op(self, ename, fn, reads=(), writes=(), dma=False, inc=True):
        deps = {}
        my_sk = self.semkey(ename, dma)
        my_csk = self.semkey(ename, False)

        def merge(d):
            for sk, v in d.items():
                if sk == my_csk and not dma and ename == "tensor":
                    continue
                if deps.get(sk, 0) < v:
                    deps[sk] = v

        for b in reads:
            st = self.bufs.get(b)
            if st:
                merge(st["w"])
        for b in writes:
            st = self.bufs.get(b)
            if st:
                merge(st["w"])
                merge(st["r"])
        self._waits(ename, deps)
        inst = fn()
        if inc:
            amt = 16 if dma else 1
            inst.then_inc(self.sem[my_sk], amt)
            self.cnt[my_sk] += amt
        tok = (my_sk, self.cnt[my_sk])
        for b in writes:
            self.bufs[b] = {"w": {tok[0]: tok[1]}, "r": {}}
        for b in reads:
            st = self.bufs.setdefault(b, {"w": {}, "r": {}})
            if st["r"].get(tok[0], 0) < tok[1]:
                st["r"][tok[0]] = tok[1]
        return tok

    def barrier(self):
        assert self.scope == ""
        nc = self.nc
        for ename in self.ENG_SEM:
            gk = self.ENG_SEM[ename][0]
            getattr(nc, ename).drain().then_inc(self.sem[gk], 1)
            self.cnt[gk] += 1
        targets = {s: self.cnt[s] for s in self.ALL_SEMS}
        for ename in self.ENG_SEM:
            eng = getattr(nc, ename)
            for s, v in targets.items():
                if v > 0:
                    eng.wait_ge(self.sem[s], v)
            for s, v in targets.items():
                self.known[(ename, s)] = v
        for b in self.bufs.values():
            pass


def build_program(stop_after=None, dbg=()):
    KSC = int(_os.environ.get("KDBG_SC", "16"))
    KSKIP = set(_os.environ.get("KDBG_SKIP", "").split(","))
    """stop_after in {None,'P1k','P1v','P2','A','O','F1'}; dbg: iterable of
    scratch names to expose as ExternalOutput."""
    nc = bass.Bass("TRN2", target_bir_lowering=False)

    def kind_of(name):
        return "ExternalOutput" if name in dbg else "Internal"

    x_ownT = nc.dram_tensor("x_ownT", [D, NT], F32, kind="ExternalInput").ap()
    wq = nc.dram_tensor("wq", [D, D], F32, kind="ExternalInput").ap()
    wk = nc.dram_tensor("wk", [D, D], F32, kind="ExternalInput").ap()
    wv = nc.dram_tensor("wv", [D, D], F32, kind="ExternalInput").ap()
    wo = nc.dram_tensor("wo", [D, D], F32, kind="ExternalInput").ap()
    wgt = nc.dram_tensor("wgt", [D, HID], F32, kind="ExternalInput").ap()
    wut = nc.dram_tensor("wut", [D, HID], F32, kind="ExternalInput").ap()
    wdt = nc.dram_tensor("wdt", [HID, D], F32, kind="ExternalInput").ap()
    cidx = nc.dram_tensor("cidx", [1, 1], I32, kind="ExternalInput").ap()
    out_fm = nc.dram_tensor("out_fm", [D, NT], F32, kind="ExternalOutput").ap()
    k_own = nc.dram_tensor("k_own", [D, NT], BF16, kind="Internal")
    bs_own = nc.dram_tensor("bs_own", [128, 256], F32, kind="Internal")
    bs_ag = nc.dram_tensor("bs_ag", [512, 256], F32, kind="Internal")
    ag_dbg = {}
    for nm, src_shape, dt in (("k_ag", [4 * D, NT], BF16), ("v_ag", [4 * NT, D], BF16),
                              ("bs_ag", [512, 256], F32), ("v_own", [NT, D], BF16)):
        if nm in dbg:
            ag_dbg[nm] = nc.dram_tensor(nm + "_dbg", src_shape, dt, kind="ExternalOutput")
    v_own = nc.dram_tensor("v_own", [NT, D], BF16, kind="Internal")
    k_ag = nc.dram_tensor("k_ag", [4 * D, NT], BF16, kind="Internal")
    v_ag = nc.dram_tensor("v_ag", [4 * NT, D], BF16, kind="Internal")
    q_scr = nc.dram_tensor("q_scr", [H, 128, NT], BF16, kind=kind_of("q_scr")).ap()
    a_scr = nc.dram_tensor("a_scr", [H, 128, NT], BF16, kind=kind_of("a_scr")).ap()
    y1f_scr = nc.dram_tensor("y1f_scr", [NDK, 128, NT], F32, kind=kind_of("y1f_scr")).ap()
    y1b_scr = nc.dram_tensor("y1b_scr", [NDK, 128, NT], BF16, kind=kind_of("y1b_scr")).ap()
    hid_scr = nc.dram_tensor("hid_scr", [HID // 128, 128, NT], BF16, kind=kind_of("hid_scr")).ap()
    dn_scr = nc.dram_tensor("dn_scr", [NDK, 128, NT], F32, kind=kind_of("dn_scr")).ap()
    r2_scr = nc.dram_tensor("r2_scr", [NDK, 128, NT], F32, kind=kind_of("r2_scr")).ap()

    fw = FW(nc)
    from contextlib import ExitStack

    with ExitStack() as outer:
        c_sb = outer.enter_context(nc.sbuf_tensor("c_sb", [1, 1], I32))
        ones_f = outer.enter_context(nc.sbuf_tensor("ones_f", [128, 1], F32))
        ones_b = outer.enter_context(nc.sbuf_tensor("ones_b", [128, 1], BF16))
        onesr_f = outer.enter_context(nc.sbuf_tensor("onesr_f", [1, 128], F32))
        masks = outer.enter_context(nc.sbuf_tensor("masks", [128, 4 * 512], F32))
        iotat = outer.enter_context(nc.sbuf_tensor("iotat", [128, 512], F32))
        sufA = outer.enter_context(nc.sbuf_tensor("sufA", [128, 65 * NDK], F32))
        sufB = outer.enter_context(nc.sbuf_tensor("sufB", [128, 65 * NDK], F32))
        sufXb = outer.enter_context(nc.sbuf_tensor("sufXb", [128, NDK * 65], BF16))
        sufV = outer.enter_context(nc.sbuf_tensor("sufV", [128, H * 65], F32))

        fw.op("sync", lambda: nc.sync.dma_start(c_sb[:, :], cidx),
              writes=["c_sb"], dma=True)
        if "init" not in KSKIP:
            fw.op("vector", lambda: nc.vector.memset(ones_f[:, :], 1.0), writes=["ones_f"])
            fw.op("vector", lambda: nc.vector.memset(ones_b[:, :], 1.0), writes=["ones_b"])
            fw.op("vector", lambda: nc.vector.memset(onesr_f[:, :], 1.0), writes=["onesr_f"])
            fw.op("scalar", lambda: load_act_set(nc, SET_LN_EXP), inc=False)
        for k in range(4 if "init" not in KSKIP else 0):
            fw.op("gpsimd",
                  lambda k=k: nc.gpsimd.iota(
                      iotat[:, :], [[1, 512]], base=-(128 * k),
                      channel_multiplier=-1,
                      allow_small_or_imprecise_dtypes=True),
                  writes=["iotat"])
            fw.op("vector",
                  lambda k=k: nc.vector.tensor_single_scalar(
                      masks[:, 512 * k:512 * (k + 1)], iotat[:, :], 0.0, ALU.is_ge),
                  reads=["iotat"], writes=["masks"])

        with ExitStack() as st:
            wk_sb = st.enter_context(nc.sbuf_tensor("wk_sb", [128, NDK * D], BF16))
            xc = [st.enter_context(nc.sbuf_tensor(f"xc{i}", [128, NDK * 512], BF16))
                  for i in range(2)]
            kst = [st.enter_context(nc.sbuf_tensor(f"kst{i}", [128, 512], BF16))
                   for i in range(2)]
            kps = st.enter_context(nc.psum_tensor("kps", [128, 4 * 512], F32))

            fw.op("gpsimd", lambda: nc.gpsimd.dma_start(
                wk_sb[:, :].rearrange("p (a n) -> p a n", a=NDK),
                wk.rearrange("(a p) n -> p a n", p=128)),
                writes=["wk_sb"], dma=True)

            for sc in range(4):
                xb = f"xc{sc % 2}"
                fw.op("gpsimd", lambda sc=sc: nc.gpsimd.dma_start(
                    xc[sc % 2][:, :].rearrange("p (a s) -> p a s", a=NDK),
                    x_ownT.rearrange("(a p) s -> p a s", p=128)[:, :, 512 * sc:512 * (sc + 1)]),
                    writes=[xb], dma=True)
                for dcol in range(16):
                    pb = f"kps{dcol % 4}"
                    for dk in range(NDK):
                        fw.op("tensor", lambda sc=sc, dcol=dcol, dk=dk: nc.tensor.matmul(
                            kps[:, 512 * (dcol % 4):512 * (dcol % 4 + 1)],
                            wk_sb[:, D * dk + 128 * dcol: D * dk + 128 * (dcol + 1)],
                            xc[sc % 2][:, 512 * dk:512 * (dk + 1)],
                            start=(dk == 0), stop=(dk == NDK - 1)),
                            reads=["wk_sb", xb], writes=[pb])
                    ks = f"kst{dcol % 2}"
                    fw.op("scalar", lambda dcol=dcol: nc.scalar.copy(
                        kst[dcol % 2][:, :], kps[:, 512 * (dcol % 4):512 * (dcol % 4 + 1)]),
                        reads=[pb, ks], writes=[ks])
                    fw.op("sync", lambda sc=sc, dcol=dcol: nc.sync.dma_start(
                        k_own[128 * dcol:128 * (dcol + 1), 512 * sc:512 * (sc + 1)],
                        kst[dcol % 2][:, :]),
                        reads=[ks], writes=["k_own"], dma=True)
                for sb in range(4):
                    lb = sc * 4 + sb
                    fw.op("vector", lambda sc=sc, sb=sb, lb=lb: nc.vector.reduce_sum(
                        sufA[:, 16 * lb:16 * (lb + 1)].rearrange("p (a o) -> p a o", o=1),
                        xc[sc % 2][:, :].rearrange("p (a s) -> p a s", a=NDK)[:, :, 128 * sb:128 * (sb + 1)],
                        axis=AX.X),
                        reads=[xb], writes=["sufA"])
            fw.op("sync", lambda: nc.sync.dma_start(bs_own.ap(), sufA[:, 0:256]),
                  reads=["sufA"], writes=["bs_own"], dma=True)

        fw.barrier()
        if stop_after == "P1k":
            return nc

        with ExitStack() as st:
            wv_sb = st.enter_context(nc.sbuf_tensor("wv_sb", [128, NDK * D], BF16))
            xc = [st.enter_context(nc.sbuf_tensor(f"xv{i}", [128, NDK * 512], BF16))
                  for i in range(2)]
            vst = [st.enter_context(nc.sbuf_tensor(f"vst{i}", [128, D], BF16))
                   for i in range(2)]
            vps = [st.enter_context(nc.psum_tensor(f"vps{i}", [128, D], F32))
                   for i in range(2)]
            svps = vps[0]

            fw.op("gpsimd", lambda: nc.gpsimd.dma_start(
                wv_sb[:, :].rearrange("p (a n) -> p a n", a=NDK),
                wv.rearrange("(a p) n -> p a n", p=128)),
                writes=["wv_sb"], dma=True)

            for sc in range(4):
                xb = f"xv{sc % 2}"
                fw.op("gpsimd", lambda sc=sc: nc.gpsimd.dma_start(
                    xc[sc % 2][:, :].rearrange("p (a s) -> p a s", a=NDK),
                    x_ownT.rearrange("(a p) s -> p a s", p=128)[:, :, 512 * sc:512 * (sc + 1)]),
                    writes=[xb], dma=True)
                for sblk in range(4):
                    sb = sc * 4 + sblk
                    pb = f"vps{sb % 2}"
                    for dcb in range(4):
                        for dk in range(NDK):
                            fw.op("tensor", lambda sc=sc, sblk=sblk, sb=sb, dcb=dcb, dk=dk:
                                  nc.tensor.matmul(
                                      vps[sb % 2][:, 512 * dcb:512 * (dcb + 1)],
                                      xc[sc % 2][:, 512 * dk + 128 * sblk:512 * dk + 128 * (sblk + 1)],
                                      wv_sb[:, D * dk + 512 * dcb: D * dk + 512 * (dcb + 1)],
                                      start=(dk == 0), stop=(dk == NDK - 1)),
                                  reads=["wv_sb", xb], writes=[pb])
                    vs = f"vst{sb % 2}"
                    fw.op("scalar", lambda sb=sb: nc.scalar.copy(
                        vst[sb % 2][:, :], vps[sb % 2][:, :]),
                        reads=[pb, vs], writes=[vs])
                    fw.op("sync", lambda sb=sb: nc.sync.dma_start(
                        v_own[128 * sb:128 * (sb + 1), :], vst[sb % 2][:, :]),
                        reads=[vs], writes=["v_own"], dma=True)

            wq_sb = st.enter_context(nc.sbuf_tensor("wq_sb", [128, NDK * D], BF16))
            qst = [st.enter_context(nc.sbuf_tensor(f"qst{i}", [128, 512], BF16))
                   for i in range(2)]
            qps = vps[0]
            fw.op("gpsimd", lambda: nc.gpsimd.dma_start(
                wq_sb[:, :].rearrange("p (a n) -> p a n", a=NDK),
                wq.rearrange("(a p) n -> p a n", p=128)),
                writes=["wq_sb"], dma=True)
            for sc in range(4):
                xb = f"xv{sc % 2}"
                fw.op("gpsimd", lambda sc=sc: nc.gpsimd.dma_start(
                    xc[sc % 2][:, :].rearrange("p (a s) -> p a s", a=NDK),
                    x_ownT.rearrange("(a p) s -> p a s", p=128)[:, :, 512 * sc:512 * (sc + 1)]),
                    writes=[xb], dma=True)
                for dcol in range(16):
                    pb = f"qp{dcol % 4}"
                    extra = ["vps0", "vps1"] if (sc == 0 and dcol < 4) else []
                    for dk in range(NDK):
                        fw.op("tensor", lambda sc=sc, dcol=dcol, dk=dk: nc.tensor.matmul(
                            qps[:, 512 * (dcol % 4):512 * (dcol % 4 + 1)],
                            wq_sb[:, D * dk + 128 * dcol: D * dk + 128 * (dcol + 1)],
                            xc[sc % 2][:, 512 * dk:512 * (dk + 1)],
                            start=(dk == 0), stop=(dk == NDK - 1)),
                            reads=["wq_sb", xb], writes=[pb] + extra)
                    qs = f"qst{dcol % 2}"
                    fw.op("scalar", lambda dcol=dcol: nc.scalar.copy(
                        qst[dcol % 2][:, :], qps[:, 512 * (dcol % 4):512 * (dcol % 4 + 1)]),
                        reads=[f"qp{dcol % 4}", qs], writes=[qs])
                    fw.op("sync", lambda sc=sc, dcol=dcol: nc.sync.dma_start(
                        q_scr[dcol, :, 512 * sc:512 * (sc + 1)], qst[dcol % 2][:, :]),
                        reads=[qs], writes=[f"q_scr{dcol}"], dma=True)
            fw.op("gpsimd", lambda: nc.gpsimd.collective_compute(
                "AllGather", ALU.bypass,
                replica_groups=[[0, 1, 2, 3], [4, 5, 6, 7]],
                ins=[k_own.ap().opt()], outs=[k_ag.ap().opt()]),
                reads=["k_own"], writes=["k_ag"])
            fw.op("gpsimd", lambda: nc.gpsimd.collective_compute(
                "AllGather", ALU.bypass,
                replica_groups=[[0, 1, 2, 3], [4, 5, 6, 7]],
                ins=[bs_own.ap().opt()], outs=[bs_ag.ap().opt()]),
                reads=["bs_own"], writes=["bs_ag"])
            fw.op("gpsimd", lambda: nc.gpsimd.collective_compute(
                "AllGather", ALU.bypass,
                replica_groups=[[0, 1, 2, 3], [4, 5, 6, 7]],
                ins=[v_own.ap().opt()], outs=[v_ag.ap().opt()]),
                reads=["v_own"], writes=["v_ag"])

            fw.barrier()
            bsv = bs_ag.ap().rearrange("(r p) n -> r p n", p=128)
            for g in range(8):
                r, half = (g, 0) if g < 4 else (7 - g, 1)
                fw.op("sync", lambda g=g, r=r, half=half: nc.sync.dma_start(
                    sufA[:, 16 * 8 * g:16 * 8 * (g + 1)],
                    bsv[r, :, 128 * half:128 * (half + 1)]),
                    reads=["bs_ag"], writes=["sufA"], dma=True)
            fw.op("vector", lambda: nc.vector.memset(sufA[:, 64 * 16:65 * 16], 0.0),
                  writes=["sufA"])
            cur, oth = sufA, sufB
            curb, othb = "sufA", "sufB"
            for lvl in range(6):
                sh = 1 << lvl
                w = (65 - sh) * 16
                fw.op("vector", lambda cur=cur, oth=oth, sh=sh, w=w: nc.vector.tensor_add(
                    oth[:, 0:w], cur[:, 0:w], cur[:, 16 * sh:16 * sh + w]),
                    reads=[curb], writes=[othb])
                fw.op("vector", lambda cur=cur, oth=oth, w=w: nc.vector.tensor_copy(
                    oth[:, w:65 * 16], cur[:, w:65 * 16]),
                    reads=[curb], writes=[othb])
                cur, oth = oth, cur
                curb, othb = othb, curb
            fw.op("scalar", lambda cur=cur: nc.scalar.copy(
                sufXb[:, :].rearrange("p (a j) -> p a j", a=NDK),
                cur[:, :].rearrange("p (j a) -> p a j", a=NDK)),
                reads=[curb], writes=["sufXb"])
            for h in range(H):
                for dk in range(NDK):
                    fw.op("tensor", lambda h=h, dk=dk: nc.tensor.matmul(
                        svps[:, 0:65],
                        wv_sb[:, D * dk + 128 * h: D * dk + 128 * (h + 1)],
                        sufXb[:, 65 * dk:65 * (dk + 1)],
                        start=(dk == 0), stop=(dk == NDK - 1)),
                        reads=["wv_sb", "sufXb"], writes=["vps0", "qp0"])
                fw.op("scalar", lambda h=h: nc.scalar.copy(
                    sufV[:, 65 * h:65 * (h + 1)], svps[:, 0:65]),
                    reads=["vps0", "sufV"], writes=["sufV"])
        fw.barrier()
        for nm, dst in ag_dbg.items():
            src = {"k_ag": k_ag, "v_ag": v_ag, "bs_ag": bs_ag, "v_own": v_own}[nm]
            fw.op("sync", lambda src=src, dst=dst: nc.sync.dma_start(
                dst[:, :], src[:, :]), reads=[nm], dma=True)
        if stop_after in ("P1v", "P2"):
            return nc

        with ExitStack() as st:
            qh = [st.enter_context(nc.sbuf_tensor(f"qh{i}", [128, NT], BF16))
                  for i in range(2)]
            kh = [st.enter_context(nc.sbuf_tensor(f"kh{i}", [128, T], BF16))
                  for i in range(2)]
            vh = [st.enter_context(nc.sbuf_tensor(f"vh{i}", [128, T], BF16))
                  for i in range(2)]
            pp = [st.enter_context(nc.sbuf_tensor(f"pp{i}", [128, 1024], BF16))
                  for i in range(4)]
            ptmp = [st.enter_context(nc.sbuf_tensor(f"ptmp{i}", [128, 1024], BF16))
                    for i in range(2)]
            pacc_v = st.enter_context(nc.sbuf_tensor("pacc_v", [128, 1024], F32))
            zc_sb = st.enter_context(nc.sbuf_tensor("zc_sb", [1, 512], F32))
            zi_sb = st.enter_context(nc.sbuf_tensor("zi_sb", [1, 512], F32))
            zb_sb = st.enter_context(nc.sbuf_tensor("zb_sb", [128, 512], F32))
            ast = [st.enter_context(nc.sbuf_tensor(f"ast{i}", [128, 512], BF16))
                   for i in range(2)]
            nps = st.enter_context(nc.psum_tensor("nps", [128, 2 * 512], F32))
            stp = [st.enter_context(nc.psum_tensor(f"stp{i}", [128, 1024], F32))
                   for i in range(2)]
            zps = st.enter_context(nc.psum_tensor("zps", [128, 512], F32))
            zbp = st.enter_context(nc.psum_tensor("zbp", [128, 512], F32))

            engines = [nc.tensor, nc.scalar, nc.vector, nc.gpsimd, nc.sync]
            idxs = []
            for eng, ename in zip(engines, ["tensor", "scalar", "vector", "gpsimd", "sync"]):
                fw._waits(ename, {"DQ": 16})
                r = eng.alloc_register()
                eng.reg_load(r, c_sb[0:1, 0:1])
                idxs.append(RuntimeValue(r, min_val=0, max_val=3))

            first_body = True
            for c in nc.Switch(engines=engines, index=idxs, n=4):
                if first_body:
                    fw.enter_body()
                    first_body = False
                else:
                    fw.body_reset()
                emit_attention(nc, fw, c, qh, kh, vh, pp, ptmp, pacc_v, zc_sb, zi_sb,
                               zb_sb, ast, nps, stp, zps, zbp,
                               masks, sufV, ones_f, onesr_f,
                               q_scr, k_ag, v_ag, a_scr)
                fw.exit_body()
            fw.finish_switch()
            for h in range(H):
                fw.wrote(f"a_scr{h}", "gpsimd")
        fw.barrier()
        if stop_after == "A":
            return nc

        with ExitStack() as st:
            wo_sb = st.enter_context(nc.sbuf_tensor("wo_sb", [128, NDK * D], BF16))
            atb = [st.enter_context(nc.sbuf_tensor(f"atb{i}", [128, NDK * 512], BF16))
                   for i in range(2)]
            xres = [st.enter_context(nc.sbuf_tensor(f"xres{i}", [128, 512], F32))
                    for i in range(4)]
            r1 = st.enter_context(nc.sbuf_tensor("r1", [128, NDK * 512], F32))
            r1b = [st.enter_context(nc.sbuf_tensor(f"r1b{i}", [128, 512], BF16))
                   for i in range(2)]
            sqb = [st.enter_context(nc.sbuf_tensor(f"sqb{i}", [128, 512], BF16))
                   for i in range(2)]
            trow = st.enter_context(nc.sbuf_tensor("trow", [1, 512], F32))
            mrow = st.enter_context(nc.sbuf_tensor("mrow", [1, 512], F32))
            vrow = st.enter_context(nc.sbuf_tensor("vrow", [1, 512], F32))
            irow = st.enter_context(nc.sbuf_tensor("irow", [1, 512], F32))
            brow = st.enter_context(nc.sbuf_tensor("brow", [1, 512], F32))
            y1fs = [st.enter_context(nc.sbuf_tensor(f"y1fs{i}", [128, 512], F32))
                    for i in range(2)]
            abc_sb = st.enter_context(nc.sbuf_tensor("abc_sb", [128, 512], F32))
            bbc_sb = st.enter_context(nc.sbuf_tensor("bbc_sb", [128, 512], F32))
            y1bs = [st.enter_context(nc.sbuf_tensor(f"y1bs{i}", [128, 512], BF16))
                    for i in range(2)]
            ops_ = st.enter_context(nc.psum_tensor("ops", [128, 4 * 512], F32))
            sumps = st.enter_context(nc.psum_tensor("sumps", [128, 512], F32))
            sqps = st.enter_context(nc.psum_tensor("sqps", [128, 512], F32))
            abc = st.enter_context(nc.psum_tensor("abc", [128, 512], F32))
            bbc = st.enter_context(nc.psum_tensor("bbc", [128, 512], F32))

            fw.op("gpsimd", lambda: nc.gpsimd.dma_start(
                wo_sb[:, :].rearrange("p (a n) -> p a n", a=NDK),
                wo.rearrange("(a p) n -> p a n", p=128)),
                writes=["wo_sb"], dma=True)

            for tb in range(4):
                ab = f"atb{tb % 2}"
                fw.op("sync", lambda tb=tb: nc.sync.dma_start(
                    atb[tb % 2][:, :].rearrange("p (a s) -> p a s", a=NDK),
                    a_scr.rearrange("h p t -> p h t")[:, :, 512 * tb:512 * (tb + 1)]),
                    reads=[f"a_scr{h}" for h in range(H)], writes=[ab], dma=True)
                for dcol in range(16):
                    pb = f"ops{dcol % 4}"
                    xr = f"xres{dcol % 4}"
                    fw.op("sync", lambda tb=tb, dcol=dcol: nc.sync.dma_start(
                        xres[dcol % 4][:, :],
                        x_ownT[128 * dcol:128 * (dcol + 1), 512 * tb:512 * (tb + 1)]),
                        writes=[xr], dma=True)
                    for dk in range(NDK):
                        fw.op("tensor", lambda tb=tb, dcol=dcol, dk=dk: nc.tensor.matmul(
                            ops_[:, 512 * (dcol % 4):512 * (dcol % 4 + 1)],
                            wo_sb[:, D * dk + 128 * dcol: D * dk + 128 * (dcol + 1)],
                            atb[tb % 2][:, 512 * dk:512 * (dk + 1)],
                            start=(dk == 0), stop=(dk == NDK - 1)),
                            reads=["wo_sb", ab], writes=[pb])
                    fw.op("vector", lambda dcol=dcol: nc.vector.tensor_add(
                        r1[:, 512 * dcol:512 * (dcol + 1)],
                        xres[dcol % 4][:, :],
                        ops_[:, 512 * (dcol % 4):512 * (dcol % 4 + 1)]),
                        reads=[xr, pb], writes=["r1"])
                    fw.op("scalar", lambda dcol=dcol: nc.scalar.copy(
                        r1b[dcol % 2][:, :], r1[:, 512 * dcol:512 * (dcol + 1)]),
                        reads=["r1"], writes=[f"r1b{dcol % 2}"])
                    fw.op("scalar", lambda dcol=dcol: nc.scalar.activation(
                        sqb[dcol % 2][:, :], r1[:, 512 * dcol:512 * (dcol + 1)], AF.Square),
                        reads=["r1"], writes=[f"sqb{dcol % 2}"])
                    fw.op("tensor", lambda dcol=dcol: nc.tensor.matmul(
                        sumps[0:1, :], ones_b[:, :], r1b[dcol % 2][:, :],
                        start=(dcol == 0), stop=(dcol == 15)),
                        reads=[f"r1b{dcol % 2}", "ones_b"], writes=["sumps"])
                    fw.op("tensor", lambda dcol=dcol: nc.tensor.matmul(
                        sqps[0:1, :], ones_b[:, :], sqb[dcol % 2][:, :],
                        start=(dcol == 0), stop=(dcol == 15)),
                        reads=[f"sqb{dcol % 2}", "ones_b"], writes=["sqps"])
                fw.op("scalar", lambda: nc.scalar.activation(mrow[:, :], sumps[0:1, :], AF.Identity, scale=1.0 / D),
                      reads=["sumps"], writes=["mrow"])
                fw.op("scalar", lambda: nc.scalar.activation(vrow[:, :], sqps[0:1, :], AF.Identity, scale=1.0 / D),
                      reads=["sqps"], writes=["vrow"])
                fw.op("vector", lambda: nc.vector.tensor_mul(trow[:, :], mrow[:, :], mrow[:, :]),
                      reads=["mrow"], writes=["trow"])
                fw.op("vector", lambda: nc.vector.tensor_sub(vrow[:, :], vrow[:, :], trow[:, :]),
                      reads=["vrow", "trow"], writes=["vrow"])
                fw.op("vector", lambda: nc.vector.tensor_scalar(vrow[:, :], vrow[:, :], EPS, None, ALU.add),
                      reads=["vrow"], writes=["vrow"])
                fw.op("scalar", lambda: nc.scalar.activation(trow[:, :], vrow[:, :], AF.Ln),
                      reads=["vrow"], writes=["trow"])
                fw.op("scalar", lambda: nc.scalar.activation(irow[:, :], trow[:, :], AF.Exp, scale=-0.5),
                      reads=["trow"], writes=["irow"])
                fw.op("vector", lambda: nc.vector.tensor_mul(brow[:, :], mrow[:, :], irow[:, :]),
                      reads=["mrow", "irow"], writes=["brow"])
                fw.op("vector", lambda: nc.vector.tensor_scalar_mul(brow[:, :], brow[:, :], -1.0),
                      reads=["brow"], writes=["brow"])
                fw.op("tensor", lambda: nc.tensor.matmul(
                    abc[:, :], onesr_f[:, :], irow[:, :], start=True, stop=True),
                    reads=["irow", "onesr_f"], writes=["abc"])
                fw.op("tensor", lambda: nc.tensor.matmul(
                    bbc[:, :], onesr_f[:, :], brow[:, :], start=True, stop=True),
                    reads=["brow", "onesr_f"], writes=["bbc"])
                fw.op("scalar", lambda: nc.scalar.copy(abc_sb[:, :], abc[:, :]),
                      reads=["abc", "abc_sb"], writes=["abc_sb"])
                fw.op("scalar", lambda: nc.scalar.copy(bbc_sb[:, :], bbc[:, :]),
                      reads=["bbc", "bbc_sb"], writes=["bbc_sb"])
                for dcol in range(16):
                    yf = f"y1fs{dcol % 2}"
                    yb = f"y1bs{dcol % 2}"
                    fw.op("vector", lambda dcol=dcol: nc.vector.tensor_mul(
                        y1fs[dcol % 2][:, :], r1[:, 512 * dcol:512 * (dcol + 1)], abc_sb[:, :]),
                        reads=["r1", "abc_sb"], writes=[yf])
                    fw.op("vector", lambda dcol=dcol: nc.vector.tensor_add(
                        y1fs[dcol % 2][:, :], y1fs[dcol % 2][:, :], bbc_sb[:, :]),
                        reads=[yf, "bbc_sb"], writes=[yf])
                    fw.op("scalar", lambda dcol=dcol: nc.scalar.copy(
                        y1bs[dcol % 2][:, :], y1fs[dcol % 2][:, :]),
                        reads=[yf], writes=[yb])
                    fw.op("gpsimd", lambda tb=tb, dcol=dcol: nc.gpsimd.dma_start(
                        y1f_scr[dcol, :, 512 * tb:512 * (tb + 1)], y1fs[dcol % 2][:, :]),
                        reads=[yf], writes=[f"y1f_scr{dcol}"], dma=True)
                    fw.op("gpsimd", lambda tb=tb, dcol=dcol: nc.gpsimd.dma_start(
                        y1b_scr[dcol, :, 512 * tb:512 * (tb + 1)], y1bs[dcol % 2][:, :]),
                        reads=[yb], writes=["y1b_scr"], dma=True)
        fw.barrier()
        if stop_after == "O":
            return nc

        with ExitStack() as st:
            y1 = st.enter_context(nc.sbuf_tensor("y1", [128, NDK * NT], BF16))
            wgu = [st.enter_context(nc.sbuf_tensor(f"wgu{i}", [128, 2 * NDK * 128], BF16))
                   for i in range(2)]
            sg = [st.enter_context(nc.sbuf_tensor(f"sg{i}", [128, 1024], F32))
                  for i in range(2)]
            hst = [st.enter_context(nc.sbuf_tensor(f"hst{i}", [128, 1024], BF16))
                   for i in range(2)]
            gps = [st.enter_context(nc.psum_tensor(f"gps{i}", [128, 1024], F32))
                   for i in range(2)]
            ups = [st.enter_context(nc.psum_tensor(f"ups{i}", [128, 1024], F32))
                   for i in range(2)]

            fw.op("scalar", lambda: load_act_set(nc, SET_SILU), inc=False)
            fw.op("sync", lambda: nc.sync.dma_start(
                y1[:, :].rearrange("p (a t) -> p a t", a=NDK),
                y1b_scr.rearrange("a p t -> p a t")),
                reads=["y1b_scr"], writes=["y1"], dma=True)

            for ht in range(HID // 128):
                wb = f"wgu{ht % 2}"
                fw.op("gpsimd", lambda ht=ht: nc.gpsimd.dma_start(
                    wgu[ht % 2][:, 0:NDK * 128].rearrange("p (a n) -> p a n", a=NDK),
                    wgt.rearrange("(a p) n -> p a n", p=128)[:, :, 128 * ht:128 * (ht + 1)]),
                    writes=[wb], dma=True)
                fw.op("gpsimd", lambda ht=ht: nc.gpsimd.dma_start(
                    wgu[ht % 2][:, NDK * 128:2 * NDK * 128].rearrange("p (a n) -> p a n", a=NDK),
                    wut.rearrange("(a p) n -> p a n", p=128)[:, :, 128 * ht:128 * (ht + 1)]),
                    writes=[wb], dma=True)
                for half in range(2):
                    gb, ub = f"gps{half}", f"ups{half}"
                    for dk in range(NDK):
                        for q in range(2):
                            fw.op("tensor", lambda ht=ht, half=half, dk=dk, q=q: nc.tensor.matmul(
                                gps[half][:, 512 * q:512 * (q + 1)],
                                wgu[ht % 2][:, 128 * dk:128 * (dk + 1)],
                                y1[:, NT * dk + 1024 * half + 512 * q: NT * dk + 1024 * half + 512 * (q + 1)],
                                start=(dk == 0), stop=(dk == NDK - 1)),
                                reads=[wb, "y1"], writes=[gb])
                        for q in range(2):
                            fw.op("tensor", lambda ht=ht, half=half, dk=dk, q=q: nc.tensor.matmul(
                                ups[half][:, 512 * q:512 * (q + 1)],
                                wgu[ht % 2][:, NDK * 128 + 128 * dk:NDK * 128 + 128 * (dk + 1)],
                                y1[:, NT * dk + 1024 * half + 512 * q: NT * dk + 1024 * half + 512 * (q + 1)],
                                start=(dk == 0), stop=(dk == NDK - 1)),
                                reads=[wb, "y1"], writes=[ub])
                    sb_ = f"sg{half}"
                    hb = f"hst{half}"
                    fw.op("scalar", lambda half=half: nc.scalar.activation(
                        sg[half][:, :], gps[half][:, :], AF.Silu),
                        reads=[gb], writes=[sb_])
                    fw.op("vector", lambda half=half: nc.vector.tensor_mul(
                        hst[half][:, :], sg[half][:, :], ups[half][:, :]),
                        reads=[sb_, ub], writes=[hb])
                    fw.op("sync", lambda ht=ht, half=half: nc.sync.dma_start(
                        hid_scr[ht, :, 1024 * half:1024 * (half + 1)], hst[half][:, :]),
                        reads=[hb], writes=["hid_scr"], dma=True)
        fw.barrier()
        if stop_after == "F1":
            return nc

        with ExitStack() as st:
            hidh = st.enter_context(nc.sbuf_tensor("hidh", [128, 32 * 1024], BF16))
            wd = [st.enter_context(nc.sbuf_tensor(f"wd{i}", [128, 32 * 128], BF16))
                  for i in range(2)]
            acc = st.enter_context(nc.sbuf_tensor("acc", [128, NDK * 1024], F32))
            y1r = [st.enter_context(nc.sbuf_tensor(f"y1r{i}", [128, 1024], F32))
                   for i in range(2)]
            r2b = [st.enter_context(nc.sbuf_tensor(f"r2b{i}", [128, 1024], BF16))
                   for i in range(2)]
            sq2 = [st.enter_context(nc.sbuf_tensor(f"sq2{i}", [128, 1024], BF16))
                   for i in range(2)]
            trow = st.enter_context(nc.sbuf_tensor("trow2", [1, 1024], F32))
            mrow = st.enter_context(nc.sbuf_tensor("mrow2", [1, 1024], F32))
            vrow = st.enter_context(nc.sbuf_tensor("vrow2", [1, 1024], F32))
            irow = st.enter_context(nc.sbuf_tensor("irow2", [1, 1024], F32))
            brow = st.enter_context(nc.sbuf_tensor("brow2", [1, 1024], F32))
            dps = [st.enter_context(nc.psum_tensor(f"dps{i}", [128, 1024], F32))
                   for i in range(2)]
            sum2 = st.enter_context(nc.psum_tensor("sum2", [128, 1024], F32))
            sq2p = st.enter_context(nc.psum_tensor("sq2p", [128, 1024], F32))

            fw.op("scalar", lambda: load_act_set(nc, SET_LN_EXP), inc=False)

            for tp in range(2):
                for half in range(2):
                    fw.op("sync", lambda tp=tp, half=half: nc.sync.dma_start(
                        hidh[:, :].rearrange("p (a t) -> p a t", a=32),
                        hid_scr.rearrange("a p t -> p a t")[:, 32 * half:32 * (half + 1),
                                                           1024 * tp:1024 * (tp + 1)]),
                        reads=["hid_scr"], writes=["hidh"], dma=True)
                    for dcol in range(16):
                        wb = f"wd{dcol % 2}"
                        fw.op("gpsimd", lambda tp=tp, half=half, dcol=dcol: nc.gpsimd.dma_start(
                            wd[dcol % 2][:, :].rearrange("p (a n) -> p a n", a=32),
                            wdt[4096 * half:4096 * (half + 1),
                                128 * dcol:128 * (dcol + 1)].rearrange("(a p) n -> p a n", p=128)),
                            writes=[wb], dma=True)
                        pb = f"dps{dcol % 2}"
                        for hb in range(32):
                            for q in range(2):
                                fw.op("tensor", lambda half=half, dcol=dcol, hb=hb, q=q:
                                      nc.tensor.matmul(
                                          dps[dcol % 2][:, 512 * q:512 * (q + 1)],
                                          wd[dcol % 2][:, 128 * hb:128 * (hb + 1)],
                                          hidh[:, 1024 * hb + 512 * q:1024 * hb + 512 * (q + 1)],
                                          start=(hb == 0), stop=(hb == 31)),
                                      reads=[wb, "hidh"], writes=[pb])
                        if half == 0:
                            fw.op("vector", lambda dcol=dcol: nc.vector.tensor_copy(
                                acc[:, 1024 * dcol:1024 * (dcol + 1)], dps[dcol % 2][:, :]),
                                reads=[pb], writes=["acc"])
                        else:
                            fw.op("vector", lambda dcol=dcol: nc.vector.tensor_add(
                                acc[:, 1024 * dcol:1024 * (dcol + 1)],
                                acc[:, 1024 * dcol:1024 * (dcol + 1)], dps[dcol % 2][:, :]),
                                reads=[pb, "acc"], writes=["acc"])
                            if "dn_scr" in dbg:
                                fw.op("sync", lambda tp=tp, dcol=dcol: nc.sync.dma_start(
                                    dn_scr[dcol, :, 1024 * tp:1024 * (tp + 1)],
                                    acc[:, 1024 * dcol:1024 * (dcol + 1)]),
                                    reads=["acc"], writes=["dn_scr"], dma=True)
                for dcol in range(16):
                    yb = f"y1r{dcol % 2}"
                    fw.op("sync", lambda tp=tp, dcol=dcol: nc.sync.dma_start(
                        y1r[dcol % 2][:, :],
                        y1f_scr[dcol, :, 1024 * tp:1024 * (tp + 1)]),
                        reads=[f"y1f_scr{dcol}"], writes=[yb], dma=True)
                    fw.op("vector", lambda dcol=dcol: nc.vector.tensor_add(
                        acc[:, 1024 * dcol:1024 * (dcol + 1)],
                        acc[:, 1024 * dcol:1024 * (dcol + 1)], y1r[dcol % 2][:, :]),
                        reads=["acc", yb], writes=["acc"])
                    if "r2_scr" in dbg:
                        fw.op("sync", lambda tp=tp, dcol=dcol: nc.sync.dma_start(
                            r2_scr[dcol, :, 1024 * tp:1024 * (tp + 1)],
                            acc[:, 1024 * dcol:1024 * (dcol + 1)]),
                            reads=["acc"], writes=["r2_scr"], dma=True)
                    fw.op("scalar", lambda dcol=dcol: nc.scalar.copy(
                        r2b[dcol % 2][:, :], acc[:, 1024 * dcol:1024 * (dcol + 1)]),
                        reads=["acc"], writes=[f"r2b{dcol % 2}"])
                    fw.op("scalar", lambda dcol=dcol: nc.scalar.activation(
                        sq2[dcol % 2][:, :], acc[:, 1024 * dcol:1024 * (dcol + 1)], AF.Square),
                        reads=["acc"], writes=[f"sq2{dcol % 2}"])
                    for q in range(2):
                        fw.op("tensor", lambda dcol=dcol, q=q: nc.tensor.matmul(
                            sum2[0:1, 512 * q:512 * (q + 1)], ones_b[:, :],
                            r2b[dcol % 2][:, 512 * q:512 * (q + 1)],
                            start=(dcol == 0), stop=(dcol == 15)),
                            reads=[f"r2b{dcol % 2}", "ones_b"], writes=["sum2"])
                        fw.op("tensor", lambda dcol=dcol, q=q: nc.tensor.matmul(
                            sq2p[0:1, 512 * q:512 * (q + 1)], ones_b[:, :],
                            sq2[dcol % 2][:, 512 * q:512 * (q + 1)],
                            start=(dcol == 0), stop=(dcol == 15)),
                            reads=[f"sq2{dcol % 2}", "ones_b"], writes=["sq2p"])
                fw.op("scalar", lambda: nc.scalar.activation(mrow[:, :], sum2[0:1, :], AF.Identity, scale=1.0 / D),
                      reads=["sum2"], writes=["mrow2"])
                fw.op("scalar", lambda: nc.scalar.activation(vrow[:, :], sq2p[0:1, :], AF.Identity, scale=1.0 / D),
                      reads=["sq2p"], writes=["vrow2"])
                fw.op("vector", lambda: nc.vector.tensor_mul(trow[:, :], mrow[:, :], mrow[:, :]),
                      reads=["mrow2"], writes=["trow2"])
                fw.op("vector", lambda: nc.vector.tensor_sub(vrow[:, :], vrow[:, :], trow[:, :]),
                      reads=["vrow2", "trow2"], writes=["vrow2"])
                fw.op("vector", lambda: nc.vector.tensor_scalar(vrow[:, :], vrow[:, :], EPS, None, ALU.add),
                      reads=["vrow2"], writes=["vrow2"])
                fw.op("scalar", lambda: nc.scalar.activation(trow[:, :], vrow[:, :], AF.Ln),
                      reads=["vrow2"], writes=["trow2"])
                fw.op("scalar", lambda: nc.scalar.activation(irow[:, :], trow[:, :], AF.Exp, scale=-0.5),
                      reads=["trow2"], writes=["irow2"])
                fw.op("vector", lambda: nc.vector.tensor_mul(brow[:, :], mrow[:, :], irow[:, :]),
                      reads=["mrow2", "irow2"], writes=["brow2"])
                fw.op("vector", lambda: nc.vector.tensor_scalar_mul(brow[:, :], brow[:, :], -1.0),
                      reads=["brow2"], writes=["brow2"])
                for q in range(2):
                    fw.op("tensor", lambda q=q: nc.tensor.matmul(
                        dps[0][:, 512 * q:512 * (q + 1)], onesr_f[:, :],
                        irow[0:1, 512 * q:512 * (q + 1)], start=True, stop=True),
                        reads=["irow2"], writes=["dps0"])
                    fw.op("tensor", lambda q=q: nc.tensor.matmul(
                        dps[1][:, 512 * q:512 * (q + 1)], onesr_f[:, :],
                        brow[0:1, 512 * q:512 * (q + 1)], start=True, stop=True),
                        reads=["brow2"], writes=["dps1"])
                fw.op("scalar", lambda: nc.scalar.copy(y1r[0][:, :], dps[0][:, :]),
                      reads=["dps0", "y1r0"], writes=["y1r0"])
                fw.op("scalar", lambda: nc.scalar.copy(y1r[1][:, :], dps[1][:, :]),
                      reads=["dps1", "y1r1"], writes=["y1r1"])
                for dcol in range(16):
                    fw.op("vector", lambda dcol=dcol: nc.vector.tensor_mul(
                        acc[:, 1024 * dcol:1024 * (dcol + 1)],
                        acc[:, 1024 * dcol:1024 * (dcol + 1)], y1r[0][:, :]),
                        reads=["acc", "y1r0"], writes=["acc"])
                    fw.op("vector", lambda dcol=dcol: nc.vector.tensor_add(
                        acc[:, 1024 * dcol:1024 * (dcol + 1)],
                        acc[:, 1024 * dcol:1024 * (dcol + 1)], y1r[1][:, :]),
                        reads=["acc", "y1r1"], writes=["acc"])
                    fw.op("sync", lambda tp=tp, dcol=dcol: nc.sync.dma_start(
                        out_fm[128 * dcol:128 * (dcol + 1), 1024 * tp:1024 * (tp + 1)],
                        acc[:, 1024 * dcol:1024 * (dcol + 1)]),
                        reads=["acc"], writes=["out_fm"], dma=True)
        fw.barrier()
    return nc


def emit_attention(nc, fw, c, qh, kh, vh, pp, ptmp, pacc_v, zc_sb, zi_sb,
                   zb_sb, ast, nps, stp, zps, zbp, masks, sufV, ones_f, onesr_f,
                   q_scr, k_ag, v_ag, a_scr):
    ns = [8 * c + 4, 8 * c + 8, 60 - 8 * c, 64 - 8 * c]
    counts = [float(T - 128 * n) for n in ns]
    span_sb = ns[3]
    vchunk = span_sb // 8

    span_chunks = span_sb // 8
    kag = k_ag.ap().rearrange("(r d) s -> r d s", r=4)
    vag = v_ag.ap().rearrange("(r s) d -> r s d", r=4)

    def load_head(h):
        par = h % 2
        qb, kb, vb = f"qh{par}", f"kh{par}", f"vh{par}"
        fw.op("sync", lambda h=h: nc.sync.dma_start(
            qh[h % 2][:, :], q_scr[h, :, :]), writes=[qb], dma=True)
        for g in range(span_chunks):
            r, half = (g, 0) if g < 4 else (7 - g, 1)
            fw.op("sync", lambda h=h, g=g, r=r, half=half: nc.sync.dma_start(
                kh[h % 2][:, 1024 * g:1024 * (g + 1)],
                kag[r, 128 * h:128 * (h + 1), 1024 * half:1024 * (half + 1)]),
                writes=[kb], dma=True)
            fw.op("gpsimd", lambda h=h, g=g, r=r, half=half: nc.gpsimd.dma_start(
                vh[h % 2][:, 1024 * g:1024 * (g + 1)]
                  .rearrange("p (a d) -> p a d", a=8),
                vag[r, 1024 * half:1024 * (half + 1), 128 * h:128 * (h + 1)]
                  .rearrange("(a p) d -> p a d", p=128)),
                writes=[vb], dma=True)

    def finalize(h, tb, ti):
        prow = 0 if ti == 0 else 32
        fw.op("tensor", lambda ti=ti, prow=prow: nc.tensor.matmul(
            zps[prow:prow + 1, :], ones_f[:, :],
            pacc_v[:, 512 * ti:512 * (ti + 1)], start=True, stop=True),
            reads=[f"paccv{ti}", "ones_f"], writes=["zps"])
        fw.op("vector", lambda tb=tb, prow=prow: nc.vector.tensor_scalar(
            zc_sb[:, :], zps[prow:prow + 1, :], counts[tb], None, ALU.add),
            reads=["zps"], writes=["zc_sb"])
        fw.op("vector", lambda: nc.vector.reciprocal(zi_sb[:, :], zc_sb[:, :]),
              reads=["zc_sb"], writes=["zi_sb"])
        fw.op("tensor", lambda: nc.tensor.matmul(
            zbp[:, :], onesr_f[:, :], zi_sb[:, :], start=True, stop=True),
            reads=["zi_sb", "onesr_f"], writes=["zbp"])
        fw.op("scalar", lambda: nc.scalar.copy(zb_sb[:, :], zbp[:, :]),
              reads=["zbp", "zb_sb"], writes=["zb_sb"])
        apar = tb % 2
        fw.op("vector", lambda h=h, tb=tb, ti=ti, apar=apar: nc.vector.scalar_tensor_tensor(
            ast[apar][:, :], nps[:, 512 * ti:512 * (ti + 1)],
            sufV[:, 65 * h + ns[tb]:65 * h + ns[tb] + 1],
            zb_sb[:, :], ALU.add, ALU.mult),
            reads=[f"nps{ti}", "sufV", "zb_sb"], writes=[f"ast{apar}"])
        fw.op("gpsimd", lambda h=h, tb=tb, apar=apar: nc.gpsimd.dma_start(
            a_scr[h, :, 512 * tb:512 * (tb + 1)], ast[apar][:, :]),
            reads=[f"ast{apar}"], writes=[f"a_scr{h}"], dma=True)

    load_head(0)
    for h in range(H):
        par = h % 2
        qb, kb, vb = f"qh{par}", f"kh{par}", f"vh{par}"
        if h + 1 < H:
            load_head(h + 1)
        for p in range(2):
            lo, hi = 2 * p, 2 * p + 1
            n_lo, n_hi = ns[lo], ns[hi]
            def av_emit(sb):
                both = sb < n_lo
                ppar = sb % 4
                ppb = f"pp{ppar}"
                if both:
                    fw.op("tensor", lambda sb=sb, par=par, ppar=ppar, n_lo=n_lo:
                          nc.tensor.matmul(
                              nps[:, 0:512],
                              vh[par][:, 128 * sb:128 * (sb + 1)],
                              pp[ppar][:, 0:512],
                              start=(sb == 0), stop=(sb == n_lo - 1)),
                          reads=[f"vh{par}", ppb], writes=["nps0"])
                fw.op("tensor", lambda sb=sb, par=par, ppar=ppar, n_hi=n_hi:
                      nc.tensor.matmul(
                          nps[:, 512:1024],
                          vh[par][:, 128 * sb:128 * (sb + 1)],
                          pp[ppar][:, 512:1024],
                          start=(sb == 0), stop=(sb == n_hi - 1)),
                      reads=[f"vh{par}", ppb], writes=["nps1"])

            LAG = 2
            for sb in range(n_hi):
                both = sb < n_lo
                ppar = sb % 4
                spar = sb % 2
                ppb = f"pp{ppar}"
                stL, stH = f"st{spar}L", f"st{spar}H"
                lo0 = 0 if both else 512
                if both:
                    fw.op("tensor", lambda sb=sb, lo=lo, par=par, spar=spar: nc.tensor.matmul(
                        stp[spar][:, 0:512],
                        kh[par][:, 128 * sb:128 * (sb + 1)],
                        qh[par][:, 512 * lo:512 * (lo + 1)],
                        start=True, stop=True),
                        reads=[kb, qb], writes=[stL])
                fw.op("tensor", lambda sb=sb, hi=hi, par=par, spar=spar: nc.tensor.matmul(
                    stp[spar][:, 512:1024],
                    kh[par][:, 128 * sb:128 * (sb + 1)],
                    qh[par][:, 512 * hi:512 * (hi + 1)],
                    start=True, stop=True),
                    reads=[kb, qb], writes=[stH])
                if both and n_lo - 4 <= sb < n_lo:
                    k = sb - (n_lo - 4)
                    fw.op("vector", lambda k=k, spar=spar: nc.vector.tensor_mul(
                        stp[spar][:, 0:512], stp[spar][:, 0:512],
                        masks[:, 512 * k:512 * (k + 1)]),
                        reads=[stL, "masks"], writes=[stL])
                if n_hi - 4 <= sb < n_hi:
                    k = sb - (n_hi - 4)
                    fw.op("vector", lambda k=k, spar=spar: nc.vector.tensor_mul(
                        stp[spar][:, 512:1024], stp[spar][:, 512:1024],
                        masks[:, 512 * k:512 * (k + 1)]),
                        reads=[stH, "masks"], writes=[stH])
                st_reads = ([stL, stH] if both else [stH])
                fw.op("scalar", lambda ppar=ppar, lo0=lo0, spar=spar: nc.scalar.activation(
                    pp[ppar][:, lo0:1024], stp[spar][:, lo0:1024], AF.Exp, scale=SCALE),
                    reads=st_reads, writes=[ppb])
                if sb % 2 == 1:
                    tpar = (sb // 2) % 2
                    tpb = f"ptmp{tpar}"
                    ppb_prev = f"pp{(sb - 1) % 4}"
                    plo0 = 0 if (sb - 1) < n_lo else 512
                    pkeys = ["paccv0", "paccv1"] if plo0 == 0 else ["paccv1"]
                    fw.op("vector", lambda ppar=ppar, sb=sb, plo0=plo0, tpar=tpar:
                          nc.vector.tensor_add(
                              ptmp[tpar][:, plo0:1024],
                              pp[(sb - 1) % 4][:, plo0:1024], pp[ppar][:, plo0:1024]),
                          reads=[ppb, ppb_prev, tpb], writes=[tpb])
                    if sb == 1:
                        fw.op("vector", lambda tpar=tpar: nc.vector.tensor_copy(
                            pacc_v[:, :], ptmp[tpar][:, :]),
                            reads=[tpb], writes=["paccv0", "paccv1"])
                    else:
                        fw.op("vector", lambda tpar=tpar, plo0=plo0: nc.vector.tensor_add(
                            pacc_v[:, plo0:1024], pacc_v[:, plo0:1024], ptmp[tpar][:, plo0:1024]),
                            reads=[tpb] + pkeys, writes=pkeys)
                if sb >= LAG:
                    av_emit(sb - LAG)
                if sb - LAG == n_lo + 1:
                    finalize(h, lo, 0)
            for sb in range(max(n_hi - LAG, 0), n_hi):
                av_emit(sb)
                if sb == n_lo + 1:
                    finalize(h, lo, 0)
            finalize(h, hi, 1)


def kernel(L0_attn_x, L0_attn_Wq, L0_attn_Wk, L0_attn_Wv, L0_attn_Wo,
           L0_ffn_w_gate, L0_ffn_w_up, L0_ffn_w_down):
    x = np.asarray(L0_attn_x)
    wq = np.ascontiguousarray(np.asarray(L0_attn_Wq), dtype=np.float32)
    wk = np.ascontiguousarray(np.asarray(L0_attn_Wk), dtype=np.float32)
    wv = np.ascontiguousarray(np.asarray(L0_attn_Wv), dtype=np.float32)
    wo = np.ascontiguousarray(np.asarray(L0_attn_Wo), dtype=np.float32)
    wgt = np.ascontiguousarray(np.asarray(L0_ffn_w_gate).T, dtype=np.float32)
    wut = np.ascontiguousarray(np.asarray(L0_ffn_w_up).T, dtype=np.float32)
    wdt = np.ascontiguousarray(np.asarray(L0_ffn_w_down).T, dtype=np.float32)

    xT = [np.ascontiguousarray(x[b].T, dtype=np.float32) for b in range(B)]

    nc = build_program()
    in_maps = []
    for core in range(8):
        b, c = core // 4, core % 4
        xo = np.concatenate([xT[b][:, 1024 * c:1024 * (c + 1)],
                             xT[b][:, 1024 * (7 - c):1024 * (8 - c)]], axis=1)
        in_maps.append({
            "x_ownT": np.ascontiguousarray(xo),
            "wq": wq, "wk": wk, "wv": wv, "wo": wo,
            "wgt": wgt, "wut": wut, "wdt": wdt,
            "cidx": np.array([[c]], dtype=np.int32),
        })
    res = run_bass_kernel_spmd(nc, in_maps, core_ids=list(range(8)))
    out = np.empty((B, T, D), dtype=np.float32)
    for core in range(8):
        b, c = core // 4, core % 4
        o = np.asarray(res.results[core]["out_fm"])
        out[b, 1024 * c:1024 * (c + 1), :] = o[:, 0:1024].T
        out[b, 1024 * (7 - c):1024 * (8 - c), :] = o[:, 1024:2048].T
    return out


# version 10
# speedup vs baseline: 1.3727x; 1.0157x over previous
import math
import os as _os
import numpy as np
import ml_dtypes
import concourse.bass as bass
import concourse.mybir as mybir
from concourse.expressions import RuntimeValue
from concourse.bass_utils import run_bass_kernel_spmd

F32 = mybir.dt.float32
BF16 = mybir.dt.bfloat16
I32 = mybir.dt.int32
AF = mybir.ActivationFunctionType
ALU = mybir.AluOpType
AX = mybir.AxisListType

D = 2048
T = 8192
B = 2
H = 16
DH = 128
HID = 8192
NT = 2048
SCALE = 1.0 / math.sqrt(T)
EPS = 1e-5
NDK = D // 128
NSB = T // 128

SET_LN_EXP = 6
SET_SILU = 18


def load_act_set(nc, set_id):
    return nc.scalar.add_instruction(mybir.InstLoadActFuncSet(
        name=nc.get_next_instruction_name(),
        act_func_set_id=set_id, ins=[], outs=[]))


class FW:
    ENG_SEM = {
        "tensor": ("T", None),
        "scalar": ("S", None),
        "vector": ("V", None),
        "gpsimd": ("PG", "DG"),
        "sync": ("SY", "DQ"),
    }
    ALL_SEMS = ["T", "S", "V", "PG", "SY", "DG", "DQ"]

    def __init__(self, nc):
        self.nc = nc
        self.sem = {}
        self.scope = ""
        self.cnt = {}
        self.known = {}
        self.bufs = {}
        for s in self.ALL_SEMS:
            for suf in ("", "b"):
                k = s + suf
                self.sem[k] = nc.alloc_semaphore(name="sem_" + k)
                self.cnt[k] = 0

    def semkey(self, engine, dma):
        c, d = self.ENG_SEM[engine]
        base = d if dma else c
        if self.scope and base in ("T", "S", "V", "PG", "SY", "DG", "DQ"):
            return base + self.scope
        return base

    def enter_body(self):
        self.scope = "b"
        for s in self.ALL_SEMS:
            self.cnt[s + "b"] = 0
        self.known = {k: v for k, v in self.known.items() if not k[1].endswith("b")}
        self._saved_bufs = {k: {"w": dict(v["w"]), "r": dict(v["r"])}
                            for k, v in self.bufs.items()}
        self._saved_known = dict(self.known)
        self._saved_cnt = dict(self.cnt)

    def exit_body(self):
        nc = self.nc
        finals = {s + "b": self.cnt[s + "b"] for s in self.ALL_SEMS}
        for ename in self.ENG_SEM:
            eng = getattr(nc, ename)
            for sk, v in finals.items():
                if v > 0:
                    eng.wait_ge(self.sem[sk], v)
            gk = self.ENG_SEM[ename][0]
            eng.drain().then_inc(self.sem[gk], 1)
        self.scope = ""

    def body_reset(self):
        self.scope = "b"
        self.bufs = {k: {"w": dict(v["w"]), "r": dict(v["r"])}
                     for k, v in self._saved_bufs.items()}
        self.known = dict(self._saved_known)
        self.cnt = dict(self._saved_cnt)
        for s in self.ALL_SEMS:
            self.cnt[s + "b"] = 0

    def finish_switch(self):
        self.scope = ""
        for ename in self.ENG_SEM:
            gk = self.ENG_SEM[ename][0]
            self.cnt[gk] += 1

    def wrote(self, buf, engine):
        gk = self.ENG_SEM[engine][0]
        self.bufs[buf] = {"w": {gk: self.cnt[gk]}, "r": {}}

    def _waits(self, ename, deps):
        eng = getattr(self.nc, ename)
        for sk, v in sorted(deps.items()):
            if v <= 0:
                continue
            kk = (ename, sk)
            if self.known.get(kk, 0) >= v:
                continue
            eng.wait_ge(self.sem[sk], v)
            self.known[kk] = v

    def op(self, ename, fn, reads=(), writes=(), dma=False, inc=True):
        deps = {}
        my_sk = self.semkey(ename, dma)
        my_csk = self.semkey(ename, False)

        def merge(d):
            for sk, v in d.items():
                if sk == my_csk and not dma and ename == "tensor":
                    continue
                if deps.get(sk, 0) < v:
                    deps[sk] = v

        for b in reads:
            st = self.bufs.get(b)
            if st:
                merge(st["w"])
        for b in writes:
            st = self.bufs.get(b)
            if st:
                merge(st["w"])
                merge(st["r"])
        self._waits(ename, deps)
        inst = fn()
        if inc:
            amt = 16 if dma else 1
            inst.then_inc(self.sem[my_sk], amt)
            self.cnt[my_sk] += amt
        tok = (my_sk, self.cnt[my_sk])
        for b in writes:
            self.bufs[b] = {"w": {tok[0]: tok[1]}, "r": {}}
        for b in reads:
            st = self.bufs.setdefault(b, {"w": {}, "r": {}})
            if st["r"].get(tok[0], 0) < tok[1]:
                st["r"][tok[0]] = tok[1]
        return tok

    def barrier(self):
        assert self.scope == ""
        nc = self.nc
        for ename in self.ENG_SEM:
            gk = self.ENG_SEM[ename][0]
            getattr(nc, ename).drain().then_inc(self.sem[gk], 1)
            self.cnt[gk] += 1
        targets = {s: self.cnt[s] for s in self.ALL_SEMS}
        for ename in self.ENG_SEM:
            eng = getattr(nc, ename)
            for s, v in targets.items():
                if v > 0:
                    eng.wait_ge(self.sem[s], v)
            for s, v in targets.items():
                self.known[(ename, s)] = v
        for b in self.bufs.values():
            pass


def build_program(stop_after=None, dbg=()):
    KSC = int(_os.environ.get("KDBG_SC", "16"))
    KSKIP = set(_os.environ.get("KDBG_SKIP", "").split(","))
    """stop_after in {None,'P1k','P1v','P2','A','O','F1'}; dbg: iterable of
    scratch names to expose as ExternalOutput."""
    nc = bass.Bass("TRN2", target_bir_lowering=False)

    def kind_of(name):
        return "ExternalOutput" if name in dbg else "Internal"

    x_ownT = nc.dram_tensor("x_ownT", [D, NT], F32, kind="ExternalInput").ap()
    wq = nc.dram_tensor("wq", [D, D], F32, kind="ExternalInput").ap()
    wk = nc.dram_tensor("wk", [D, D], F32, kind="ExternalInput").ap()
    wv = nc.dram_tensor("wv", [D, D], F32, kind="ExternalInput").ap()
    wo = nc.dram_tensor("wo", [D, D], F32, kind="ExternalInput").ap()
    wgt = nc.dram_tensor("wgt", [D, HID], F32, kind="ExternalInput").ap()
    wut = nc.dram_tensor("wut", [D, HID], F32, kind="ExternalInput").ap()
    wdt = nc.dram_tensor("wdt", [HID, D], F32, kind="ExternalInput").ap()
    cidx = nc.dram_tensor("cidx", [1, 1], I32, kind="ExternalInput").ap()
    out_fm = nc.dram_tensor("out_fm", [D, NT], F32, kind="ExternalOutput").ap()
    k_own = nc.dram_tensor("k_own", [D, NT], BF16, kind="Internal")
    bs_own = nc.dram_tensor("bs_own", [128, 256], F32, kind="Internal")
    bs_ag = nc.dram_tensor("bs_ag", [512, 256], F32, kind="Internal")
    ag_dbg = {}
    for nm, src_shape, dt in (("k_ag", [4 * D, NT], BF16), ("v_ag", [4 * NT, D], BF16),
                              ("bs_ag", [512, 256], F32), ("v_own", [NT, D], BF16)):
        if nm in dbg:
            ag_dbg[nm] = nc.dram_tensor(nm + "_dbg", src_shape, dt, kind="ExternalOutput")
    v_own = nc.dram_tensor("v_own", [NT, D], BF16, kind="Internal")
    k_ag = nc.dram_tensor("k_ag", [4 * D, NT], BF16, kind="Internal")
    v_ag = nc.dram_tensor("v_ag", [4 * NT, D], BF16, kind="Internal")
    q_scr = nc.dram_tensor("q_scr", [H, 128, NT], BF16, kind=kind_of("q_scr")).ap()
    a_scr = nc.dram_tensor("a_scr", [H, 128, NT], BF16, kind=kind_of("a_scr")).ap()
    y1f_scr = nc.dram_tensor("y1f_scr", [NDK, 128, NT], F32, kind=kind_of("y1f_scr")).ap()
    y1b_scr = nc.dram_tensor("y1b_scr", [NDK, 128, NT], BF16, kind=kind_of("y1b_scr")).ap()
    hid_scr = nc.dram_tensor("hid_scr", [HID // 128, 128, NT], BF16, kind=kind_of("hid_scr")).ap()
    dn_scr = nc.dram_tensor("dn_scr", [NDK, 128, NT], F32, kind=kind_of("dn_scr")).ap()
    r2_scr = nc.dram_tensor("r2_scr", [NDK, 128, NT], F32, kind=kind_of("r2_scr")).ap()

    fw = FW(nc)
    from contextlib import ExitStack

    with ExitStack() as outer:
        c_sb = outer.enter_context(nc.sbuf_tensor("c_sb", [1, 1], I32))
        ones_f = outer.enter_context(nc.sbuf_tensor("ones_f", [128, 1], F32))
        ones_b = outer.enter_context(nc.sbuf_tensor("ones_b", [128, 1], BF16))
        onesr_f = outer.enter_context(nc.sbuf_tensor("onesr_f", [1, 128], F32))
        masks = outer.enter_context(nc.sbuf_tensor("masks", [128, 4 * 512], F32))
        iotat = outer.enter_context(nc.sbuf_tensor("iotat", [128, 512], F32))
        sufA = outer.enter_context(nc.sbuf_tensor("sufA", [128, 65 * NDK], F32))
        sufB = outer.enter_context(nc.sbuf_tensor("sufB", [128, 65 * NDK], F32))
        sufXb = outer.enter_context(nc.sbuf_tensor("sufXb", [128, NDK * 65], BF16))
        sufV = outer.enter_context(nc.sbuf_tensor("sufV", [128, H * 65], F32))

        fw.op("sync", lambda: nc.sync.dma_start(c_sb[:, :], cidx),
              writes=["c_sb"], dma=True)
        if "init" not in KSKIP:
            fw.op("vector", lambda: nc.vector.memset(ones_f[:, :], 1.0), writes=["ones_f"])
            fw.op("vector", lambda: nc.vector.memset(ones_b[:, :], 1.0), writes=["ones_b"])
            fw.op("vector", lambda: nc.vector.memset(onesr_f[:, :], 1.0), writes=["onesr_f"])
            fw.op("scalar", lambda: load_act_set(nc, SET_LN_EXP), inc=False)
        for k in range(4 if "init" not in KSKIP else 0):
            fw.op("gpsimd",
                  lambda k=k: nc.gpsimd.iota(
                      iotat[:, :], [[1, 512]], base=-(128 * k),
                      channel_multiplier=-1,
                      allow_small_or_imprecise_dtypes=True),
                  writes=["iotat"])
            fw.op("vector",
                  lambda k=k: nc.vector.tensor_single_scalar(
                      masks[:, 512 * k:512 * (k + 1)], iotat[:, :], 0.0, ALU.is_ge),
                  reads=["iotat"], writes=["masks"])

        with ExitStack() as st:
            wk_sb = st.enter_context(nc.sbuf_tensor("wk_sb", [128, NDK * D], BF16))
            xc = [st.enter_context(nc.sbuf_tensor(f"xc{i}", [128, NDK * 512], BF16))
                  for i in range(2)]
            kst = [st.enter_context(nc.sbuf_tensor(f"kst{i}", [128, 512], BF16))
                   for i in range(2)]
            kps = st.enter_context(nc.psum_tensor("kps", [128, 4 * 512], F32))

            fw.op("gpsimd", lambda: nc.gpsimd.dma_start(
                wk_sb[:, :].rearrange("p (a n) -> p a n", a=NDK),
                wk.rearrange("(a p) n -> p a n", p=128)),
                writes=["wk_sb"], dma=True)

            for sc in range(4):
                xb = f"xc{sc % 2}"
                fw.op("gpsimd", lambda sc=sc: nc.gpsimd.dma_start(
                    xc[sc % 2][:, :].rearrange("p (a s) -> p a s", a=NDK),
                    x_ownT.rearrange("(a p) s -> p a s", p=128)[:, :, 512 * sc:512 * (sc + 1)]),
                    writes=[xb], dma=True)
                for dcol in range(16):
                    pb = f"kps{dcol % 4}"
                    for dk in range(NDK):
                        fw.op("tensor", lambda sc=sc, dcol=dcol, dk=dk: nc.tensor.matmul(
                            kps[:, 512 * (dcol % 4):512 * (dcol % 4 + 1)],
                            wk_sb[:, D * dk + 128 * dcol: D * dk + 128 * (dcol + 1)],
                            xc[sc % 2][:, 512 * dk:512 * (dk + 1)],
                            start=(dk == 0), stop=(dk == NDK - 1)),
                            reads=["wk_sb", xb], writes=[pb])
                    ks = f"kst{dcol % 2}"
                    fw.op("scalar", lambda dcol=dcol: nc.scalar.copy(
                        kst[dcol % 2][:, :], kps[:, 512 * (dcol % 4):512 * (dcol % 4 + 1)]),
                        reads=[pb, ks], writes=[ks])
                    fw.op("sync", lambda sc=sc, dcol=dcol: nc.sync.dma_start(
                        k_own[128 * dcol:128 * (dcol + 1), 512 * sc:512 * (sc + 1)],
                        kst[dcol % 2][:, :]),
                        reads=[ks], writes=["k_own"], dma=True)
                for sb in range(4):
                    lb = sc * 4 + sb
                    fw.op("vector", lambda sc=sc, sb=sb, lb=lb: nc.vector.reduce_sum(
                        sufA[:, 16 * lb:16 * (lb + 1)].rearrange("p (a o) -> p a o", o=1),
                        xc[sc % 2][:, :].rearrange("p (a s) -> p a s", a=NDK)[:, :, 128 * sb:128 * (sb + 1)],
                        axis=AX.X),
                        reads=[xb], writes=["sufA"])
            fw.op("sync", lambda: nc.sync.dma_start(bs_own.ap(), sufA[:, 0:256]),
                  reads=["sufA"], writes=["bs_own"], dma=True)

        fw.barrier()
        if stop_after == "P1k":
            return nc

        with ExitStack() as st:
            wv_sb = st.enter_context(nc.sbuf_tensor("wv_sb", [128, NDK * D], BF16))
            xc = [st.enter_context(nc.sbuf_tensor(f"xv{i}", [128, NDK * 512], BF16))
                  for i in range(2)]
            vst = [st.enter_context(nc.sbuf_tensor(f"vst{i}", [128, D], BF16))
                   for i in range(2)]
            vps = [st.enter_context(nc.psum_tensor(f"vps{i}", [128, D], F32))
                   for i in range(2)]
            svps = vps[0]

            fw.op("gpsimd", lambda: nc.gpsimd.dma_start(
                wv_sb[:, :].rearrange("p (a n) -> p a n", a=NDK),
                wv.rearrange("(a p) n -> p a n", p=128)),
                writes=["wv_sb"], dma=True)

            for sc in range(4):
                xb = f"xv{sc % 2}"
                fw.op("gpsimd", lambda sc=sc: nc.gpsimd.dma_start(
                    xc[sc % 2][:, :].rearrange("p (a s) -> p a s", a=NDK),
                    x_ownT.rearrange("(a p) s -> p a s", p=128)[:, :, 512 * sc:512 * (sc + 1)]),
                    writes=[xb], dma=True)
                for sblk in range(4):
                    sb = sc * 4 + sblk
                    pb = f"vps{sb % 2}"
                    for dcb in range(4):
                        for dk in range(NDK):
                            fw.op("tensor", lambda sc=sc, sblk=sblk, sb=sb, dcb=dcb, dk=dk:
                                  nc.tensor.matmul(
                                      vps[sb % 2][:, 512 * dcb:512 * (dcb + 1)],
                                      xc[sc % 2][:, 512 * dk + 128 * sblk:512 * dk + 128 * (sblk + 1)],
                                      wv_sb[:, D * dk + 512 * dcb: D * dk + 512 * (dcb + 1)],
                                      start=(dk == 0), stop=(dk == NDK - 1)),
                                  reads=["wv_sb", xb], writes=[pb])
                    vs = f"vst{sb % 2}"
                    fw.op("scalar", lambda sb=sb: nc.scalar.copy(
                        vst[sb % 2][:, :], vps[sb % 2][:, :]),
                        reads=[pb, vs], writes=[vs])
                    fw.op("sync", lambda sb=sb: nc.sync.dma_start(
                        v_own[128 * sb:128 * (sb + 1), :], vst[sb % 2][:, :]),
                        reads=[vs], writes=["v_own"], dma=True)

            wq_sb = st.enter_context(nc.sbuf_tensor("wq_sb", [128, NDK * D], BF16))
            qst = [st.enter_context(nc.sbuf_tensor(f"qst{i}", [128, 512], BF16))
                   for i in range(2)]
            qps = vps[0]
            fw.op("gpsimd", lambda: nc.gpsimd.dma_start(
                wq_sb[:, :].rearrange("p (a n) -> p a n", a=NDK),
                wq.rearrange("(a p) n -> p a n", p=128)),
                writes=["wq_sb"], dma=True)
            for sc in range(4):
                xb = f"xv{sc % 2}"
                fw.op("gpsimd", lambda sc=sc: nc.gpsimd.dma_start(
                    xc[sc % 2][:, :].rearrange("p (a s) -> p a s", a=NDK),
                    x_ownT.rearrange("(a p) s -> p a s", p=128)[:, :, 512 * sc:512 * (sc + 1)]),
                    writes=[xb], dma=True)
                for dcol in range(16):
                    pb = f"qp{dcol % 4}"
                    extra = ["vps0", "vps1"] if (sc == 0 and dcol < 4) else []
                    for dk in range(NDK):
                        fw.op("tensor", lambda sc=sc, dcol=dcol, dk=dk: nc.tensor.matmul(
                            qps[:, 512 * (dcol % 4):512 * (dcol % 4 + 1)],
                            wq_sb[:, D * dk + 128 * dcol: D * dk + 128 * (dcol + 1)],
                            xc[sc % 2][:, 512 * dk:512 * (dk + 1)],
                            start=(dk == 0), stop=(dk == NDK - 1)),
                            reads=["wq_sb", xb], writes=[pb] + extra)
                    qs = f"qst{dcol % 2}"
                    fw.op("scalar", lambda dcol=dcol: nc.scalar.copy(
                        qst[dcol % 2][:, :], qps[:, 512 * (dcol % 4):512 * (dcol % 4 + 1)]),
                        reads=[f"qp{dcol % 4}", qs], writes=[qs])
                    fw.op("sync", lambda sc=sc, dcol=dcol: nc.sync.dma_start(
                        q_scr[dcol, :, 512 * sc:512 * (sc + 1)], qst[dcol % 2][:, :]),
                        reads=[qs], writes=[f"q_scr{dcol}"], dma=True)
            fw.op("gpsimd", lambda: nc.gpsimd.collective_compute(
                "AllGather", ALU.bypass,
                replica_groups=[[0, 1, 2, 3], [4, 5, 6, 7]],
                ins=[k_own.ap().opt()], outs=[k_ag.ap().opt()]),
                reads=["k_own"], writes=["k_ag"])
            fw.op("gpsimd", lambda: nc.gpsimd.collective_compute(
                "AllGather", ALU.bypass,
                replica_groups=[[0, 1, 2, 3], [4, 5, 6, 7]],
                ins=[bs_own.ap().opt()], outs=[bs_ag.ap().opt()]),
                reads=["bs_own"], writes=["bs_ag"])
            fw.op("gpsimd", lambda: nc.gpsimd.collective_compute(
                "AllGather", ALU.bypass,
                replica_groups=[[0, 1, 2, 3], [4, 5, 6, 7]],
                ins=[v_own.ap().opt()], outs=[v_ag.ap().opt()]),
                reads=["v_own"], writes=["v_ag"])

            fw.barrier()
            bsv = bs_ag.ap().rearrange("(r p) n -> r p n", p=128)
            for g in range(8):
                r, half = (g, 0) if g < 4 else (7 - g, 1)
                fw.op("sync", lambda g=g, r=r, half=half: nc.sync.dma_start(
                    sufA[:, 16 * 8 * g:16 * 8 * (g + 1)],
                    bsv[r, :, 128 * half:128 * (half + 1)]),
                    reads=["bs_ag"], writes=["sufA"], dma=True)
            fw.op("vector", lambda: nc.vector.memset(sufA[:, 64 * 16:65 * 16], 0.0),
                  writes=["sufA"])
            cur, oth = sufA, sufB
            curb, othb = "sufA", "sufB"
            for lvl in range(6):
                sh = 1 << lvl
                w = (65 - sh) * 16
                fw.op("vector", lambda cur=cur, oth=oth, sh=sh, w=w: nc.vector.tensor_add(
                    oth[:, 0:w], cur[:, 0:w], cur[:, 16 * sh:16 * sh + w]),
                    reads=[curb], writes=[othb])
                fw.op("vector", lambda cur=cur, oth=oth, w=w: nc.vector.tensor_copy(
                    oth[:, w:65 * 16], cur[:, w:65 * 16]),
                    reads=[curb], writes=[othb])
                cur, oth = oth, cur
                curb, othb = othb, curb
            fw.op("scalar", lambda cur=cur: nc.scalar.copy(
                sufXb[:, :].rearrange("p (a j) -> p a j", a=NDK),
                cur[:, :].rearrange("p (j a) -> p a j", a=NDK)),
                reads=[curb], writes=["sufXb"])
            for h in range(H):
                for dk in range(NDK):
                    fw.op("tensor", lambda h=h, dk=dk: nc.tensor.matmul(
                        svps[:, 0:65],
                        wv_sb[:, D * dk + 128 * h: D * dk + 128 * (h + 1)],
                        sufXb[:, 65 * dk:65 * (dk + 1)],
                        start=(dk == 0), stop=(dk == NDK - 1)),
                        reads=["wv_sb", "sufXb"], writes=["vps0", "qp0"])
                fw.op("scalar", lambda h=h: nc.scalar.copy(
                    sufV[:, 65 * h:65 * (h + 1)], svps[:, 0:65]),
                    reads=["vps0", "sufV"], writes=["sufV"])
        fw.barrier()
        for nm, dst in ag_dbg.items():
            src = {"k_ag": k_ag, "v_ag": v_ag, "bs_ag": bs_ag, "v_own": v_own}[nm]
            fw.op("sync", lambda src=src, dst=dst: nc.sync.dma_start(
                dst[:, :], src[:, :]), reads=[nm], dma=True)
        if stop_after in ("P1v", "P2"):
            return nc

        with ExitStack() as st:
            wo_sb = st.enter_context(nc.sbuf_tensor("wo_sb", [128, NDK * D], BF16))
            qh = [st.enter_context(nc.sbuf_tensor(f"qh{i}", [128, NT], BF16))
                  for i in range(2)]
            kh = [st.enter_context(nc.sbuf_tensor(f"kh{i}", [128, T], BF16))
                  for i in range(2)]
            vh = [st.enter_context(nc.sbuf_tensor(f"vh{i}", [128, T], BF16))
                  for i in range(2)]
            pp = [st.enter_context(nc.sbuf_tensor(f"pp{i}", [128, 1024], BF16))
                  for i in range(4)]
            ptmp = [st.enter_context(nc.sbuf_tensor(f"ptmp{i}", [128, 1024], BF16))
                    for i in range(2)]
            pacc_v = st.enter_context(nc.sbuf_tensor("pacc_v", [128, 1024], F32))
            zc_sb = st.enter_context(nc.sbuf_tensor("zc_sb", [1, 512], F32))
            zi_sb = st.enter_context(nc.sbuf_tensor("zi_sb", [1, 512], F32))
            zb_sb = st.enter_context(nc.sbuf_tensor("zb_sb", [128, 512], F32))
            ast = [st.enter_context(nc.sbuf_tensor(f"ast{i}", [128, 512], BF16))
                   for i in range(2)]
            nps = st.enter_context(nc.psum_tensor("nps", [128, 2 * 512], F32))
            stp = [st.enter_context(nc.psum_tensor(f"stp{i}", [128, 1024], F32))
                   for i in range(2)]
            zps = st.enter_context(nc.psum_tensor("zps", [128, 512], F32))
            zbp = st.enter_context(nc.psum_tensor("zbp", [128, 512], F32))

            fw.op("gpsimd", lambda: nc.gpsimd.dma_start(
                wo_sb[:, :].rearrange("p (a n) -> p a n", a=NDK),
                wo.rearrange("(a p) n -> p a n", p=128)),
                writes=["wo_sb"], dma=True)
            engines = [nc.tensor, nc.scalar, nc.vector, nc.gpsimd, nc.sync]
            idxs = []
            for eng, ename in zip(engines, ["tensor", "scalar", "vector", "gpsimd", "sync"]):
                fw._waits(ename, {"DQ": 16})
                r = eng.alloc_register()
                eng.reg_load(r, c_sb[0:1, 0:1])
                idxs.append(RuntimeValue(r, min_val=0, max_val=3))

            first_body = True
            for c in nc.Switch(engines=engines, index=idxs, n=4):
                if first_body:
                    fw.enter_body()
                    first_body = False
                else:
                    fw.body_reset()
                emit_attention(nc, fw, c, qh, kh, vh, pp, ptmp, pacc_v, zc_sb, zi_sb,
                               zb_sb, ast, nps, stp, zps, zbp,
                               masks, sufV, ones_f, onesr_f,
                               q_scr, k_ag, v_ag, a_scr)
                fw.exit_body()
            fw.finish_switch()
            for h in range(H):
                fw.wrote(f"a_scr{h}", "gpsimd")
        fw.barrier()
        if stop_after == "A":
            return nc

        with ExitStack() as st:
            wo_sb = st.enter_context(nc.sbuf_tensor("wo_sb_o", [128, NDK * D], BF16))
            atb = [st.enter_context(nc.sbuf_tensor(f"atb{i}", [128, NDK * 512], BF16))
                   for i in range(2)]
            xres = [st.enter_context(nc.sbuf_tensor(f"xres{i}", [128, 512], F32))
                    for i in range(4)]
            r1 = st.enter_context(nc.sbuf_tensor("r1", [128, NDK * 512], F32))
            r1b = st.enter_context(nc.sbuf_tensor("r1b", [128, 8 * 512], BF16))
            sqb = st.enter_context(nc.sbuf_tensor("sqb", [128, 8 * 512], BF16))
            trow = st.enter_context(nc.sbuf_tensor("trow", [1, 512], F32))
            mrow = st.enter_context(nc.sbuf_tensor("mrow", [1, 512], F32))
            vrow = st.enter_context(nc.sbuf_tensor("vrow", [1, 512], F32))
            irow = st.enter_context(nc.sbuf_tensor("irow", [1, 512], F32))
            brow = st.enter_context(nc.sbuf_tensor("brow", [1, 512], F32))
            y1fs = [st.enter_context(nc.sbuf_tensor(f"y1fs{i}", [128, 512], F32))
                    for i in range(2)]
            abc_sb = st.enter_context(nc.sbuf_tensor("abc_sb", [128, 512], F32))
            bbc_sb = st.enter_context(nc.sbuf_tensor("bbc_sb", [128, 512], F32))
            y1bs = [st.enter_context(nc.sbuf_tensor(f"y1bs{i}", [128, 512], BF16))
                    for i in range(2)]
            ops_ = st.enter_context(nc.psum_tensor("ops", [128, 4 * 512], F32))
            sumps = st.enter_context(nc.psum_tensor("sumps", [128, 512], F32))
            sqps = st.enter_context(nc.psum_tensor("sqps", [128, 512], F32))
            abc = st.enter_context(nc.psum_tensor("abc", [128, 512], F32))
            bbc = st.enter_context(nc.psum_tensor("bbc", [128, 512], F32))

            for tb in range(4):
                ab = f"atb{tb % 2}"
                fw.op("sync", lambda tb=tb: nc.sync.dma_start(
                    atb[tb % 2][:, :].rearrange("p (a s) -> p a s", a=NDK),
                    a_scr.rearrange("h p t -> p h t")[:, :, 512 * tb:512 * (tb + 1)]),
                    reads=[f"a_scr{h}" for h in range(H)], writes=[ab], dma=True)
                for dcol in range(16):
                    pb = f"ops{dcol % 4}"
                    xr = f"xres{dcol % 4}"
                    fw.op("sync", lambda tb=tb, dcol=dcol: nc.sync.dma_start(
                        xres[dcol % 4][:, :],
                        x_ownT[128 * dcol:128 * (dcol + 1), 512 * tb:512 * (tb + 1)]),
                        writes=[xr], dma=True)
                    for dk in range(NDK):
                        fw.op("tensor", lambda tb=tb, dcol=dcol, dk=dk: nc.tensor.matmul(
                            ops_[:, 512 * (dcol % 4):512 * (dcol % 4 + 1)],
                            wo_sb[:, D * dk + 128 * dcol: D * dk + 128 * (dcol + 1)],
                            atb[tb % 2][:, 512 * dk:512 * (dk + 1)],
                            start=(dk == 0), stop=(dk == NDK - 1)),
                            reads=["wo_sb", ab], writes=[pb])
                    fw.op("vector", lambda dcol=dcol: nc.vector.tensor_add(
                        r1[:, 512 * dcol:512 * (dcol + 1)],
                        xres[dcol % 4][:, :],
                        ops_[:, 512 * (dcol % 4):512 * (dcol % 4 + 1)]),
                        reads=[xr, pb], writes=["r1"])
                    fw.op("scalar", lambda dcol=dcol: nc.scalar.copy(
                        r1b[:, 512 * (dcol % 8):512 * (dcol % 8 + 1)],
                        r1[:, 512 * dcol:512 * (dcol + 1)]),
                        reads=["r1", "r1b"], writes=["r1b"])
                    fw.op("scalar", lambda dcol=dcol: nc.scalar.activation(
                        sqb[:, 512 * (dcol % 8):512 * (dcol % 8 + 1)],
                        r1[:, 512 * dcol:512 * (dcol + 1)], AF.Square),
                        reads=["r1", "sqb"], writes=["sqb"])
                    if dcol % 8 == 7:
                        for j in range(8):
                            fw.op("tensor", lambda dcol=dcol, j=j: nc.tensor.matmul(
                                sumps[0:1, :], ones_b[:, :], r1b[:, 512 * j:512 * (j + 1)],
                                start=(dcol == 7 and j == 0), stop=(dcol == 15 and j == 7)),
                                reads=["r1b", "ones_b"], writes=["sumps"])
                            fw.op("tensor", lambda dcol=dcol, j=j: nc.tensor.matmul(
                                sqps[0:1, :], ones_b[:, :], sqb[:, 512 * j:512 * (j + 1)],
                                start=(dcol == 7 and j == 0), stop=(dcol == 15 and j == 7)),
                                reads=["sqb", "ones_b"], writes=["sqps"])
                fw.op("scalar", lambda: nc.scalar.activation(mrow[:, :], sumps[0:1, :], AF.Identity, scale=1.0 / D),
                      reads=["sumps"], writes=["mrow"])
                fw.op("scalar", lambda: nc.scalar.activation(vrow[:, :], sqps[0:1, :], AF.Identity, scale=1.0 / D),
                      reads=["sqps"], writes=["vrow"])
                fw.op("vector", lambda: nc.vector.tensor_mul(trow[:, :], mrow[:, :], mrow[:, :]),
                      reads=["mrow"], writes=["trow"])
                fw.op("vector", lambda: nc.vector.tensor_sub(vrow[:, :], vrow[:, :], trow[:, :]),
                      reads=["vrow", "trow"], writes=["vrow"])
                fw.op("vector", lambda: nc.vector.tensor_scalar(vrow[:, :], vrow[:, :], EPS, None, ALU.add),
                      reads=["vrow"], writes=["vrow"])
                fw.op("scalar", lambda: nc.scalar.activation(trow[:, :], vrow[:, :], AF.Ln),
                      reads=["vrow"], writes=["trow"])
                fw.op("scalar", lambda: nc.scalar.activation(irow[:, :], trow[:, :], AF.Exp, scale=-0.5),
                      reads=["trow"], writes=["irow"])
                fw.op("vector", lambda: nc.vector.tensor_mul(brow[:, :], mrow[:, :], irow[:, :]),
                      reads=["mrow", "irow"], writes=["brow"])
                fw.op("vector", lambda: nc.vector.tensor_scalar_mul(brow[:, :], brow[:, :], -1.0),
                      reads=["brow"], writes=["brow"])
                fw.op("tensor", lambda: nc.tensor.matmul(
                    abc[:, :], onesr_f[:, :], irow[:, :], start=True, stop=True),
                    reads=["irow", "onesr_f"], writes=["abc"])
                fw.op("tensor", lambda: nc.tensor.matmul(
                    bbc[:, :], onesr_f[:, :], brow[:, :], start=True, stop=True),
                    reads=["brow", "onesr_f"], writes=["bbc"])
                fw.op("scalar", lambda: nc.scalar.copy(abc_sb[:, :], abc[:, :]),
                      reads=["abc", "abc_sb"], writes=["abc_sb"])
                fw.op("scalar", lambda: nc.scalar.copy(bbc_sb[:, :], bbc[:, :]),
                      reads=["bbc", "bbc_sb"], writes=["bbc_sb"])
                for dcol in range(16):
                    yf = f"y1fs{dcol % 2}"
                    yb = f"y1bs{dcol % 2}"
                    fw.op("vector", lambda dcol=dcol: nc.vector.tensor_mul(
                        y1fs[dcol % 2][:, :], r1[:, 512 * dcol:512 * (dcol + 1)], abc_sb[:, :]),
                        reads=["r1", "abc_sb"], writes=[yf])
                    fw.op("vector", lambda dcol=dcol: nc.vector.tensor_add(
                        y1fs[dcol % 2][:, :], y1fs[dcol % 2][:, :], bbc_sb[:, :]),
                        reads=[yf, "bbc_sb"], writes=[yf])
                    fw.op("scalar", lambda dcol=dcol: nc.scalar.copy(
                        y1bs[dcol % 2][:, :], y1fs[dcol % 2][:, :]),
                        reads=[yf], writes=[yb])
                    fw.op("gpsimd", lambda tb=tb, dcol=dcol: nc.gpsimd.dma_start(
                        y1f_scr[dcol, :, 512 * tb:512 * (tb + 1)], y1fs[dcol % 2][:, :]),
                        reads=[yf], writes=[f"y1f_scr{dcol}"], dma=True)
                    fw.op("gpsimd", lambda tb=tb, dcol=dcol: nc.gpsimd.dma_start(
                        y1b_scr[dcol, :, 512 * tb:512 * (tb + 1)], y1bs[dcol % 2][:, :]),
                        reads=[yb], writes=["y1b_scr"], dma=True)
        fw.barrier()
        if stop_after == "O":
            return nc

        with ExitStack() as st:
            y1 = st.enter_context(nc.sbuf_tensor("y1", [128, NDK * NT], BF16))
            wgu = [st.enter_context(nc.sbuf_tensor(f"wgu{i}", [128, 2 * NDK * 128], BF16))
                   for i in range(2)]
            sg = [st.enter_context(nc.sbuf_tensor(f"sg{i}", [128, 1024], F32))
                  for i in range(2)]
            hst = [st.enter_context(nc.sbuf_tensor(f"hst{i}", [128, 1024], BF16))
                   for i in range(2)]
            gps = [st.enter_context(nc.psum_tensor(f"gps{i}", [128, 1024], F32))
                   for i in range(2)]
            ups = [st.enter_context(nc.psum_tensor(f"ups{i}", [128, 1024], F32))
                   for i in range(2)]

            fw.op("scalar", lambda: load_act_set(nc, SET_SILU), inc=False)
            fw.op("sync", lambda: nc.sync.dma_start(
                y1[:, :].rearrange("p (a t) -> p a t", a=NDK),
                y1b_scr.rearrange("a p t -> p a t")),
                reads=["y1b_scr"], writes=["y1"], dma=True)

            for ht in range(HID // 128):
                wb = f"wgu{ht % 2}"
                fw.op("gpsimd", lambda ht=ht: nc.gpsimd.dma_start(
                    wgu[ht % 2][:, 0:NDK * 128].rearrange("p (a n) -> p a n", a=NDK),
                    wgt.rearrange("(a p) n -> p a n", p=128)[:, :, 128 * ht:128 * (ht + 1)]),
                    writes=[wb], dma=True)
                fw.op("gpsimd", lambda ht=ht: nc.gpsimd.dma_start(
                    wgu[ht % 2][:, NDK * 128:2 * NDK * 128].rearrange("p (a n) -> p a n", a=NDK),
                    wut.rearrange("(a p) n -> p a n", p=128)[:, :, 128 * ht:128 * (ht + 1)]),
                    writes=[wb], dma=True)
                for half in range(2):
                    gb, ub = f"gps{half}", f"ups{half}"
                    for dk in range(NDK):
                        for q in range(2):
                            fw.op("tensor", lambda ht=ht, half=half, dk=dk, q=q: nc.tensor.matmul(
                                gps[half][:, 512 * q:512 * (q + 1)],
                                wgu[ht % 2][:, 128 * dk:128 * (dk + 1)],
                                y1[:, NT * dk + 1024 * half + 512 * q: NT * dk + 1024 * half + 512 * (q + 1)],
                                start=(dk == 0), stop=(dk == NDK - 1)),
                                reads=[wb, "y1"], writes=[gb])
                        for q in range(2):
                            fw.op("tensor", lambda ht=ht, half=half, dk=dk, q=q: nc.tensor.matmul(
                                ups[half][:, 512 * q:512 * (q + 1)],
                                wgu[ht % 2][:, NDK * 128 + 128 * dk:NDK * 128 + 128 * (dk + 1)],
                                y1[:, NT * dk + 1024 * half + 512 * q: NT * dk + 1024 * half + 512 * (q + 1)],
                                start=(dk == 0), stop=(dk == NDK - 1)),
                                reads=[wb, "y1"], writes=[ub])
                    sb_ = f"sg{half}"
                    hb = f"hst{half}"
                    fw.op("scalar", lambda half=half: nc.scalar.activation(
                        sg[half][:, :], gps[half][:, :], AF.Silu),
                        reads=[gb], writes=[sb_])
                    fw.op("vector", lambda half=half: nc.vector.tensor_mul(
                        hst[half][:, :], sg[half][:, :], ups[half][:, :]),
                        reads=[sb_, ub], writes=[hb])
                    fw.op("sync", lambda ht=ht, half=half: nc.sync.dma_start(
                        hid_scr[ht, :, 1024 * half:1024 * (half + 1)], hst[half][:, :]),
                        reads=[hb], writes=["hid_scr"], dma=True)
        fw.barrier()
        if stop_after == "F1":
            return nc

        with ExitStack() as st:
            hidh = [st.enter_context(nc.sbuf_tensor(f"hidh{i}", [128, 16 * 1024], BF16))
                    for i in range(2)]
            wd = [st.enter_context(nc.sbuf_tensor(f"wd{i}", [128, 16 * 128], BF16))
                  for i in range(2)]
            acc = st.enter_context(nc.sbuf_tensor("acc", [128, NDK * 1024], F32))
            y1r = [st.enter_context(nc.sbuf_tensor(f"y1r{i}", [128, 1024], F32))
                   for i in range(2)]
            r2b = [st.enter_context(nc.sbuf_tensor(f"r2b{i}", [128, 1024], BF16))
                   for i in range(2)]
            sq2 = [st.enter_context(nc.sbuf_tensor(f"sq2{i}", [128, 1024], BF16))
                   for i in range(2)]
            trow = st.enter_context(nc.sbuf_tensor("trow2", [1, 1024], F32))
            mrow = st.enter_context(nc.sbuf_tensor("mrow2", [1, 1024], F32))
            vrow = st.enter_context(nc.sbuf_tensor("vrow2", [1, 1024], F32))
            irow = st.enter_context(nc.sbuf_tensor("irow2", [1, 1024], F32))
            brow = st.enter_context(nc.sbuf_tensor("brow2", [1, 1024], F32))
            dps = [st.enter_context(nc.psum_tensor(f"dps{i}", [128, 1024], F32))
                   for i in range(2)]
            sum2 = st.enter_context(nc.psum_tensor("sum2", [128, 1024], F32))
            sq2p = st.enter_context(nc.psum_tensor("sq2p", [128, 1024], F32))

            fw.op("scalar", lambda: load_act_set(nc, SET_LN_EXP), inc=False)

            for tp in range(2):
                for quarter in range(4):
                    qb = f"hidh{quarter % 2}"
                    fw.op("sync", lambda tp=tp, quarter=quarter, qpar=quarter % 2: nc.sync.dma_start(
                        hidh[qpar][:, :].rearrange("p (a t) -> p a t", a=16),
                        hid_scr.rearrange("a p t -> p a t")[:, 16 * quarter:16 * (quarter + 1),
                                                           1024 * tp:1024 * (tp + 1)]),
                        reads=["hid_scr"], writes=[qb], dma=True)
                    for dcol in range(16):
                        wb = f"wd{dcol % 2}"
                        fw.op("gpsimd", lambda tp=tp, quarter=quarter, dcol=dcol: nc.gpsimd.dma_start(
                            wd[dcol % 2][:, :].rearrange("p (a n) -> p a n", a=16),
                            wdt[2048 * quarter:2048 * (quarter + 1),
                                128 * dcol:128 * (dcol + 1)].rearrange("(a p) n -> p a n", p=128)),
                            writes=[wb], dma=True)
                        pb = f"dps{dcol % 2}"
                        for hb in range(16):
                            for q in range(2):
                                fw.op("tensor", lambda quarter=quarter, dcol=dcol, hb=hb, q=q, qpar=quarter % 2:
                                      nc.tensor.matmul(
                                          dps[dcol % 2][:, 512 * q:512 * (q + 1)],
                                          wd[dcol % 2][:, 128 * hb:128 * (hb + 1)],
                                          hidh[qpar][:, 1024 * hb + 512 * q:1024 * hb + 512 * (q + 1)],
                                          start=(hb == 0), stop=(hb == 15)),
                                      reads=[wb, qb], writes=[pb])
                        if quarter == 0:
                            fw.op("vector", lambda dcol=dcol: nc.vector.tensor_copy(
                                acc[:, 1024 * dcol:1024 * (dcol + 1)], dps[dcol % 2][:, :]),
                                reads=[pb, "acc"], writes=["acc"])
                        else:
                            fw.op("vector", lambda dcol=dcol: nc.vector.tensor_add(
                                acc[:, 1024 * dcol:1024 * (dcol + 1)],
                                acc[:, 1024 * dcol:1024 * (dcol + 1)], dps[dcol % 2][:, :]),
                                reads=[pb, "acc"], writes=["acc"])
                if "dn_scr" in dbg:
                    for dcol in range(16):
                        fw.op("sync", lambda tp=tp, dcol=dcol: nc.sync.dma_start(
                            dn_scr[dcol, :, 1024 * tp:1024 * (tp + 1)],
                            acc[:, 1024 * dcol:1024 * (dcol + 1)]),
                            reads=["acc"], writes=["dn_scr"], dma=True)
                for dcol in range(16):
                    yb = f"y1r{dcol % 2}"
                    fw.op("sync", lambda tp=tp, dcol=dcol: nc.sync.dma_start(
                        y1r[dcol % 2][:, :],
                        y1f_scr[dcol, :, 1024 * tp:1024 * (tp + 1)]),
                        reads=[f"y1f_scr{dcol}"], writes=[yb], dma=True)
                    fw.op("vector", lambda dcol=dcol: nc.vector.tensor_add(
                        acc[:, 1024 * dcol:1024 * (dcol + 1)],
                        acc[:, 1024 * dcol:1024 * (dcol + 1)], y1r[dcol % 2][:, :]),
                        reads=["acc", yb], writes=["acc"])
                    if "r2_scr" in dbg:
                        fw.op("sync", lambda tp=tp, dcol=dcol: nc.sync.dma_start(
                            r2_scr[dcol, :, 1024 * tp:1024 * (tp + 1)],
                            acc[:, 1024 * dcol:1024 * (dcol + 1)]),
                            reads=["acc"], writes=["r2_scr"], dma=True)
                    fw.op("scalar", lambda dcol=dcol: nc.scalar.copy(
                        r2b[dcol % 2][:, :], acc[:, 1024 * dcol:1024 * (dcol + 1)]),
                        reads=["acc"], writes=[f"r2b{dcol % 2}"])
                    fw.op("scalar", lambda dcol=dcol: nc.scalar.activation(
                        sq2[dcol % 2][:, :], acc[:, 1024 * dcol:1024 * (dcol + 1)], AF.Square),
                        reads=["acc"], writes=[f"sq2{dcol % 2}"])
                    for q in range(2):
                        fw.op("tensor", lambda dcol=dcol, q=q: nc.tensor.matmul(
                            sum2[0:1, 512 * q:512 * (q + 1)], ones_b[:, :],
                            r2b[dcol % 2][:, 512 * q:512 * (q + 1)],
                            start=(dcol == 0), stop=(dcol == 15)),
                            reads=[f"r2b{dcol % 2}", "ones_b"], writes=["sum2"])
                        fw.op("tensor", lambda dcol=dcol, q=q: nc.tensor.matmul(
                            sq2p[0:1, 512 * q:512 * (q + 1)], ones_b[:, :],
                            sq2[dcol % 2][:, 512 * q:512 * (q + 1)],
                            start=(dcol == 0), stop=(dcol == 15)),
                            reads=[f"sq2{dcol % 2}", "ones_b"], writes=["sq2p"])
                fw.op("scalar", lambda: nc.scalar.activation(mrow[:, :], sum2[0:1, :], AF.Identity, scale=1.0 / D),
                      reads=["sum2"], writes=["mrow2"])
                fw.op("scalar", lambda: nc.scalar.activation(vrow[:, :], sq2p[0:1, :], AF.Identity, scale=1.0 / D),
                      reads=["sq2p"], writes=["vrow2"])
                fw.op("vector", lambda: nc.vector.tensor_mul(trow[:, :], mrow[:, :], mrow[:, :]),
                      reads=["mrow2"], writes=["trow2"])
                fw.op("vector", lambda: nc.vector.tensor_sub(vrow[:, :], vrow[:, :], trow[:, :]),
                      reads=["vrow2", "trow2"], writes=["vrow2"])
                fw.op("vector", lambda: nc.vector.tensor_scalar(vrow[:, :], vrow[:, :], EPS, None, ALU.add),
                      reads=["vrow2"], writes=["vrow2"])
                fw.op("scalar", lambda: nc.scalar.activation(trow[:, :], vrow[:, :], AF.Ln),
                      reads=["vrow2"], writes=["trow2"])
                fw.op("scalar", lambda: nc.scalar.activation(irow[:, :], trow[:, :], AF.Exp, scale=-0.5),
                      reads=["trow2"], writes=["irow2"])
                fw.op("vector", lambda: nc.vector.tensor_mul(brow[:, :], mrow[:, :], irow[:, :]),
                      reads=["mrow2", "irow2"], writes=["brow2"])
                fw.op("vector", lambda: nc.vector.tensor_scalar_mul(brow[:, :], brow[:, :], -1.0),
                      reads=["brow2"], writes=["brow2"])
                for q in range(2):
                    fw.op("tensor", lambda q=q: nc.tensor.matmul(
                        dps[0][:, 512 * q:512 * (q + 1)], onesr_f[:, :],
                        irow[0:1, 512 * q:512 * (q + 1)], start=True, stop=True),
                        reads=["irow2"], writes=["dps0"])
                    fw.op("tensor", lambda q=q: nc.tensor.matmul(
                        dps[1][:, 512 * q:512 * (q + 1)], onesr_f[:, :],
                        brow[0:1, 512 * q:512 * (q + 1)], start=True, stop=True),
                        reads=["brow2"], writes=["dps1"])
                fw.op("scalar", lambda: nc.scalar.copy(y1r[0][:, :], dps[0][:, :]),
                      reads=["dps0", "y1r0"], writes=["y1r0"])
                fw.op("scalar", lambda: nc.scalar.copy(y1r[1][:, :], dps[1][:, :]),
                      reads=["dps1", "y1r1"], writes=["y1r1"])
                for dcol in range(16):
                    fw.op("vector", lambda dcol=dcol: nc.vector.tensor_mul(
                        acc[:, 1024 * dcol:1024 * (dcol + 1)],
                        acc[:, 1024 * dcol:1024 * (dcol + 1)], y1r[0][:, :]),
                        reads=["acc", "y1r0"], writes=["acc"])
                    fw.op("vector", lambda dcol=dcol: nc.vector.tensor_add(
                        acc[:, 1024 * dcol:1024 * (dcol + 1)],
                        acc[:, 1024 * dcol:1024 * (dcol + 1)], y1r[1][:, :]),
                        reads=["acc", "y1r1"], writes=["acc"])
                    fw.op("sync", lambda tp=tp, dcol=dcol: nc.sync.dma_start(
                        out_fm[128 * dcol:128 * (dcol + 1), 1024 * tp:1024 * (tp + 1)],
                        acc[:, 1024 * dcol:1024 * (dcol + 1)]),
                        reads=["acc"], writes=["out_fm"], dma=True)
        fw.barrier()
    return nc


def emit_attention(nc, fw, c, qh, kh, vh, pp, ptmp, pacc_v, zc_sb, zi_sb,
                   zb_sb, ast, nps, stp, zps, zbp, masks, sufV, ones_f, onesr_f,
                   q_scr, k_ag, v_ag, a_scr):
    ns = [8 * c + 4, 8 * c + 8, 60 - 8 * c, 64 - 8 * c]
    counts = [float(T - 128 * n) for n in ns]
    span_sb = ns[3]
    vchunk = span_sb // 8

    span_chunks = span_sb // 8
    kag = k_ag.ap().rearrange("(r d) s -> r d s", r=4)
    vag = v_ag.ap().rearrange("(r s) d -> r s d", r=4)

    def load_head(h):
        par = h % 2
        qb, kb, vb = f"qh{par}", f"kh{par}", f"vh{par}"
        fw.op("sync", lambda h=h: nc.sync.dma_start(
            qh[h % 2][:, :], q_scr[h, :, :]), writes=[qb], dma=True)
        for g in range(span_chunks):
            r, half = (g, 0) if g < 4 else (7 - g, 1)
            fw.op("sync", lambda h=h, g=g, r=r, half=half: nc.sync.dma_start(
                kh[h % 2][:, 1024 * g:1024 * (g + 1)],
                kag[r, 128 * h:128 * (h + 1), 1024 * half:1024 * (half + 1)]),
                writes=[kb], dma=True)
            fw.op("gpsimd", lambda h=h, g=g, r=r, half=half: nc.gpsimd.dma_start(
                vh[h % 2][:, 1024 * g:1024 * (g + 1)]
                  .rearrange("p (a d) -> p a d", a=8),
                vag[r, 1024 * half:1024 * (half + 1), 128 * h:128 * (h + 1)]
                  .rearrange("(a p) d -> p a d", p=128)),
                writes=[vb], dma=True)

    def finalize(h, tb, ti):
        prow = 0 if ti == 0 else 32
        fw.op("tensor", lambda ti=ti, prow=prow: nc.tensor.matmul(
            zps[prow:prow + 1, :], ones_f[:, :],
            pacc_v[:, 512 * ti:512 * (ti + 1)], start=True, stop=True),
            reads=[f"paccv{ti}", "ones_f"], writes=["zps"])
        fw.op("vector", lambda tb=tb, prow=prow: nc.vector.tensor_scalar(
            zc_sb[:, :], zps[prow:prow + 1, :], counts[tb], None, ALU.add),
            reads=["zps"], writes=["zc_sb"])
        fw.op("vector", lambda: nc.vector.reciprocal(zi_sb[:, :], zc_sb[:, :]),
              reads=["zc_sb"], writes=["zi_sb"])
        fw.op("tensor", lambda: nc.tensor.matmul(
            zbp[:, :], onesr_f[:, :], zi_sb[:, :], start=True, stop=True),
            reads=["zi_sb", "onesr_f"], writes=["zbp"])
        fw.op("scalar", lambda: nc.scalar.copy(zb_sb[:, :], zbp[:, :]),
              reads=["zbp", "zb_sb"], writes=["zb_sb"])
        apar = tb % 2
        fw.op("vector", lambda h=h, tb=tb, ti=ti, apar=apar: nc.vector.scalar_tensor_tensor(
            ast[apar][:, :], nps[:, 512 * ti:512 * (ti + 1)],
            sufV[:, 65 * h + ns[tb]:65 * h + ns[tb] + 1],
            zb_sb[:, :], ALU.add, ALU.mult),
            reads=[f"nps{ti}", "sufV", "zb_sb"], writes=[f"ast{apar}"])
        fw.op("gpsimd", lambda h=h, tb=tb, apar=apar: nc.gpsimd.dma_start(
            a_scr[h, :, 512 * tb:512 * (tb + 1)], ast[apar][:, :]),
            reads=[f"ast{apar}"], writes=[f"a_scr{h}"], dma=True)

    load_head(0)
    for h in range(H):
        par = h % 2
        qb, kb, vb = f"qh{par}", f"kh{par}", f"vh{par}"
        if h + 1 < H:
            load_head(h + 1)
        for p in range(2):
            lo, hi = 2 * p, 2 * p + 1
            n_lo, n_hi = ns[lo], ns[hi]
            def av_emit(sb):
                both = sb < n_lo
                ppar = sb % 4
                ppb = f"pp{ppar}"
                if both:
                    fw.op("tensor", lambda sb=sb, par=par, ppar=ppar, n_lo=n_lo:
                          nc.tensor.matmul(
                              nps[:, 0:512],
                              vh[par][:, 128 * sb:128 * (sb + 1)],
                              pp[ppar][:, 0:512],
                              start=(sb == 0), stop=(sb == n_lo - 1)),
                          reads=[f"vh{par}", ppb], writes=["nps0"])
                fw.op("tensor", lambda sb=sb, par=par, ppar=ppar, n_hi=n_hi:
                      nc.tensor.matmul(
                          nps[:, 512:1024],
                          vh[par][:, 128 * sb:128 * (sb + 1)],
                          pp[ppar][:, 512:1024],
                          start=(sb == 0), stop=(sb == n_hi - 1)),
                      reads=[f"vh{par}", ppb], writes=["nps1"])

            LAG = 2
            for sb in range(n_hi):
                both = sb < n_lo
                ppar = sb % 4
                spar = sb % 2
                ppb = f"pp{ppar}"
                stL, stH = f"st{spar}L", f"st{spar}H"
                lo0 = 0 if both else 512
                if both:
                    fw.op("tensor", lambda sb=sb, lo=lo, par=par, spar=spar: nc.tensor.matmul(
                        stp[spar][:, 0:512],
                        kh[par][:, 128 * sb:128 * (sb + 1)],
                        qh[par][:, 512 * lo:512 * (lo + 1)],
                        start=True, stop=True),
                        reads=[kb, qb], writes=[stL])
                fw.op("tensor", lambda sb=sb, hi=hi, par=par, spar=spar: nc.tensor.matmul(
                    stp[spar][:, 512:1024],
                    kh[par][:, 128 * sb:128 * (sb + 1)],
                    qh[par][:, 512 * hi:512 * (hi + 1)],
                    start=True, stop=True),
                    reads=[kb, qb], writes=[stH])
                if both and n_lo - 4 <= sb < n_lo:
                    k = sb - (n_lo - 4)
                    fw.op("vector", lambda k=k, spar=spar: nc.vector.tensor_mul(
                        stp[spar][:, 0:512], stp[spar][:, 0:512],
                        masks[:, 512 * k:512 * (k + 1)]),
                        reads=[stL, "masks"], writes=[stL])
                if n_hi - 4 <= sb < n_hi:
                    k = sb - (n_hi - 4)
                    fw.op("vector", lambda k=k, spar=spar: nc.vector.tensor_mul(
                        stp[spar][:, 512:1024], stp[spar][:, 512:1024],
                        masks[:, 512 * k:512 * (k + 1)]),
                        reads=[stH, "masks"], writes=[stH])
                st_reads = ([stL, stH] if both else [stH])
                fw.op("scalar", lambda ppar=ppar, lo0=lo0, spar=spar: nc.scalar.activation(
                    pp[ppar][:, lo0:1024], stp[spar][:, lo0:1024], AF.Exp, scale=SCALE),
                    reads=st_reads, writes=[ppb])
                if sb % 2 == 1:
                    tpar = (sb // 2) % 2
                    tpb = f"ptmp{tpar}"
                    ppb_prev = f"pp{(sb - 1) % 4}"
                    plo0 = 0 if (sb - 1) < n_lo else 512
                    pkeys = ["paccv0", "paccv1"] if plo0 == 0 else ["paccv1"]
                    fw.op("vector", lambda ppar=ppar, sb=sb, plo0=plo0, tpar=tpar:
                          nc.vector.tensor_add(
                              ptmp[tpar][:, plo0:1024],
                              pp[(sb - 1) % 4][:, plo0:1024], pp[ppar][:, plo0:1024]),
                          reads=[ppb, ppb_prev, tpb], writes=[tpb])
                    if sb == 1:
                        fw.op("vector", lambda tpar=tpar: nc.vector.tensor_copy(
                            pacc_v[:, :], ptmp[tpar][:, :]),
                            reads=[tpb], writes=["paccv0", "paccv1"])
                    else:
                        fw.op("vector", lambda tpar=tpar, plo0=plo0: nc.vector.tensor_add(
                            pacc_v[:, plo0:1024], pacc_v[:, plo0:1024], ptmp[tpar][:, plo0:1024]),
                            reads=[tpb] + pkeys, writes=pkeys)
                if sb >= LAG:
                    av_emit(sb - LAG)
                if sb - LAG == n_lo + 1:
                    finalize(h, lo, 0)
            for sb in range(max(n_hi - LAG, 0), n_hi):
                av_emit(sb)
                if sb == n_lo + 1:
                    finalize(h, lo, 0)
            finalize(h, hi, 1)


def kernel(L0_attn_x, L0_attn_Wq, L0_attn_Wk, L0_attn_Wv, L0_attn_Wo,
           L0_ffn_w_gate, L0_ffn_w_up, L0_ffn_w_down):
    x = np.asarray(L0_attn_x)
    wq = np.ascontiguousarray(np.asarray(L0_attn_Wq), dtype=np.float32)
    wk = np.ascontiguousarray(np.asarray(L0_attn_Wk), dtype=np.float32)
    wv = np.ascontiguousarray(np.asarray(L0_attn_Wv), dtype=np.float32)
    wo = np.ascontiguousarray(np.asarray(L0_attn_Wo), dtype=np.float32)
    wgt = np.ascontiguousarray(np.asarray(L0_ffn_w_gate).T, dtype=np.float32)
    wut = np.ascontiguousarray(np.asarray(L0_ffn_w_up).T, dtype=np.float32)
    wdt = np.ascontiguousarray(np.asarray(L0_ffn_w_down).T, dtype=np.float32)

    xT = [np.ascontiguousarray(x[b].T, dtype=np.float32) for b in range(B)]

    nc = build_program()
    in_maps = []
    for core in range(8):
        b, c = core // 4, core % 4
        xo = np.concatenate([xT[b][:, 1024 * c:1024 * (c + 1)],
                             xT[b][:, 1024 * (7 - c):1024 * (8 - c)]], axis=1)
        in_maps.append({
            "x_ownT": np.ascontiguousarray(xo),
            "wq": wq, "wk": wk, "wv": wv, "wo": wo,
            "wgt": wgt, "wut": wut, "wdt": wdt,
            "cidx": np.array([[c]], dtype=np.int32),
        })
    res = run_bass_kernel_spmd(nc, in_maps, core_ids=list(range(8)))
    out = np.empty((B, T, D), dtype=np.float32)
    for core in range(8):
        b, c = core // 4, core % 4
        o = np.asarray(res.results[core]["out_fm"])
        out[b, 1024 * c:1024 * (c + 1), :] = o[:, 0:1024].T
        out[b, 1024 * (7 - c):1024 * (8 - c), :] = o[:, 1024:2048].T
    return out
